# version 09
# baseline (speedup 1.0000x reference)
import contextlib
import numpy as np
import ml_dtypes
import concourse.bass as bass
import concourse.mybir as mybir
from concourse.bass_utils import run_bass_kernel_spmd

F32 = mybir.dt.float32
BF16 = mybir.dt.bfloat16
AF = mybir.ActivationFunctionType
ALU = mybir.AluOpType
AX = mybir.AxisListType

NCORES = 8
DM = 1024

ENGS = ("tensor", "vector", "scalar", "sync", "gpsimd")
ENGSEM = {"tensor": "cT", "vector": "cV", "scalar": "cS", "sync": "cY", "gpsimd": "cG"}
DMASEM = {"sync": "dS", "gpsimd": "dG", "scalar": "dC", "vector": "dV", "tensor": "dT"}


class Sem:
    __slots__ = ("h", "n", "name")

    def __init__(self, h, name):
        self.h = h
        self.n = 0
        self.name = name

    def plan(self, kk):
        self.n += kk
        return self.n


class K:

    def __init__(self, nc, es, seq):
        self.nc = nc
        self.es = es
        self.seq = seq
        self.nt = seq // 128
        self.nq = seq // 512
        self.ops = {e: [] for e in ENGS}
        self.sems = {}
        for name in set(ENGSEM.values()):
            self.sems[name] = Sem(es.enter_context(nc.semaphore(name)), name)
        self.dpool = {}
        self.didx = {}
        for q, base in DMASEM.items():
            npool = 6 if q in ("sync", "gpsimd") else 2
            self.dpool[q] = []
            for j in range(npool):
                nm = f"{base}{j}"
                s = Sem(es.enter_context(nc.semaphore(nm)), nm)
                self.sems[nm] = s
                self.dpool[q].append(s)
            self.didx[q] = 0
        self.dram = {}
        self.out_tr = []
        self.out_sb2sb = []

    def emit(self, eng, f):
        self.ops[eng].append(f)

    def wt(self, eng, semname, v):
        if not v:
            return
        s = self.sems[semname]
        assert v <= s.n, f"wait on {semname}={v} > planned {s.n}"
        self.emit(eng, lambda e, s=s, v=v: e.wait_ge(s.h, v))

    _opid = [0]
    _oplog = {}

    def op(self, eng, meth, inc=1, selfwait=None, tag="", **kw):
        import sys as _sys
        s = self.sems[ENGSEM[eng]]
        v = s.plan(inc) if inc else None
        K._opid[0] += 1
        fr = _sys._getframe(1)
        K._oplog[K._opid[0]] = f"{meth} @ {fr.f_code.co_name}:{fr.f_lineno}"
        label = f"{meth}.{tag}.{K._opid[0]}"

        def f(e, meth=meth, kw=kw, s=s, inc=inc, label=label):
            sid, _ = self.nc.enter_named_scope(label, False)
            try:
                r = getattr(e, meth)(**kw)
                if inc:
                    r.then_inc(s.h, inc)
            finally:
                self.nc.leave_named_scope(label, sid, False)

        self.emit(eng, f)
        if selfwait is None:
            selfwait = eng in ("vector", "scalar", "gpsimd")
        if selfwait and inc:
            self.wt(eng, ENGSEM[eng], v)
        return v

    def dma(self, q, out, in_, sem=None, **kw):
        if sem is None:
            pool = self.dpool[q]
            s = pool[self.didx[q] % len(pool)]
            self.didx[q] += 1
        else:
            s = self.sems[sem]
        is_tr = bool(kw.get("transpose"))
        try:
            sp_in, sp_out = str(in_.space), str(out.space)
        except Exception:
            sp_in = sp_out = "?"
        is_sb2sb = (not is_tr) and ("SBUF" in sp_in) and ("SBUF" in sp_out)
        if is_tr and self.out_sb2sb:
            self.wtd(q, self.out_sb2sb)
            self.out_sb2sb = []
        if is_sb2sb and self.out_tr:
            self.wtd(q, self.out_tr)
            self.out_tr = []
        if s.n:
            self.wt(q, s.name, s.n)
        v = s.plan(16)

        def f(e, out=out, in_=in_, kw=kw, s=s):
            e.dma_start(out=out, in_=in_, **kw).then_inc(s.h, 16)

        self.emit(q, f)
        tok = (s.name, v)
        if is_tr:
            self.out_tr.append(tok)
        elif is_sb2sb:
            self.out_sb2sb.append(tok)
        return tok

    def wtd(self, eng, tok):
        if not tok:
            return
        if isinstance(tok, list):
            best = {}
            for t in tok:
                if t:
                    best[t[0]] = max(best.get(t[0], 0), t[1])
            for name, v in best.items():
                self.wt(eng, name, v)
            return
        self.wt(eng, tok[0], tok[1])

    def snap(self):
        return {n: s.n for n, s in self.sems.items()}

    def fence(self, engines=ENGS):
        snap = self.snap()
        for eng in engines:
            own = {ENGSEM[eng], DMASEM.get(eng)}
            for name, v in snap.items():
                if name in own or v == 0:
                    continue
                self.wt(eng, name, v)

    def wait_all(self, eng, include_own_dma=False):
        own = {ENGSEM[eng]}
        if not include_own_dma:
            own.add(DMASEM.get(eng))
        for name, s in self.sems.items():
            if name in own or s.n == 0:
                continue
            self.wt(eng, name, s.n)

    def build(self):
        with self.nc.Block() as blk:
            for eng in ENGS:
                def mk(eng):
                    def body(e):
                        for f in self.ops[eng]:
                            f(e)
                    return body
                getattr(blk, eng)(mk(eng))



def host_tables(seq):
    t = {}
    t["ident"] = np.eye(128, dtype=ml_dtypes.bfloat16)
    kk = np.arange(128)[:, None]
    qq = np.arange(512)[None, :]
    t["trmask"] = np.stack([(kk + 128 * e <= qq) for e in range(4)]).astype(ml_dtypes.bfloat16)
    jj = np.arange(128)[None, :]
    t["pvlr"] = np.broadcast_to(
        np.stack([(jj < 64), (jj >= 64)], axis=0)[None, :, 0, :], (128, 2, 128)
    ).astype(ml_dtypes.bfloat16)
    t["pvq4"] = np.broadcast_to(
        np.stack([(jj[0] // 32 == m) for m in range(4)], axis=0)[None, :, :],
        (128, 4, 128)).astype(ml_dtypes.bfloat16)
    return t



def bcast_row(k, name, src_dram, n, ph):
    nc = k.nc
    tile = ph.enter_context(nc.sbuf_tensor(name, [128, n], F32))
    with contextlib.ExitStack() as s:
        row = s.enter_context(nc.sbuf_tensor(name + "_row", [1, n], F32))
        ones = s.enter_context(nc.sbuf_tensor(name + "_ones", [1, 128], F32))
        ps = s.enter_context(nc.psum_tensor(name + "_ps", [128, n], F32))
        k.fence()
        vld = k.dma("sync", out=row[:], in_=src_dram)
        vo = k.op("vector", "memset", ap=ones[:], constant=1.0)
        k.wtd("tensor", vld)
        k.wt("tensor", "cV", k.sems["cV"].n)
        vmm = None
        for c0 in range(0, n, 512):
            c1 = min(c0 + 512, n)
            vmm = k.op("tensor", "matmul", out=ps[:, c0:c1], lhsT=ones[:, :],
                       rhs=row[:, c0:c1], start=True, stop=True)
        k.wt("vector", "cT", vmm)
        vcp = k.op("vector", "tensor_copy", out=tile[:], in_=ps[:])
    k.fence()
    return tile, vcp


def rsqrt_cols(k, buf, src_col, dst_col, eps):
    vv = k.op("vector", "tensor_scalar", out=buf[:, dst_col], in0=buf[:, src_col],
              scalar1=float(eps), scalar2=None, op0=ALU.add)
    k.wt("scalar", "cV", vv)
    vs = k.op("scalar", "activation", out=buf[:, dst_col], in_=buf[:, dst_col], func=AF.Sqrt)
    k.wt("vector", "cS", vs)
    vr = k.op("vector", "reciprocal", out=buf[:, dst_col], in_=buf[:, dst_col])
    k.wt("vector", "cV", vr)
    return vr


def layer_norm_tile(k, x_ap, out_ap, stats, mv, eps=1e-5, n=DM):
    nh = n // 512
    vs = None
    for hh in range(nh):
        vs = k.op("vector", "bn_stats", out=stats[:, hh * 6:(hh + 1) * 6], in_=x_ap[:, hh * 512:(hh + 1) * 512])
    k.wt("vector", "cV", vs)
    va = k.op("vector", "bn_aggr", out=mv[:, 0:2], in_=stats[:, 0:6 * nh])
    k.wt("vector", "cV", va)
    rsqrt_cols(k, mv, slice(1, 2), slice(2, 3), eps)
    return k.op("vector", "tensor_scalar", out=out_ap, in0=x_ap, scalar1=mv[:, 0:1],
                scalar2=mv[:, 2:3], op0=ALU.subtract, op1=ALU.mult)



def emit_l0a(k, xin, xout):
    nc = k.nc
    seq, nt, nq = k.seq, k.nt, k.nq
    D = k.dram
    H, HD, INNER = 8, 64, 512
    hscr = D["hscr"]

    with contextlib.ExitStack() as ph:
        k.fence()
        ones_col = ph.enter_context(nc.sbuf_tensor("a_ones_col", [128, 1], BF16))
        mneg8 = ph.enter_context(nc.sbuf_tensor("a_mneg8", [128, 1], F32))
        trmask = ph.enter_context(nc.sbuf_tensor("a_trmask", [128, 4, 512], BF16))
        ident = ph.enter_context(nc.sbuf_tensor("a_ident", [128, 128], BF16))
        k.op("vector", "memset", ap=ones_col[:], constant=1.0)
        k.op("vector", "memset", ap=mneg8[:], constant=-8.0)
        k.dma("sync", out=trmask[:], in_=D["t_trmask"].rearrange("e p q -> p e q"))
        k.dma("sync", out=ident[:], in_=D["t_ident"][:, :])

        with contextlib.ExitStack() as s1:
            lnw_bc, _ = bcast_row(k, "a_lnw", D["L0_attn_ln_w"][None, :], DM, s1)
            lnb_bc, vbc = bcast_row(k, "a_lnb", D["L0_attn_ln_b"][None, :], DM, s1)
            x_sb = s1.enter_context(nc.sbuf_tensor("a_x", [128, 2, DM], F32))
            xn1 = s1.enter_context(nc.sbuf_tensor("a_xn1", [128, DM], F32))
            scr1 = s1.enter_context(nc.sbuf_tensor("a_scr1", [128, DM], F32))
            xn2 = s1.enter_context(nc.sbuf_tensor("a_xn2", [128, DM], BF16))
            stats = s1.enter_context(nc.sbuf_tensor("a_stats", [128, 12], F32))
            mv = s1.enter_context(nc.sbuf_tensor("a_mv", [128, 4], F32))
            k.wt("vector", "cV", vbc)
            last_st = None
            xload = {0: None, 1: None}
            prev_use = {0: 0, 1: 0}
            for tc in range(nt):
                b = tc % 2
                k.wt("sync", "cV", prev_use[b])
                xload[b] = k.dma("sync", out=x_sb[:, b, :], in_=xin[tc * 128:(tc + 1) * 128, :])
                k.wtd("vector", xload[b])
                layer_norm_tile(k, x_sb[:, b, :], xn1[:], stats, mv)
                layer_norm_tile(k, xn1[:], scr1[:], stats, mv)
                k.op("vector", "tensor_tensor", out=xn1[:], in0=scr1[:], in1=lnw_bc[:], op=ALU.mult)
                k.wtd("vector", last_st)
                vn = k.op("vector", "tensor_tensor", out=xn2[:], in0=xn1[:], in1=lnb_bc[:], op=ALU.add)
                prev_use[b] = vn
                k.wt("sync", "cV", vn)
                last_st = k.dma("sync", out=hscr[tc * 128:(tc + 1) * 128, 0:DM], in_=xn2[:])

        v_sb = ph.enter_context(nc.sbuf_tensor("a_v", [128, nt, INNER], BF16))
        with contextlib.ExitStack() as s2:
            k.fence()
            xn2T = s2.enter_context(nc.sbuf_tensor("a_xn2T", [128, 8, seq], BF16))
            k.wtd("sync", last_st)
            vtr = []
            for c in range(8):
                vtr.append(k.dma("sync", out=xn2T[:, c, :], in_=hscr[:, c * 128:(c + 1) * 128],
                                 transpose=True))
            wqkv = s2.enter_context(nc.sbuf_tensor("a_wqkv", [128, 8, 1536], BF16))
            vwl = k.dma("gpsimd", out=wqkv[:], in_=D["L0_attn_w_qkv"].rearrange("(c p) n -> p c n", p=128))
            qk_bf = s2.enter_context(nc.sbuf_tensor("a_qkbf", [128, INNER], BF16))
            sq_scr = s2.enter_context(nc.sbuf_tensor("a_sq", [128, INNER], F32))
            hsums = s2.enter_context(nc.sbuf_tensor("a_hs", [128, 2 * H], F32))
            psA = s2.enter_context(nc.psum_tensor("a_psA", [128, 512], F32))
            psB = s2.enter_context(nc.psum_tensor("a_psB", [128, 512], F32))
            k.wtd("tensor", vtr)
            k.wtd("tensor", vwl)
            ps = {0: psA, 1: psB}
            freed = {0: 0, 1: 0}
            freed_eng = {0: "cV", 1: "cV"}
            vqk_store = None
            for tc in range(nt):
                for oi, o0 in enumerate((0, 512, 1024)):
                    b = (tc * 3 + oi) % 2
                    pp = ps[b]
                    k.wt("tensor", freed_eng[b], freed[b])
                    vm = None
                    for c in range(8):
                        vm = k.op("tensor", "matmul", out=pp[:],
                                  lhsT=xn2T[:, c, tc * 128:(tc + 1) * 128],
                                  rhs=wqkv[:, c, o0:o0 + 512],
                                  start=(c == 0), stop=(c == 7))
                    if o0 == 1024:
                        k.wt("scalar", "cT", vm)
                        freed[b] = k.op("scalar", "copy", out=v_sb[:, tc, :], in_=pp[:])
                        freed_eng[b] = "cS"
                    else:
                        k.wt("scalar", "cT", vm)
                        vsq = k.op("scalar", "activation", out=sq_scr[:], in_=pp[:], func=AF.Square)
                        k.wt("vector", "cS", vsq)
                        k.wtd("vector", vqk_store)
                        k.op("vector", "tensor_reduce", out=hsums[:, 0:H],
                             in_=sq_scr[:].rearrange("p (h d) -> p h d", h=H), axis=AX.X, op=ALU.add)
                        rsqrt_cols(k, hsums, slice(0, H), slice(H, 2 * H), 1e-12)
                        vn = None
                        for h in range(H):
                            vn = k.op("vector", "tensor_scalar", out=qk_bf[:, h * HD:(h + 1) * HD],
                                      in0=pp[:, h * HD:(h + 1) * HD], scalar1=hsums[:, H + h:H + h + 1],
                                      scalar2=None, op0=ALU.mult)
                        freed[b] = vn
                        freed_eng[b] = "cV"
                        k.wt("sync", "cV", vn)
                        vqk_store = k.dma("sync", out=hscr[tc * 128:(tc + 1) * 128, DM + o0:DM + o0 + 512],
                                          in_=qk_bf[:])

        k.fence()
        qT = ph.enter_context(nc.sbuf_tensor("a_qT", [128, 4, seq], BF16))
        kT = ph.enter_context(nc.sbuf_tensor("a_kT", [128, 4, seq], BF16))
        k.wtd("sync", vqk_store)
        k.fence(engines=("sync",))
        vq = []
        for c in range(4):
            vq.append(k.dma("sync", out=qT[:, c, :], in_=hscr[:, DM + c * 128:DM + (c + 1) * 128],
                            transpose=True))
            vq.append(k.dma("sync", out=kT[:, c, :], in_=hscr[:, DM + 512 + c * 128:DM + 512 + (c + 1) * 128],
                            transpose=True))
        wout = ph.enter_context(nc.sbuf_tensor("a_wout", [128, 4, DM], BF16))
        vwo = k.dma("gpsimd", out=wout[:], in_=D["L0_attn_w_out"].rearrange("(c p) n -> p c n", p=128))

        with contextlib.ExitStack() as s3:
            k.fence()
            big2 = s3.enter_context(nc.psum_tensor("a_big2", [128, 1024], F32))
            yt = [s3.enter_context(nc.psum_tensor(f"a_yt{t}", [128, 512], F32)) for t in range(2)]
            den = [s3.enter_context(nc.psum_tensor(f"a_den{t}", [128, 512], F32)) for t in range(2)]
            outps = s3.enter_context(nc.psum_tensor("a_outps", [128, 1024], F32))
            pts = s3.enter_context(nc.sbuf_tensor("a_pts", [128, 2, 512], BF16))
            pvlr = s3.enter_context(nc.sbuf_tensor("a_pvlr", [128, 2, 128], BF16))
            zero128 = s3.enter_context(nc.sbuf_tensor("a_zero128", [128, 128], BF16))
            rden = s3.enter_context(nc.sbuf_tensor("a_rden", [128, 2, 512], F32))
            oT_sb = s3.enter_context(nc.sbuf_tensor("a_oT4", [128, 4, 512], BF16))
            xr_sb = s3.enter_context(nc.sbuf_tensor("a_xr", [128, 4, DM], F32))
            xnew = s3.enter_context(nc.sbuf_tensor("a_xnew", [128, DM], F32))

            vz = k.op("vector", "memset", ap=zero128[:], constant=0.0)
            vlr = k.dma("sync", out=pvlr[:], in_=D["t_pvlr"][:, :, :])
            k.wtd("tensor", vq)
            k.wtd("tensor", vwo)
            k.wtd("tensor", vlr)
            k.wt("tensor", "cV", vz)
            k.wait_all("tensor")
            exp_free = {0: 0, 1: 0}
            big_free = {0: ("cS", 0), 1: ("cS", 0)}
            banks_free = 0
            oT_free = 0
            out_free = {0: 0, 1: 0}
            st_out = None
            last_vadd = 0
            tig = [0]
            for qc in range(nq):
                k.wt("sync", "cV", last_vadd)
                vxl = k.dma("sync", out=xr_sb[:], in_=xin[qc * 512:(qc + 1) * 512, :].rearrange(
                    "(s p) d -> p s d", p=128))
                nkt = 4 * qc + 4
                for w in range(2):
                    k.wt("tensor", "cV", banks_free)
                    for t in range(2):
                        k.op("tensor", "matmul", out=yt[t][:, 0:128], lhsT=zero128[:],
                             rhs=zero128[:], start=True, stop=False, skip_group_check=True)
                    tiles = [(4 * w + j, kt) for j in range(4) for kt in range(nkt)]

                    def emit_pv_a(p2, h2, kt2, gate2, nkt=nkt, w=w):
                        j2 = h2 - 4 * w
                        t2, side = j2 // 2, j2 % 2
                        k.wt("tensor", gate2[0], gate2[1])
                        k.op("tensor", "matmul", out=yt[t2][side * 64:side * 64 + 64, :],
                             lhsT=v_sb[:, kt2, h2 * HD:(h2 + 1) * HD],
                             rhs=pts[:, p2, :],
                             start=False, stop=(side == 1 and kt2 == nkt - 1),
                             skip_group_check=True)
                        vpe = k.op("tensor", "matmul", out=den[t2][:],
                                   lhsT=pvlr[:, side, :], rhs=pts[:, p2, :],
                                   start=(side == 0 and kt2 == 0),
                                   stop=(side == 1 and kt2 == nkt - 1),
                                   skip_group_check=True)
                        exp_free[p2] = vpe
                        return vpe

                    pend = None
                    last_pv = 0
                    for (h, kt) in tiles:
                        cih, off = h // 2, (h % 2) * 64
                        p = tig[0] % 2
                        tig[0] += 1
                        bslice = big2[:, p * 512:(p + 1) * 512]
                        k.wt("tensor", *big_free[p])
                        vmm = k.op("tensor", "matmul", out=bslice,
                                   lhsT=kT[off:off + 64, cih, kt * 128:(kt + 1) * 128],
                                   rhs=qT[off:off + 64, cih, qc * 512:(qc + 1) * 512],
                                   start=True, stop=True)
                        k.wt("scalar", "cT", vmm)
                        k.wt("scalar", "cT", exp_free[p])
                        vexp = k.op("scalar", "activation", out=pts[:, p, :], in_=bslice,
                                    func=AF.Exp, bias=mneg8[:], scale=8.0, selfwait=False)
                        big_free[p] = ("cS", vexp)
                        e = kt - 4 * qc
                        gate = ("cS", vexp)
                        if e >= 0:
                            k.wt("vector", "cS", vexp)
                            vmask = k.op("vector", "tensor_tensor", out=pts[:, p, :],
                                         in0=pts[:, p, :], in1=trmask[:, e, :],
                                         op=ALU.mult, selfwait=False)
                            gate = ("cV", vmask)
                        if pend is not None:
                            last_pv = emit_pv_a(*pend)
                        pend = (p, h, kt, gate)
                    last_pv = emit_pv_a(*pend)
                    k.wt("vector", "cT", last_pv)
                    vm1 = 0
                    for t in range(2):
                        vr = k.op("vector", "reciprocal", out=rden[:, t, :], in_=den[t][:])
                        k.wt("vector", "cT", oT_free)
                        vm1 = k.op("vector", "tensor_tensor", out=oT_sb[:, 2 * w + t, :],
                                   in0=yt[t][:], in1=rden[:, t, :], op=ALU.mult)
                    banks_free = vm1
                k.wt("tensor", "cV", banks_free)
                vmm2 = 0
                for s in range(4):
                    pp_out = outps if s % 2 == 0 else big2
                    k.wt("tensor", "cV", out_free[s % 2])
                    if s % 2 == 1:
                        k.wt("tensor", *big_free[0])
                        k.wt("tensor", *big_free[1])
                    for c in range(4):
                        for n0 in (0, 512):
                            vmm2 = k.op("tensor", "matmul", out=pp_out[:, n0:n0 + 512],
                                        lhsT=oT_sb[:, c, s * 128:(s + 1) * 128],
                                        rhs=wout[:, c, n0:n0 + 512],
                                        start=(c == 0), stop=(c == 3))
                    k.wt("vector", "cT", vmm2)
                    k.wtd("vector", vxl)
                    k.wtd("vector", st_out)
                    vadd = k.op("vector", "tensor_tensor", out=xnew[:], in0=pp_out[:],
                                in1=xr_sb[:, s, :], op=ALU.add)
                    out_free[s % 2] = vadd
                    last_vadd = vadd
                    k.wt("sync", "cV", vadd)
                    st_out = k.dma("sync", out=xout[qc * 512 + s * 128:qc * 512 + (s + 1) * 128, :], in_=xnew[:])
                oT_free = vmm2
                big_free[0] = big_free[1] = ("cV", out_free[1])
        k.fence()



WSPECS_L0A = [
    ("L0_attn_ln_w", (DM,), F32),
    ("L0_attn_ln_b", (DM,), F32),
    ("L0_attn_w_qkv", (DM, 1536), F32),
    ("L0_attn_w_out", (512, DM), F32),
]
TSPECS = [
    ("t_ident", (128, 128), BF16),
    ("t_trmask", (4, 128, 512), BF16),
    ("t_pvlr", (128, 2, 128), BF16),
    ("t_pvq4", (128, 4, 128), BF16),
]


def build_l0a(seq):
    nc = bass.Bass("TRN2", target_bir_lowering=False)
    es = contextlib.ExitStack()
    with es:
        k = K(nc, es, seq)
        D = k.dram
        D["xin"] = nc.dram_tensor("xin", [seq, DM], F32, kind="ExternalInput").ap()
        for nm, shp, dt in WSPECS_L0A + TSPECS:
            D[nm] = nc.dram_tensor(nm, list(shp), dt, kind="ExternalInput").ap()
        D["hscr"] = nc.dram_tensor("hscr", [seq, 2048], BF16, kind="Internal").ap()
        D["xout"] = nc.dram_tensor("xout", [seq, DM], F32, kind="ExternalOutput").ap()
        emit_l0a(k, D["xin"], D["xout"])
        k.build()
    return nc





class _CloseBefore:

    def __init__(self, es):
        self.es = es

    def __enter__(self):
        return self

    def __exit__(self, *a):
        self.es.close()
        return False


def rms_tile(k, x_ap, out_bf, wbc, sq_scr, red, eps=1e-5, n=DM):
    h = n // 2
    k.op("vector", "tensor_tensor", out=sq_scr[:, 0:h], in0=x_ap[:, 0:h], in1=x_ap[:, 0:h],
         op=ALU.mult)
    k.op("vector", "tensor_reduce", out=red[:, 0:1], in_=sq_scr[:, 0:h], axis=AX.X, op=ALU.add)
    k.op("vector", "tensor_tensor", out=sq_scr[:, 0:h], in0=x_ap[:, h:n], in1=x_ap[:, h:n],
         op=ALU.mult)
    k.op("vector", "tensor_reduce", out=red[:, 2:3], in_=sq_scr[:, 0:h], axis=AX.X, op=ALU.add)
    k.op("vector", "tensor_tensor", out=red[:, 0:1], in0=red[:, 0:1], in1=red[:, 2:3], op=ALU.add)
    k.op("vector", "tensor_scalar", out=red[:, 0:1], in0=red[:, 0:1], scalar1=1.0 / n,
         scalar2=None, op0=ALU.mult)
    rsqrt_cols(k, red, slice(0, 1), slice(1, 2), eps)
    return k.op("vector", "scalar_tensor_tensor", out=out_bf[:], in0=x_ap,
                scalar=red[:, 1:2], in1=wbc[:], op0=ALU.mult, op1=ALU.mult)


def rope_pairs(k, qf_view, cos_bc, sin_bc, scrE, scrO, scrE2, scrO2):
    nh, ng = qf_view.shape[1], qf_view.shape[2]
    qe = qf_view[:, :, :, 0]
    qo = qf_view[:, :, :, 1]
    sE = scrE[:, 0:nh * ng].rearrange("p (h g) -> p h g", h=nh)
    sO = scrO[:, 0:nh * ng].rearrange("p (h g) -> p h g", h=nh)
    sE2 = scrE2[:, 0:nh * ng].rearrange("p (h g) -> p h g", h=nh)
    sO2 = scrO2[:, 0:nh * ng].rearrange("p (h g) -> p h g", h=nh)
    k.op("vector", "tensor_tensor", out=sE, in0=qe, in1=cos_bc, op=ALU.mult)
    k.op("vector", "tensor_tensor", out=sE2, in0=qo, in1=sin_bc, op=ALU.mult)
    k.op("vector", "tensor_tensor", out=sO, in0=qo, in1=cos_bc, op=ALU.mult)
    k.op("vector", "tensor_tensor", out=sO2, in0=qe, in1=sin_bc, op=ALU.mult)
    k.op("vector", "tensor_tensor", out=qe, in0=sE, in1=sE2, op=ALU.subtract)
    return k.op("vector", "tensor_tensor", out=qo, in0=sO, in1=sO2, op=ALU.add)


def emit_l0f(k, xin, xout):
    nc = k.nc
    seq, nt, nq = k.seq, k.nt, k.nq
    D = k.dram
    H = 16
    hscr, zb, r1b, macc = D["hscr"], D["zb"], D["r1b"], D["macc"]

    with contextlib.ExitStack() as ph:
        k.fence()
        ident = ph.enter_context(nc.sbuf_tensor("f_ident", [128, 128], BF16))
        k.dma("sync", out=ident[:], in_=D["t_ident"][:, :])
        red = ph.enter_context(nc.sbuf_tensor("f_red", [128, 4], F32))
        stats = ph.enter_context(nc.sbuf_tensor("f_stats", [128, 12], F32))
        mvst = ph.enter_context(nc.sbuf_tensor("f_mv", [128, 4], F32))

        with contextlib.ExitStack() as s1:
            k.fence()
            w1bc, vbc = bcast_row(k, "f_w1", D["L0_ffn_w_ln1"][None, :], DM, s1)
            x_sb = s1.enter_context(nc.sbuf_tensor("f_x", [128, 2, DM], F32))
            z_sb = s1.enter_context(nc.sbuf_tensor("f_z", [128, DM], F32))
            sq = s1.enter_context(nc.sbuf_tensor("f_sq", [128, DM], F32))
            tbf = s1.enter_context(nc.sbuf_tensor("f_tbf", [128, DM], BF16))
            k.wt("vector", "cV", vbc)
            last_st = None
            zst = None
            prev = {0: 0, 1: 0}
            for tc in range(nt):
                b = tc % 2
                k.wt("sync", "cV", prev[b])
                vl = k.dma("sync", out=x_sb[:, b, :], in_=xin[tc * 128:(tc + 1) * 128, :])
                k.wtd("vector", vl)
                k.wtd("vector", zst)
                prev[b] = layer_norm_tile(k, x_sb[:, b, :], z_sb[:], stats, mvst)
                k.wt("sync", "cV", prev[b])
                zst = k.dma("sync", out=zb[tc * 128:(tc + 1) * 128, :], in_=z_sb[:])
                k.wtd("vector", last_st)
                vn = rms_tile(k, z_sb[:], tbf, w1bc, sq, red)
                prev[b] = vn
                k.wt("sync", "cV", vn)
                last_st = k.dma("sync", out=hscr[tc * 128:(tc + 1) * 128, 0:DM], in_=tbf[:])

        with contextlib.ExitStack() as satt:
            kT = satt.enter_context(nc.sbuf_tensor("f_kT", [128, 8, seq], BF16))
            kv_aug = satt.enter_context(nc.sbuf_tensor("f_kvaug", [128, nt, 16, 33], BF16))
            sproj = contextlib.ExitStack()
            satt.enter_context(_CloseBefore(sproj))
            latT = sproj.enter_context(nc.sbuf_tensor("f_latT", [128, seq], BF16))

            with contextlib.ExitStack() as s2:
                k.fence()
                tT = s2.enter_context(nc.sbuf_tensor("f_tT", [128, 8, seq], BF16))
                k.wtd("sync", last_st)
                vtr = []
                for c in range(8):
                    vtr.append(k.dma("sync", out=tT[:, c, :], in_=hscr[:, c * 128:(c + 1) * 128],
                                     transpose=True))
                wkva = s2.enter_context(nc.sbuf_tensor("f_wkva", [128, 8, 160], BF16))
                vw1 = k.dma("gpsimd", out=wkva[:], in_=D["L0_ffn_Wkva"].rearrange("(c p) n -> p c n", p=128))
                wq = s2.enter_context(nc.sbuf_tensor("f_wq", [128, 8, DM], BF16))
                vw2 = k.dma("gpsimd", out=wq[:], in_=D["L0_ffn_Wq"].rearrange("(c p) n -> p c n", p=128))
                cosP = s2.enter_context(nc.sbuf_tensor("f_cosP", [128, nt, 16], F32))
                sinP = s2.enter_context(nc.sbuf_tensor("f_sinP", [128, nt, 16], F32))
                vc1 = k.dma("sync", out=cosP[:], in_=D["t_mla_cos"].rearrange("(t p) g -> p t g", p=128))
                vc2 = k.dma("sync", out=sinP[:], in_=D["t_mla_sin"].rearrange("(t p) g -> p t g", p=128))
                psq = s2.enter_context(nc.psum_tensor("f_psq", [128, 1024], F32))
                psl = s2.enter_context(nc.psum_tensor("f_psl", [128, 512], F32))
                psr = s2.enter_context(nc.psum_tensor("f_psr", [128, 32], F32))
                qf = s2.enter_context(nc.sbuf_tensor("f_qf", [128, DM], F32))
                qb = s2.enter_context(nc.sbuf_tensor("f_qb", [128, DM], BF16))
                krf = s2.enter_context(nc.sbuf_tensor("f_krf", [128, 32], F32))
                krb = s2.enter_context(nc.sbuf_tensor("f_krb", [128, 32], BF16))
                scrE = s2.enter_context(nc.sbuf_tensor("f_scrE", [128, 256], F32))
                scrO = s2.enter_context(nc.sbuf_tensor("f_scrO", [128, 256], F32))
                scrE2 = s2.enter_context(nc.sbuf_tensor("f_scrE2", [128, 256], F32))
                scrO2 = s2.enter_context(nc.sbuf_tensor("f_scrO2", [128, 256], F32))
                k.wtd("tensor", vtr)
                k.wtd("tensor", [vw1, vw2])
                k.wtd("vector", [vc1, vc2])
                vlat_ev = None
                for ch in range(seq // 512):
                    vm = None
                    for c in range(8):
                        vm = k.op("tensor", "matmul", out=psl[:],
                                  lhsT=wkva[:, c, 0:128], rhs=tT[:, c, ch * 512:(ch + 1) * 512],
                                  start=(c == 0), stop=(c == 7))
                    k.wt("scalar", "cT", vm)
                    vlat_ev = k.op("scalar", "copy", out=latT[:, ch * 512:(ch + 1) * 512], in_=psl[:])
                    k.wt("tensor", "cS", vlat_ev)
                vst_q = None
                vst_kr = None
                for tc in range(nt):
                    cs_bc = cosP[:, tc, :].unsqueeze(1)
                    sn_bc = sinP[:, tc, :].unsqueeze(1)
                    vm = None
                    for c in range(8):
                        vm = k.op("tensor", "matmul", out=psr[:],
                                  lhsT=tT[:, c, tc * 128:(tc + 1) * 128], rhs=wkva[:, c, 128:160],
                                  start=(c == 0), stop=(c == 7))
                    k.wt("vector", "cT", vm)
                    k.wtd("vector", vst_kr)
                    k.op("vector", "tensor_copy", out=krf[:], in_=psr[:])
                    vr = rope_pairs(k, krf[:].rearrange("p (h g two) -> p h g two", h=1, two=2),
                                    cs_bc.broadcast_to([128, 1, 16]), sn_bc.broadcast_to([128, 1, 16]),
                                    scrE, scrO, scrE2, scrO2)
                    vkb = k.op("vector", "tensor_copy", out=krb[:], in_=krf[:])
                    k.wt("tensor", "cV", vr)
                    k.wt("sync", "cV", vkb)
                    vst_kr = k.dma("sync", out=hscr[tc * 128:(tc + 1) * 128, 0:32], in_=krb[:])
                    vm = None
                    for half in range(2):
                        for c in range(8):
                            vm = k.op("tensor", "matmul", out=psq[:, half * 512:(half + 1) * 512],
                                      lhsT=tT[:, c, tc * 128:(tc + 1) * 128],
                                      rhs=wq[:, c, half * 512:(half + 1) * 512],
                                      start=(c == 0), stop=(c == 7))
                    k.wt("vector", "cT", vm)
                    k.wtd("vector", vst_q)
                    k.op("vector", "tensor_scalar", out=qf[:], in0=psq[:], scalar1=0.125,
                         scalar2=None, op0=ALU.mult)
                    qv = qf[:].rearrange("p (h d) -> p h d", h=16)[:, :, 32:64]
                    vq2 = rope_pairs(k, qv.rearrange("p h (g two) -> p h g two", two=2),
                                     cs_bc.broadcast_to([128, 16, 16]), sn_bc.broadcast_to([128, 16, 16]),
                                     scrE, scrO, scrE2, scrO2)
                    vqb = k.op("vector", "tensor_copy", out=qb[:], in_=qf[:])
                    k.wt("tensor", "cV", vq2)
                    k.wt("sync", "cV", vqb)
                    vst_q = k.dma("sync", out=hscr[tc * 128:(tc + 1) * 128, DM:2 * DM], in_=qb[:])

            with contextlib.ExitStack() as s3:
                k.fence()
                kvT = s3.enter_context(nc.sbuf_tensor("f_kvT", [128, 4, seq], BF16))
                wkvb = s3.enter_context(nc.sbuf_tensor("f_wkvb", [128, 512], BF16))
                vwb = k.dma("gpsimd", out=wkvb[:], in_=D["L0_ffn_Wkvb"][:, :])
                psA = s3.enter_context(nc.psum_tensor("f_psA", [128, 512], F32))
                psB = s3.enter_context(nc.psum_tensor("f_psB", [128, 512], F32))
                k.wtd("tensor", vwb)
                freed = {0: 0, 1: 0}
                for c4 in range(4):
                    for ch in range(seq // 512):
                        b = (c4 * (seq // 512) + ch) % 2
                        pp = psA if b == 0 else psB
                        k.wt("tensor", "cS", freed[b])
                        vm = k.op("tensor", "matmul", out=pp[:],
                                  lhsT=wkvb[:, c4 * 128:(c4 + 1) * 128],
                                  rhs=latT[:, ch * 512:(ch + 1) * 512], start=True, stop=True)
                        k.wt("scalar", "cT", vm)
                        freed[b] = k.op("scalar", "copy", out=kvT[:, c4, ch * 512:(ch + 1) * 512], in_=pp[:])
                k.fence(engines=("sync",))
                vasm = []
                for c in range(8):
                    for par, h in ((0, 2 * c), (64, 2 * c + 1)):
                        vasm.append(k.dma("sync", out=kT[par:par + 32, c, :],
                                          in_=kvT[(h % 4) * 32:(h % 4) * 32 + 32, h // 4, :]))
                krT = s3.enter_context(nc.sbuf_tensor("f_krT", [128, seq], BF16))
                vkt = k.dma("sync", out=krT[:], in_=hscr[:, 0:128], transpose=True)
                k.wtd("sync", vkt)
                for c in range(8):
                    for par in (32, 96):
                        vasm.append(k.dma("sync", out=kT[par:par + 32, c, :],
                                          in_=krT[0:32, :]))
                vko = k.op("vector", "memset", ap=kv_aug[:, :, :, 32], constant=1.0)
                k.wt("tensor", "cS", k.sems["cS"].n)
                freed = {0: 0, 1: 0}
                for tc in range(nt):
                    b = tc % 2
                    pp = psA if b == 0 else psB
                    k.wt("tensor", "cV", freed[b] if freed[b] else vko)
                    vm = k.op("tensor", "matmul", out=pp[:], lhsT=latT[:, tc * 128:(tc + 1) * 128],
                              rhs=wkvb[:], start=True, stop=True)
                    k.wt("vector", "cT", vm)
                    freed[b] = k.op("vector", "tensor_copy",
                                    out=kv_aug[:, tc, :, 0:32],
                                    in_=pp[:].rearrange("p (h d) -> p h d", h=16))
                k.wtd("sync", vasm)

            sproj.close()
            with contextlib.ExitStack() as s4:
                k.fence()
                qT = s4.enter_context(nc.sbuf_tensor("f_qT", [128, 8, seq], BF16))
                vqt = []
                for c in range(8):
                    vqt.append(k.dma("sync", out=qT[:, c, :], in_=hscr[:, DM + c * 128:DM + (c + 1) * 128],
                                     transpose=True))
                wosub = s4.enter_context(nc.sbuf_tensor("f_wosub", [128, 4, DM], BF16))
                wo4 = D["L0_ffn_Wo"].rearrange("(h t d) n -> t h d n", t=2, d=32)[0]
                vwo = []
                for h in range(16):
                    vwo.append(k.dma("gpsimd", out=wosub[(h % 4) * 32:(h % 4) * 32 + 32, h // 4, :],
                                     in_=wo4[h]))
                trm = s4.enter_context(nc.sbuf_tensor("f_trm", [128, 4, 512], BF16))
                vmk = k.dma("sync", out=trm[:], in_=D["t_trmask"].rearrange("e p q -> p e q"))
                w2bc, _ = bcast_row(k, "f_w2", D["L0_ffn_w_ln2"][None, :], DM, s4)
                mbias = s4.enter_context(nc.sbuf_tensor("f_mbias", [128, 1], F32))
                vmb = k.op("vector", "memset", ap=mbias[:], constant=-12.0)
                sc = s4.enter_context(nc.psum_tensor("f_sc", [128, 3, 512], F32))
                ytf = [s4.enter_context(nc.psum_tensor(f"f_ytf{t}", [128, 512], F32))
                       for t in range(2)]
                denf = [s4.enter_context(nc.psum_tensor(f"f_denf{t}", [128, 512], F32))
                        for t in range(2)]
                pts = s4.enter_context(nc.sbuf_tensor("f_pts", [128, 3, 512], BF16))
                pvq4 = s4.enter_context(nc.sbuf_tensor("f_pvq4", [128, 4, 128], BF16))
                z128f = s4.enter_context(nc.sbuf_tensor("f_z128", [128, 128], BF16))
                rdf = s4.enter_context(nc.sbuf_tensor("f_rdf", [128, 2, 512], F32))
                oTf = s4.enter_context(nc.sbuf_tensor("f_oTf", [128, 4, 512], BF16))
                zt = s4.enter_context(nc.sbuf_tensor("f_zt", [128, 2, DM], F32))
                r1sb = s4.enter_context(nc.sbuf_tensor("f_r1", [128, DM], F32))
                usb = s4.enter_context(nc.sbuf_tensor("f_u", [128, DM], BF16))
                sqc = s4.enter_context(nc.sbuf_tensor("f_sqc", [128, 512], F32))
                vq4 = k.dma("sync", out=pvq4[:], in_=D["t_pvq4"][:, :, :])
                vzf = k.op("vector", "memset", ap=z128f[:], constant=0.0)
                k.wtd("tensor", vqt)
                k.wtd("tensor", vwo)
                k.wtd("tensor", vq4)
                k.wt("tensor", "cV", vzf)
                k.wtd("vector", vmk)
                k.wt("scalar", "cV", vmb)

                scfree = [None, None, None]
                sci = [0]
                ptsfree = [None, None, None]
                ptsi = [0]
                fbanks_free = 0
                oTf_free = 0
                st_r1 = {0: None, 1: None}
                st_u = {0: None, 1: None}
                zprev = {0: 0, 1: 0}

                def sc_slot(eng):
                    s = sci[0] % 3
                    sci[0] += 1
                    if scfree[s]:
                        k.wt(eng, *scfree[s])
                    return s

                for qc in range(nq):
                    nkt = 4 * (qc + 1)
                    q0, q1 = qc * 512, (qc + 1) * 512
                    for hg in range(4):
                        k.wt("tensor", "cV", fbanks_free)
                        for t in range(2):
                            k.op("tensor", "matmul", out=ytf[t][:, 0:128],
                                 lhsT=z128f[:], rhs=z128f[:], start=True,
                                 stop=False, skip_group_check=True)
                        tiles = [(hg * 4 + hi, kt) for hi in range(4) for kt in range(nkt)]

                        def emit_pv(p, nkt=nkt):
                            ps2, h2, kt2, gate2 = p
                            hi2 = h2 % 4
                            t2, m2 = hi2 // 2, hi2 % 2
                            k.wt("tensor", *gate2)
                            k.op("tensor", "matmul",
                                 out=ytf[t2][m2 * 32:m2 * 32 + 32, :],
                                 lhsT=kv_aug[:, kt2, h2, 0:32],
                                 rhs=pts[:, ps2, :],
                                 start=False, stop=(m2 == 1 and kt2 == nkt - 1),
                                 skip_group_check=True)
                            vpv = k.op("tensor", "matmul", out=denf[t2][:],
                                       lhsT=pvq4[:, m2, :], rhs=pts[:, ps2, :],
                                       start=(m2 == 0 and kt2 == 0),
                                       stop=(m2 == 1 and kt2 == nkt - 1),
                                       skip_group_check=True)
                            ptsfree[ps2] = ("cT", vpv)
                            return vpv

                        pend = None
                        last_pv = 0
                        for (h, kt) in tiles:
                            cih, off = h // 2, (h % 2) * 64
                            ss = sc_slot("tensor")
                            vmm = k.op("tensor", "matmul", out=sc[:, ss, :],
                                       lhsT=kT[off:off + 64, cih, kt * 128:(kt + 1) * 128],
                                       rhs=qT[off:off + 64, cih, q0:q1],
                                       start=True, stop=True)
                            ps = ptsi[0] % 3
                            ptsi[0] += 1
                            if ptsfree[ps]:
                                k.wt("scalar", *ptsfree[ps])
                            k.wt("scalar", "cT", vmm)
                            vexp = k.op("scalar", "activation", out=pts[:, ps, :],
                                        in_=sc[:, ss, :], func=AF.Exp, bias=mbias[:],
                                        scale=1.0, selfwait=False)
                            scfree[ss] = ("cS", vexp)
                            e = kt - 4 * qc
                            gate = ("cS", vexp)
                            if e >= 0:
                                k.wt("vector", "cS", vexp)
                                vmsk = k.op("vector", "tensor_tensor", out=pts[:, ps, :],
                                            in0=pts[:, ps, :], in1=trm[:, e, :], op=ALU.mult,
                                            selfwait=False)
                                gate = ("cV", vmsk)
                            if pend is not None:
                                last_pv = emit_pv(pend)
                            pend = (ps, h, kt, gate)
                        last_pv = emit_pv(pend)
                        k.wt("vector", "cT", last_pv)
                        vm1 = 0
                        for t in range(2):
                            k.op("vector", "reciprocal", out=rdf[0:64, t, :],
                                 in_=denf[t][0:64, :])
                            k.wt("vector", "cT", oTf_free)
                            vm1 = k.op("vector", "tensor_tensor",
                                       out=oTf[64 * t:64 * t + 64, hg, :],
                                       in0=ytf[t][0:64, :],
                                       in1=rdf[0:64, t, :], op=ALU.mult)
                        fbanks_free = vm1
                    for qs in range(4):
                        b = qs % 2
                        tok0 = q0 + qs * 128
                        k.wt("sync", "cV", zprev[b])
                        vzl = k.dma("sync", out=zt[:, b, :], in_=zb[tok0:tok0 + 128, :])
                        if st_r1[1]:
                            k.wt("vector", *st_r1[1])
                        k.wt("tensor", "cV", fbanks_free)
                        sa = sc_slot("tensor")
                        sb2 = sc_slot("tensor")
                        vmo = None
                        for c in range(4):
                            k.op("tensor", "matmul", out=sc[:, sa, :],
                                 lhsT=oTf[:, c, qs * 128:(qs + 1) * 128],
                                 rhs=wosub[:, c, 0:512],
                                 start=(c == 0), stop=(c == 3), skip_group_check=True)
                            vmo = k.op("tensor", "matmul", out=sc[:, sb2, :],
                                       lhsT=oTf[:, c, qs * 128:(qs + 1) * 128],
                                       rhs=wosub[:, c, 512:1024],
                                       start=(c == 0), stop=(c == 3), skip_group_check=True)
                        oTf_free = vmo
                        k.wt("vector", "cT", vmo)
                        k.wtd("vector", vzl)
                        k.wtd("vector", st_r1[0])
                        k.op("vector", "tensor_tensor", out=r1sb[:, 0:512], in0=sc[:, sa, :],
                             in1=zt[:, b, 0:512], op=ALU.add, selfwait=False)
                        vr1 = k.op("vector", "tensor_tensor", out=r1sb[:, 512:1024],
                                   in0=sc[:, sb2, :], in1=zt[:, b, 512:1024], op=ALU.add)
                        zprev[b] = vr1
                        scfree[sa] = ("cV", vr1)
                        scfree[sb2] = ("cV", vr1)
                        k.wt("sync", "cV", vr1)
                        st_r1[0] = k.dma("sync", out=r1b[tok0:tok0 + 128, :], in_=r1sb[:])
                        k.wtd("vector", st_u[0])
                        vu = rms_tile(k, r1sb[:], usb, w2bc, sqc, red)
                        st_r1[1] = ("cV", vu)
                        k.wt("sync", "cV", vu)
                        st_u[0] = k.dma("sync", out=hscr[tok0:tok0 + 128, 0:DM], in_=usb[:])


        with contextlib.ExitStack() as s5:
            k.fence()
            uT = s5.enter_context(nc.sbuf_tensor("f_uT", [128, 8, seq], BF16))
            vut = []
            for c in range(8):
                vut.append(k.dma("sync", out=uT[:, c, :], in_=hscr[:, c * 128:(c + 1) * 128],
                                 transpose=True))
            wgate = s5.enter_context(nc.sbuf_tensor("f_wgate", [128, 8, 8], BF16))
            vwg = k.dma("gpsimd", out=wgate[:], in_=D["L0_ffn_Wgate"].rearrange("(c p) n -> p c n", p=128))
            ebias, vbc2 = bcast_row(k, "f_ebias", D["L0_ffn_expert_bias"][None, :], 8, s5)
            comb = s5.enter_context(nc.sbuf_tensor("f_comb", [128, nt, 8], F32))
            lg = s5.enter_context(nc.sbuf_tensor("f_lg", [128, 8], F32))
            cscr = s5.enter_context(nc.sbuf_tensor("f_cscr", [128, 8], F32))
            cscr2 = s5.enter_context(nc.sbuf_tensor("f_cscr2", [128, 8], F32))
            cred = s5.enter_context(nc.sbuf_tensor("f_cred", [128, 4], F32))
            psg = s5.enter_context(nc.psum_tensor("f_psg", [128, 512], F32))
            psu = s5.enter_context(nc.psum_tensor("f_psu", [128, 512], F32))
            ptp2 = s5.enter_context(nc.psum_tensor("f_ptp2", [128, 512], BF16))
            dn = {j: s5.enter_context(nc.psum_tensor(f"f_dn{j}", [128, 1024], F32))
                  for j in range(2)}
            gs = s5.enter_context(nc.sbuf_tensor("f_gs", [128, 512], F32))
            hb = s5.enter_context(nc.sbuf_tensor("f_hb", [128, 512], BF16))
            hT = s5.enter_context(nc.sbuf_tensor("f_hT", [128, 2, 512], BF16))
            nth = nt // 2
            acc = s5.enter_context(nc.sbuf_tensor("f_acc", [128, nth, DM], F32))
            wg_sb = s5.enter_context(nc.sbuf_tensor("f_wgsb", [128, 2, 8, 512], BF16))
            wu_sb = s5.enter_context(nc.sbuf_tensor("f_wusb", [128, 2, 8, 512], BF16))
            wd_sb = s5.enter_context(nc.sbuf_tensor("f_wdsb", [128, 2, 4, DM], BF16))
            xa = s5.enter_context(nc.sbuf_tensor("f_xa", [128, 2, DM], F32))
            xb2 = s5.enter_context(nc.sbuf_tensor("f_xb", [128, 2, DM], F32))
            xo = s5.enter_context(nc.sbuf_tensor("f_xo", [128, 2, DM], F32))
            k.wtd("tensor", vut)
            k.wtd("tensor", vwg)
            k.wt("vector", "cV", vbc2)
            for tc in range(nt):
                vm = None
                for c in range(8):
                    vm = k.op("tensor", "matmul", out=psg[:, 0:8], lhsT=uT[:, c, tc * 128:(tc + 1) * 128],
                              rhs=wgate[:, c, :], start=(c == 0), stop=(c == 7))
                k.wt("vector", "cT", vm)
                k.op("vector", "tensor_tensor", out=lg[:], in0=psg[:, 0:8], in1=ebias[:], op=ALU.add)
                vfree = k.sems["cV"].n
                k.wt("tensor", "cV", vfree)
                k.op("vector", "tensor_reduce", out=cred[:, 0:1], in_=lg[:], axis=AX.X, op=ALU.max)
                k.op("vector", "tensor_scalar", out=cscr[:], in0=lg[:], scalar1=cred[:, 0:1],
                     scalar2=None, op0=ALU.is_equal)
                k.op("vector", "scalar_tensor_tensor", out=cscr[:], in0=cscr[:], scalar=-1e30,
                     in1=lg[:], op0=ALU.mult, op1=ALU.add)
                k.op("vector", "tensor_reduce", out=cred[:, 1:2], in_=cscr[:], axis=AX.X, op=ALU.max)
                k.op("vector", "tensor_scalar", out=cscr2[:], in0=lg[:], scalar1=cred[:, 1:2],
                     scalar2=None, op0=ALU.is_ge)
                vng = k.op("vector", "tensor_scalar", out=cred[:, 2:3], in0=cred[:, 0:1],
                           scalar1=-1.0, scalar2=None, op0=ALU.mult)
                k.wt("scalar", "cV", vng)
                vex = k.op("scalar", "activation", out=cscr[:], in_=lg[:], func=AF.Exp,
                           bias=cred[:, 2:3], scale=1.0)
                k.wt("vector", "cS", vex)
                k.op("vector", "tensor_tensor", out=cscr[:], in0=cscr[:], in1=cscr2[:], op=ALU.mult)
                k.op("vector", "tensor_reduce", out=cred[:, 3:4], in_=cscr[:], axis=AX.X, op=ALU.add)
                k.op("vector", "reciprocal", out=cred[:, 3:4], in_=cred[:, 3:4])
                k.op("vector", "tensor_scalar", out=comb[:, tc, :], in0=cscr[:],
                     scalar1=cred[:, 3:4], scalar2=None, op0=ALU.mult)
            wtoks = {0: 0, 1: 0}
            st_out = {0: None, 1: None}
            xprev2 = {0: 0, 1: 0}
            vgu_done = k.sems["cV"].n
            silu_done = 0
            dn_free = {0: 0, 1: 0}
            hT_free = {0: 0, 1: 0}
            acc_rd2 = 0
            for half in range(2):
                t0h = half * nth
                for e in range(9):
                    b = e % 2
                    if e == 0:
                        srcs = (D["L0_ffn_sWg"][0], D["L0_ffn_sWu"][0], D["L0_ffn_sWd"][0])
                    else:
                        srcs = (D["L0_ffn_Wg"][e - 1], D["L0_ffn_Wu"][e - 1], D["L0_ffn_Wd"][e - 1])
                    k.wt("gpsimd", "cT", wtoks[b])
                    t1 = k.dma("gpsimd", out=wg_sb[:, b], in_=srcs[0].rearrange("(c p) n -> p c n", p=128))
                    t2 = k.dma("gpsimd", out=wu_sb[:, b], in_=srcs[1].rearrange("(c p) n -> p c n", p=128))
                    t3 = k.dma("gpsimd", out=wd_sb[:, b], in_=srcs[2].rearrange("(c p) n -> p c n", p=128))
                    k.wtd("tensor", [t1, t2, t3])
                    pend_dn = None

                    def emit_down(pd):
                        sl, tci, ve = pd
                        k.wt("tensor", "cV", ve)
                        k.wt("tensor", "cV", dn_free[sl])
                        vdm = None
                        for c in range(4):
                            for n0 in (0, 512):
                                vdm = k.op("tensor", "matmul", out=dn[sl][:, n0:n0 + 512],
                                           lhsT=hT[:, sl, c * 128:(c + 1) * 128],
                                           rhs=wd_sb[:, b, c, n0:n0 + 512],
                                           start=(c == 0), stop=(c == 3), skip_group_check=True)
                        hT_free[sl] = vdm
                        k.wt("vector", "cT", vdm)
                        if e == 0:
                            k.wt("vector", "cV", acc_rd2)
                            vcp = k.op("vector", "tensor_copy", out=acc[:, tci, :], in_=dn[sl][:],
                                       selfwait=False)
                        else:
                            vcp = k.op("vector", "scalar_tensor_tensor", out=acc[:, tci, :],
                                       in0=dn[sl][:], scalar=comb[:, t0h + tci, e - 1:e],
                                       in1=acc[:, tci, :], op0=ALU.mult, op1=ALU.add, selfwait=False)
                        dn_free[sl] = vcp
                        return vcp

                    for ti in range(nth):
                        tc = t0h + ti
                        sl = ti % 2
                        k.wt("tensor", "cV", vgu_done)
                        k.wt("tensor", "cS", silu_done)
                        vg = None
                        for c in range(8):
                            vg = k.op("tensor", "matmul", out=psg[:], lhsT=uT[:, c, tc * 128:(tc + 1) * 128],
                                      rhs=wg_sb[:, b, c, :], start=(c == 0), stop=(c == 7))
                        for c in range(8):
                            vg = k.op("tensor", "matmul", out=psu[:], lhsT=uT[:, c, tc * 128:(tc + 1) * 128],
                                      rhs=wu_sb[:, b, c, :], start=(c == 0), stop=(c == 7))
                        k.wt("scalar", "cT", vg)
                        vgs = k.op("scalar", "activation", out=gs[:], in_=psg[:], func=AF.Silu,
                                   selfwait=False)
                        silu_done = vgs
                        k.wt("vector", "cS", vgs)
                        vhb = k.op("vector", "tensor_tensor", out=hb[:], in0=psu[:], in1=gs[:],
                                   op=ALU.mult)
                        vgu_done = vhb
                        vev = None
                        for c in range(4):
                            k.wt("tensor", "cV", vhb if c == 0 else vev)
                            if c == 0:
                                k.wt("tensor", "cT", hT_free[sl])
                            vtp = k.op("tensor", "transpose", out=ptp2[:, c * 128:(c + 1) * 128],
                                       in_=hb[:, c * 128:(c + 1) * 128], identity=ident[:])
                            k.wt("vector", "cT", vtp)
                            vev = k.op("vector", "tensor_copy", out=hT[:, sl, c * 128:(c + 1) * 128],
                                       in_=ptp2[:, c * 128:(c + 1) * 128])
                        if pend_dn is not None:
                            emit_down(pend_dn)
                        pend_dn = (sl, ti, vev)
                    vlast_acc = emit_down(pend_dn)
                    wtoks[b] = k.sems["cT"].n
                for ti in range(nth):
                    tc = t0h + ti
                    fb = ti % 2
                    k.wt("sync", "cV", xprev2[fb])
                    va = k.dma("sync", out=xa[:, fb, :], in_=xin[tc * 128:(tc + 1) * 128, :])
                    vb_ = k.dma("sync", out=xb2[:, fb, :], in_=r1b[tc * 128:(tc + 1) * 128, :])
                    k.wtd("vector", [va, vb_])
                    k.wtd("vector", st_out[fb])
                    k.wt("vector", "cV", vlast_acc)
                    k.op("vector", "tensor_tensor", out=xo[:, fb, :], in0=xa[:, fb, :],
                         in1=xb2[:, fb, :], op=ALU.add)
                    vo2 = k.op("vector", "tensor_tensor", out=xo[:, fb, :], in0=xo[:, fb, :],
                               in1=acc[:, ti, :], op=ALU.add)
                    xprev2[fb] = vo2
                    acc_rd2 = vo2
                    k.wt("sync", "cV", vo2)
                    st_out[fb] = k.dma("sync", out=xout[tc * 128:(tc + 1) * 128, :], in_=xo[:, fb, :])
        k.fence()


WSPECS_L0F = [
    ("L0_ffn_w_ln1", (DM,), F32),
    ("L0_ffn_Wq", (DM, DM), F32),
    ("L0_ffn_Wkva", (DM, 160), F32),
    ("L0_ffn_Wkvb", (128, 512), F32),
    ("L0_ffn_Wo", (DM, DM), F32),
    ("L0_ffn_w_ln2", (DM,), F32),
    ("L0_ffn_Wgate", (DM, 8), F32),
    ("L0_ffn_expert_bias", (8,), F32),
    ("L0_ffn_Wg", (8, DM, 512), F32),
    ("L0_ffn_Wu", (8, DM, 512), F32),
    ("L0_ffn_Wd", (8, 512, DM), F32),
    ("L0_ffn_sWg", (1, DM, 512), F32),
    ("L0_ffn_sWu", (1, DM, 512), F32),
    ("L0_ffn_sWd", (1, 512, DM), F32),
]


def tspecs_l0f(seq):
    return [
        ("t_ident", (128, 128), BF16),
        ("t_trmask", (4, 128, 512), BF16),
        ("t_mla_cos", (seq, 16), F32),
        ("t_mla_sin", (seq, 16), F32),
    ]


def build_l0f(seq):
    nc = bass.Bass("TRN2", target_bir_lowering=False)
    es = contextlib.ExitStack()
    with es:
        k = K(nc, es, seq)
        D = k.dram
        D["xin"] = nc.dram_tensor("xin", [seq, DM], F32, kind="ExternalInput").ap()
        for nm, shp, dt in WSPECS_L0F + tspecs_l0f(seq):
            D[nm] = nc.dram_tensor(nm, list(shp), dt, kind="ExternalInput").ap()
        D["hscr"] = nc.dram_tensor("hscr", [seq, 2048], BF16, kind="Internal").ap()
        D["zb"] = nc.dram_tensor("zb", [seq, DM], F32, kind="Internal").ap()
        D["r1b"] = nc.dram_tensor("r1b", [seq, DM], F32, kind="Internal").ap()
        D["macc"] = nc.dram_tensor("macc", [seq, DM], F32, kind="Internal").ap()
        D["xout"] = nc.dram_tensor("xout", [seq, DM], F32, kind="ExternalOutput").ap()
        emit_l0f(k, D["xin"], D["xout"])
        k.build()
    return nc





def l1a_host_tables(seq):
    H = 8
    lam = np.log1p(-2.0 ** (-5.0 - np.arange(H, dtype=np.float64)))
    i = np.arange(seq, dtype=np.float64)
    c = (1.0 - np.exp((i[None, :] + 1.0) * lam[:, None])) / (1.0 - np.exp(lam))[:, None]
    rfq = np.exp((i % 512)[None, :] * lam[:, None]) / np.sqrt(c)
    rfk = np.exp(-(i % 128)[None, :] * lam[:, None]) / np.sqrt(128.0)
    lin = np.linspace(0.0, 1.0, 64)
    theta = 1.0 / 10000.0 ** lin
    ang = np.outer(i, theta)
    cosv, sinv = np.cos(ang), np.sin(ang)
    t = {}
    t["t_l1_cosq"] = (cosv[None] * rfq[:, :, None]).astype(np.float32)
    t["t_l1_sinq"] = (sinv[None] * rfq[:, :, None]).astype(np.float32)
    t["t_l1_cosk"] = (cosv[None] * rfk[:, :, None]).astype(np.float32)
    t["t_l1_sink"] = (sinv[None] * rfk[:, :, None]).astype(np.float32)
    nq = seq // 512
    ntk = seq // 128
    bsc = np.exp((512.0 * np.arange(nq)[None, :, None] - 128.0 * np.arange(ntk)[None, None, :])
                 * lam[:, None, None])
    return t, bsc


def emit_l1a(k, xin, xout):
    nc = k.nc
    seq, nt, nq = k.seq, k.nt, k.nq
    D = k.dram
    H = 8
    hscr, hscr2, vb = D["hscr"], D["hscr2"], D["vb"]
    _, bsc = l1a_host_tables(seq)
    EPS = 1e-5

    with contextlib.ExitStack() as ph:
        k.fence()
        ident = ph.enter_context(nc.sbuf_tensor("r_ident", [128, 128], BF16))
        ones_col = ph.enter_context(nc.sbuf_tensor("r_ones", [128, 1], BF16))
        trmask = ph.enter_context(nc.sbuf_tensor("r_trmask", [128, 4, 512], BF16))
        k.dma("sync", out=ident[:], in_=D["t_ident"][:, :])
        k.dma("sync", out=trmask[:], in_=D["t_trmask"].rearrange("e p q -> p e q"))
        k.op("vector", "memset", ap=ones_col[:], constant=1.0)
        stats = ph.enter_context(nc.sbuf_tensor("r_stats", [128, 12], F32))
        mvst = ph.enter_context(nc.sbuf_tensor("r_mv", [128, 4], F32))
        xnT = ph.enter_context(nc.sbuf_tensor("r_xnT", [128, 8, seq], BF16))

        with contextlib.ExitStack() as s1:
            k.fence()
            x_sb = s1.enter_context(nc.sbuf_tensor("r_x", [128, 2, DM], F32))
            xn = s1.enter_context(nc.sbuf_tensor("r_xn", [128, DM], BF16))
            last_st = None
            prev = {0: 0, 1: 0}
            for tc in range(nt):
                b = tc % 2
                k.wt("sync", "cV", prev[b])
                vl = k.dma("sync", out=x_sb[:, b, :], in_=xin[tc * 128:(tc + 1) * 128, :])
                k.wtd("vector", vl)
                k.wtd("vector", last_st)
                vn = layer_norm_tile(k, x_sb[:, b, :], xn[:], stats, mvst)
                prev[b] = vn
                k.wt("sync", "cV", vn)
                last_st = k.dma("sync", out=hscr[tc * 128:(tc + 1) * 128, 0:DM], in_=xn[:])
            k.wtd("sync", last_st)
            for c in range(8):
                k.dma("sync", out=xnT[:, c, :], in_=hscr[:, c * 128:(c + 1) * 128], transpose=True)
            k.fence(engines=("tensor",))

        with contextlib.ExitStack() as s2:
            k.fence()
            wqk = s2.enter_context(nc.sbuf_tensor("r_wqk", [128, 2, 8, 256], BF16))
            wv = s2.enter_context(nc.sbuf_tensor("r_wv", [128, 2, 8, 256], BF16))
            tblh = s2.enter_context(nc.sbuf_tensor("r_tbl", [128, nt, 4, 64], F32))
            psq = {tb: s2.enter_context(nc.psum_tensor(f"r_psq{tb}", [128, 256], F32))
                   for tb in range(2)}
            psv = {tb: s2.enter_context(nc.psum_tensor(f"r_psv{tb}", [128, 256], F32))
                   for tb in range(2)}
            qkf = s2.enter_context(nc.sbuf_tensor("r_qkf", [128, 2, 256], F32))
            qkb = s2.enter_context(nc.sbuf_tensor("r_qkb", [128, 2, 256], BF16))
            vbf = s2.enter_context(nc.sbuf_tensor("r_vbf", [128, 2, 256], BF16))
            scrE = s2.enter_context(nc.sbuf_tensor("r_scrE", [128, 128], F32))
            scrO = s2.enter_context(nc.sbuf_tensor("r_scrO", [128, 128], F32))
            scrE2 = s2.enter_context(nc.sbuf_tensor("r_scrE2", [128, 128], F32))
            scrO2 = s2.enter_context(nc.sbuf_tensor("r_scrO2", [128, 128], F32))
            wuse = {0: 0, 1: 0}
            st_qk = {0: None, 1: None}
            st_qk2 = {0: None, 1: None}
            st_v = {0: None, 1: None}
            qk_free = {0: 0, 1: 0}
            psq_free = {0: 0, 1: 0}
            psv_free = {0: 0, 1: 0}
            tbl_rd = 0
            tsrc = {0: "t_l1_cosq", 1: "t_l1_cosk", 2: "t_l1_sinq", 3: "t_l1_sink"}
            for h in range(H):
                b = h % 2
                k.wt("gpsimd", "cT", wuse[b])
                t1 = k.dma("gpsimd", out=wqk[:, b, :, 0:128],
                           in_=D["L1_attn_wq"].rearrange("(c p) n -> p c n", p=128)[:, :, h * 128:(h + 1) * 128])
                t2 = k.dma("gpsimd", out=wqk[:, b, :, 128:256],
                           in_=D["L1_attn_wk"].rearrange("(c p) n -> p c n", p=128)[:, :, h * 128:(h + 1) * 128])
                t3 = k.dma("gpsimd", out=wv[:, b],
                           in_=D["L1_attn_wv"].rearrange("(c p) n -> p c n", p=128)[:, :, h * 256:(h + 1) * 256])
                k.wt("sync", "cV", tbl_rd)
                tt = [k.dma("sync", out=tblh[:, :, j, :],
                            in_=D[tsrc[j]][h].rearrange("(t p) g -> p t g", p=128))
                      for j in range(4)]
                k.wtd("tensor", [t1, t2, t3])
                k.wtd("vector", tt)
                for tc in range(nt):
                    tb = tc % 2
                    r0, r1_ = tc * 128, (tc + 1) * 128
                    k.wt("tensor", "cV", psq_free[tb])
                    vm = None
                    for c in range(8):
                        vm = k.op("tensor", "matmul", out=psq[tb][:],
                                  lhsT=xnT[:, c, r0:r1_], rhs=wqk[:, b, c, :],
                                  start=(c == 0), stop=(c == 7))
                    k.wt("tensor", "cS", psv_free[tb])
                    vm2 = None
                    for c in range(8):
                        vm2 = k.op("tensor", "matmul", out=psv[tb][:],
                                   lhsT=xnT[:, c, r0:r1_], rhs=wv[:, b, c, :],
                                   start=(c == 0), stop=(c == 7))
                    wuse[b] = vm2
                    k.wt("vector", "cT", vm)
                    k.wt("vector", "cV", qk_free[tb])
                    vcp = k.op("vector", "tensor_copy", out=qkf[:, tb, :], in_=psq[tb][:])
                    psq_free[tb] = vcp
                    vr = rope_pairs(k, qkf[:, tb, :].rearrange("p (h2 g two) -> p h2 g two",
                                                               h2=2, two=2),
                                    tblh[:, tc, 0:2, :], tblh[:, tc, 2:4, :],
                                    scrE, scrO, scrE2, scrO2)
                    tbl_rd = vr
                    k.wtd("vector", st_qk[tb])
                    k.wtd("vector", st_qk2[tb])
                    vqk = k.op("vector", "tensor_copy", out=qkb[:, tb, :], in_=qkf[:, tb, :])
                    qk_free[tb] = vqk
                    k.wt("sync", "cV", vqk)
                    st_qk[tb] = k.dma("sync", out=hscr[r0:r1_, DM + h * 128:DM + (h + 1) * 128],
                                      in_=qkb[:, tb, 0:128])
                    st_qk2[tb] = k.dma("sync", out=hscr2[r0:r1_, h * 128:(h + 1) * 128],
                                       in_=qkb[:, tb, 128:256])
                    k.wt("scalar", "cT", vm2)
                    k.wtd("scalar", st_v[tb])
                    vv = k.op("scalar", "copy", out=vbf[:, tb, :], in_=psv[tb][:],
                              selfwait=False)
                    psv_free[tb] = vv
                    k.wt("sync", "cS", vv)
                    st_v[tb] = k.dma("sync", out=vb[r0:r1_, h * 256:(h + 1) * 256], in_=vbf[:, tb, :])
            k.wtd("sync", [x for d in (st_qk, st_qk2, st_v) for x in d.values() if x])

        with contextlib.ExitStack() as s3:
            k.fence()
            wg_sb = s3.enter_context(nc.sbuf_tensor("r_wg", [128, 2, 8, 256], BF16))
            wo_sb = s3.enter_context(nc.sbuf_tensor("r_wo", [128, 2, 2, DM], BF16))
            kTp = s3.enter_context(nc.sbuf_tensor("r_kT", [128, 2, seq], BF16))
            qTp = s3.enter_context(nc.sbuf_tensor("r_qT", [128, 2, 2, 512], BF16))
            v_sb = s3.enter_context(nc.sbuf_tensor("r_v", [128, 2, nt, 256], BF16))
            ssb = s3.enter_context(nc.sbuf_tensor("r_ssb", [128, 2, 512], BF16))
            osb = s3.enter_context(nc.sbuf_tensor("r_osb", [128, 4, DM], BF16))
            dsb = s3.enter_context(nc.sbuf_tensor("r_dsb", [128, 4, 4], F32))
            gsb = s3.enter_context(nc.sbuf_tensor("r_gsb", [128, 2, 8, 256], BF16))
            densb = s3.enter_context(nc.sbuf_tensor("r_den", [128, 8], F32))
            dtmp = s3.enter_context(nc.sbuf_tensor("r_dtmp", [128, 8, 4], F32))
            sqs = s3.enter_context(nc.sbuf_tensor("r_sqs", [128, 1024], F32))
            ysub = s3.enter_context(nc.sbuf_tensor("r_ysub", [128, 256], BF16))
            yT = s3.enter_context(nc.sbuf_tensor("r_yT", [128, 256], BF16))
            acc = s3.enter_context(nc.sbuf_tensor("r_acc", [128, 8, DM], F32))
            xr = s3.enter_context(nc.sbuf_tensor("r_xr", [128, 2, DM], F32))
            st2 = s3.enter_context(nc.psum_tensor("r_st2", [128, 1024], F32))
            denp = s3.enter_context(nc.psum_tensor("r_denp", [128, 512], F32))
            outp = s3.enter_context(nc.psum_tensor("r_outp", [128, 1024], F32))
            ptp = s3.enter_context(nc.psum_tensor("r_ptp", [128, 256], BF16))
            wop = s3.enter_context(nc.psum_tensor("r_wop", [128, 1024], F32))

            items = [(pp, h) for pp in range(nq // 2) for h in range(H)]
            NI = len(items)
            ld = {0: None, 1: None}
            free_w = {0: 0, 1: 0}
            free_kq = {0: 0, 1: 0}
            free_v = {0: 0, 1: 0}
            slot_free = [0, 0, 0, 0]
            wgr = D["L1_attn_wg"].rearrange("(c p) n -> p c n", p=128)

            def emit_loads(i):
                pp, h = items[i]
                b = i % 2
                ntk_hi = 8 * pp + 8
                k.wt("gpsimd", "cT", free_w[b])
                t_wg = k.dma("gpsimd", out=wg_sb[:, b], in_=wgr[:, :, h * 256:(h + 1) * 256])
                t_wo = k.dma("gpsimd", out=wo_sb[:, b],
                             in_=D["L1_attn_wo"][h * 256:(h + 1) * 256].rearrange("(c p) n -> p c n", p=128))
                k.wt("sync", "cT", max(free_kq[b], free_v[b]))
                t_k = k.dma("sync", out=kTp[:, b, 0:ntk_hi * 128],
                            in_=hscr2[0:ntk_hi * 128, h * 128:(h + 1) * 128], transpose=True)
                qts = [t_k]
                for qi in range(2):
                    qc = 2 * pp + qi
                    qts.append(k.dma("sync", out=qTp[:, b, qi],
                                     in_=hscr[qc * 512:(qc + 1) * 512, DM + h * 128:DM + (h + 1) * 128],
                                     transpose=True))
                t_v = k.dma("sync", out=v_sb[:, b, 0:ntk_hi, :],
                            in_=vb[0:ntk_hi * 128, h * 256:(h + 1) * 256].rearrange(
                                "(t p) d -> p t d", p=128))
                ld[b] = {"w": [t_wg, t_wo], "kq": qts, "v": t_v}

            gsb_free = {0: 0, 1: 0}
            ev_done = {0: ("cS", 0), 1: ("cS", 0)}
            pv_done = {0: 0, 1: 0}
            outp_free = 0
            den_free = {0: 0, 1: 0}
            pcount = 0
            st_x = {0: None, 1: None}

            def emit_A(i):
                nonlocal outp_free, pcount
                pp, h = items[i]
                b = i % 2
                k.wtd("tensor", ld[b]["kq"])
                k.wtd("tensor", ld[b]["v"])
                k.wtd("tensor", ld[b]["w"])
                gp_units = [(qi, s) for qi in range(2) for s in range(4)]
                gpi = [0]

                def emit_gp():
                    nonlocal pcount
                    if gpi[0] >= len(gp_units):
                        return
                    qi, s = gp_units[gpi[0]]
                    u = qi * 4 + s
                    gpi[0] += 1
                    qc = 2 * pp + qi
                    p = pcount % 2
                    pcount += 1
                    sem, val = ev_done[p]
                    if val:
                        k.wt("tensor", sem, val)
                    vgp = None
                    for c in range(8):
                        vgp = k.op("tensor", "matmul", out=st2[:, p * 512:p * 512 + 256],
                                   lhsT=xnT[:, c, qc * 512 + s * 128:qc * 512 + (s + 1) * 128],
                                   rhs=wg_sb[:, b, c, :], start=(c == 0), stop=(c == 7),
                                   skip_group_check=True)
                    k.wt("scalar", "cT", vgp)
                    k.wt("scalar", "cV", gsb_free[b])
                    vgs = k.op("scalar", "activation", out=gsb[:, b, u, :],
                               in_=st2[:, p * 512:p * 512 + 256], func=AF.Silu,
                               selfwait=False)
                    ev_done[p] = ("cS", vgs)

                for qi in range(2):
                    qc = 2 * pp + qi
                    slot = b * 2 + qi
                    keep = [kt for kt in range(4 * qc + 4) if bsc[h, qc, kt] >= 1e-25]
                    first_den = [True]
                    fbank = {0: True, 1: True}
                    vpv = [0]
                    pend = None

                    def emit_pvden(pd, qi=qi):
                        p2, kt2, gsem2, vev2 = pd
                        k.wt("tensor", gsem2, vev2)
                        if fbank[0]:
                            k.wt("tensor", "cS", outp_free)
                            k.wt("tensor", "cS", den_free[qi])
                        for s in range(4):
                            k.op("tensor", "matmul", out=denp[:, qi * 4 + s:qi * 4 + s + 1],
                                 lhsT=ssb[:, p2, s * 128:(s + 1) * 128], rhs=ones_col[:],
                                 start=first_den[0], stop=False, skip_group_check=True)
                            first_den[0] = False
                            vpv[0] = k.op("tensor", "matmul", out=outp[:, s * 256:(s + 1) * 256],
                                          lhsT=ssb[:, p2, s * 128:(s + 1) * 128],
                                          rhs=v_sb[:, b, kt2, :],
                                          start=fbank[s // 2], stop=False, skip_group_check=True)
                            fbank[s // 2] = False
                        pv_done[p2] = vpv[0]

                    for bi, kt in enumerate(keep):
                        p = pcount % 2
                        pcount += 1
                        sl = st2[:, p * 512:(p + 1) * 512]
                        sem, val = ev_done[p]
                        if val:
                            k.wt("tensor", sem, val)
                        vmm = k.op("tensor", "matmul", out=sl,
                                   lhsT=kTp[:, b, kt * 128:(kt + 1) * 128],
                                   rhs=qTp[:, b, qi], start=True, stop=True)
                        scale = float(bsc[h, qc, kt])
                        e = kt - 4 * qc
                        if e >= 0:
                            k.wt("vector", "cT", vmm)
                            k.wt("vector", "cT", pv_done[p])
                            vev = k.op("vector", "scalar_tensor_tensor", out=ssb[:, p, :], in0=sl,
                                       scalar=scale, in1=trmask[:, e, :], op0=ALU.mult, op1=ALU.mult,
                                       selfwait=False)
                            ev_done[p] = ("cV", vev)
                            gsem = "cV"
                        else:
                            k.wt("scalar", "cT", vmm)
                            k.wt("scalar", "cT", pv_done[p])
                            vev = k.op("scalar", "activation", out=ssb[:, p, :], in_=sl,
                                       func=AF.Copy, scale=scale, selfwait=False)
                            ev_done[p] = ("cS", vev)
                            gsem = "cS"
                        if pend is not None:
                            emit_pvden(pend)
                        pend = (p, kt, gsem, vev)
                        if bi % 3 == 2:
                            emit_pvden(pend)
                            pend = None
                            emit_gp()
                    if pend is not None:
                        emit_pvden(pend)
                    while gpi[0] < (qi + 1) * 4:
                        emit_gp()
                    k.wt("scalar", "cT", vpv[0])
                    k.wt("scalar", "cV", slot_free[slot])
                    k.op("scalar", "activation", out=dsb[:, slot, :], in_=denp[:, qi * 4:qi * 4 + 4],
                         func=AF.Abs)
                    den_free[qi] = k.sems["cS"].n
                    outp_free = k.op("scalar", "copy", out=osb[:, slot, :], in_=outp[:])
                free_kq[b] = k.sems["cT"].n
                free_v[b] = k.sems["cT"].n
                return outp_free

            acc_rd = 0

            def emit_B(j, ready_cs):
                pp, h = items[j]
                bj = j % 2
                vwo_last = 0
                for qi in range(2):
                    qc = 2 * pp + qi
                    slot = bj * 2 + qi
                    k.wt("vector", "cS", ready_cs)
                    vcl = k.op("vector", "tensor_scalar", out=densb[:, qi * 4:qi * 4 + 4],
                               in0=dsb[:, slot, :], scalar1=1.0, scalar2=None, op0=ALU.max)
                    vsq = k.op("scalar", "activation", out=sqs[:], in_=osb[:, slot, :],
                               func=AF.Square)
                    k.wt("vector", "cS", vsq)
                    k.op("vector", "tensor_reduce", out=dtmp[:, qi, 0:4],
                         in_=sqs[:].rearrange("p (u d) -> p u d", u=4), axis=AX.X, op=ALU.add)
                    k.op("vector", "tensor_scalar", out=dtmp[:, qi, 0:4], in0=dtmp[:, qi, 0:4],
                         scalar1=1.0 / 256, scalar2=None, op0=ALU.mult)
                    k.op("vector", "tensor_tensor", out=dtmp[:, qi + 2, 0:4],
                         in0=densb[:, qi * 4:qi * 4 + 4], in1=densb[:, qi * 4:qi * 4 + 4],
                         op=ALU.mult)
                    vd2 = k.op("vector", "scalar_tensor_tensor", out=dtmp[:, qi + 4, 0:4],
                               in0=dtmp[:, qi + 2, 0:4], scalar=EPS, in1=dtmp[:, qi, 0:4],
                               op0=ALU.mult, op1=ALU.add)
                    k.wt("scalar", "cV", vd2)
                    vln = k.op("scalar", "activation", out=dtmp[:, qi + 6, 0:4],
                               in_=dtmp[:, qi + 4, 0:4], func=AF.Ln)
                    k.wt("scalar", "cS", vln)
                    vrs = k.op("scalar", "activation", out=dtmp[:, qi + 6, 0:4],
                               in_=dtmp[:, qi + 6, 0:4], func=AF.Exp, scale=-0.5)
                    k.wt("vector", "cS", vrs)
                    vy_slot = 0
                    for s in range(4):
                        u = qi * 4 + s
                        t = qi * 4 + s
                        vy = k.op("vector", "scalar_tensor_tensor", out=ysub[:],
                                  in0=osb[:, slot, s * 256:(s + 1) * 256],
                                  scalar=dtmp[:, qi + 6, s:s + 1],
                                  in1=gsb[:, bj, u, :], op0=ALU.mult, op1=ALU.mult)
                        vy_slot = vy
                        vevt = None
                        for cc in range(2):
                            k.wt("tensor", "cV", vy if cc == 0 else vevt)
                            vtp = k.op("tensor", "transpose", out=ptp[:, cc * 128:(cc + 1) * 128],
                                       in_=ysub[:, cc * 128:(cc + 1) * 128], identity=ident[:])
                            k.wt("vector", "cT", vtp)
                            vevt = k.op("vector", "tensor_copy", out=yT[:, cc * 128:(cc + 1) * 128],
                                        in_=ptp[:, cc * 128:(cc + 1) * 128])
                        k.wt("tensor", "cV", vevt)
                        if h == 0:
                            k.wt("tensor", "cV", acc_rd)
                        vmo = None
                        for cc in range(2):
                            for n0 in (0, 512):
                                vmo = k.op("tensor", "matmul", out=wop[:, n0:n0 + 512],
                                           lhsT=yT[:, cc * 128:(cc + 1) * 128],
                                           rhs=wo_sb[:, bj, cc, n0:n0 + 512],
                                           start=(cc == 0), stop=(cc == 1))
                        vwo_last = vmo
                        k.wt("vector", "cT", vmo)
                        if h == 0:
                            k.op("vector", "tensor_copy", out=acc[:, t, :], in_=wop[:])
                        else:
                            k.op("vector", "tensor_tensor", out=acc[:, t, :], in0=acc[:, t, :],
                                 in1=wop[:], op=ALU.add)
                    slot_free[slot] = max(vy_slot, vcl)
                    gsb_free[bj] = max(gsb_free[bj], vy_slot)
                free_w[bj] = vwo_last

            def emit_final(pp):
                nonlocal acc_rd
                for t in range(8):
                    fb = t % 2
                    qc = 2 * pp + t // 4
                    s = t % 4
                    k.wtd("sync", st_x[fb])
                    vxl = k.dma("sync", out=xr[:, fb, :],
                                in_=xin[qc * 512 + s * 128:qc * 512 + (s + 1) * 128, :])
                    k.wtd("vector", vxl)
                    k.wtd("vector", st_x[fb])
                    vadd = k.op("vector", "tensor_tensor", out=xr[:, fb, :], in0=acc[:, t, :],
                                in1=xr[:, fb, :], op=ALU.add)
                    acc_rd = vadd
                    k.wt("sync", "cV", vadd)
                    st_x[fb] = k.dma("sync", out=xout[qc * 512 + s * 128:qc * 512 + (s + 1) * 128, :],
                                     in_=xr[:, fb, :])

            emit_loads(0)
            prev_ready = None
            for i in range(NI):
                ready = emit_A(i)
                if i > 0:
                    emit_B(i - 1, prev_ready)
                    if items[i - 1][1] == H - 1:
                        emit_final(items[i - 1][0])
                prev_ready = ready
                if i + 1 < NI:
                    emit_loads(i + 1)
            emit_B(NI - 1, prev_ready)
            emit_final(items[NI - 1][0])
        k.fence()


WSPECS_L1A = [
    ("L1_attn_wq", (DM, DM), F32),
    ("L1_attn_wk", (DM, DM), F32),
    ("L1_attn_wv", (DM, 2048), F32),
    ("L1_attn_wg", (DM, 2048), F32),
    ("L1_attn_wo", (2048, DM), F32),
]


def tspecs_l1a(seq):
    return [
        ("t_ident", (128, 128), BF16),
        ("t_trmask", (4, 128, 512), BF16),
        ("t_l1_cosq", (8, seq, 64), F32),
        ("t_l1_sinq", (8, seq, 64), F32),
        ("t_l1_cosk", (8, seq, 64), F32),
        ("t_l1_sink", (8, seq, 64), F32),
    ]


def build_l1a(seq):
    nc = bass.Bass("TRN2", target_bir_lowering=False)
    es = contextlib.ExitStack()
    with es:
        k = K(nc, es, seq)
        D = k.dram
        D["xin"] = nc.dram_tensor("xin", [seq, DM], F32, kind="ExternalInput").ap()
        for nm, shp, dt in WSPECS_L1A + tspecs_l1a(seq):
            D[nm] = nc.dram_tensor(nm, list(shp), dt, kind="ExternalInput").ap()
        D["hscr"] = nc.dram_tensor("hscr", [seq, 2048], BF16, kind="Internal").ap()
        D["hscr2"] = nc.dram_tensor("hscr2", [seq, 1024], BF16, kind="Internal").ap()
        D["vb"] = nc.dram_tensor("vb", [seq, 2048], BF16, kind="Internal").ap()
        D["xout"] = nc.dram_tensor("xout", [seq, DM], F32, kind="ExternalOutput").ap()
        emit_l1a(k, D["xin"], D["xout"])
        k.build()
    return nc



def emit_l1f(k, xin, xout):
    nc = k.nc
    seq, nt, nq = k.seq, k.nt, k.nq
    D = k.dram
    NL = 16
    hscr, qTb, vb = D["hscr"], D["qTb"], D["vb"]

    with contextlib.ExitStack() as ph:
        k.fence()
        ident = ph.enter_context(nc.sbuf_tensor("e_ident", [128, 128], BF16))
        k.dma("sync", out=ident[:], in_=D["t_ident"][:, :])
        stats = ph.enter_context(nc.sbuf_tensor("e_stats", [128, 12], F32))
        mvst = ph.enter_context(nc.sbuf_tensor("e_mv", [128, 4], F32))
        kT = ph.enter_context(nc.sbuf_tensor("e_kT", [128, 8, seq], BF16))

        with contextlib.ExitStack() as sff:
            xnT = sff.enter_context(nc.sbuf_tensor("e_xnT", [128, 8, seq], BF16))

            with contextlib.ExitStack() as s1:
                k.fence()
                x_sb = s1.enter_context(nc.sbuf_tensor("e_x", [128, 2, DM], F32))
                xn = s1.enter_context(nc.sbuf_tensor("e_xn", [128, DM], BF16))
                last_st = None
                prev = {0: 0, 1: 0}
                for tc in range(nt):
                    b = tc % 2
                    k.wt("sync", "cV", prev[b])
                    vl = k.dma("sync", out=x_sb[:, b, :], in_=xin[tc * 128:(tc + 1) * 128, :])
                    k.wtd("vector", vl)
                    k.wtd("vector", last_st)
                    vn = layer_norm_tile(k, x_sb[:, b, :], xn[:], stats, mvst)
                    prev[b] = vn
                    k.wt("sync", "cV", vn)
                    last_st = k.dma("sync", out=hscr[tc * 128:(tc + 1) * 128, 0:DM], in_=xn[:])
                k.wtd("sync", last_st)
                vxt = []
                for c in range(8):
                    vxt.append(k.dma("sync", out=xnT[:, c, :], in_=hscr[:, c * 128:(c + 1) * 128],
                                     transpose=True))
                k.wtd("tensor", vxt)

            for pref in ("q", "v", "k"):
                with contextlib.ExitStack() as s2:
                    k.fence()
                    nw = s2.enter_context(nc.sbuf_tensor("e_nw_" + pref, [15, DM], BF16))
                    nwT = s2.enter_context(nc.sbuf_tensor("e_nwT_" + pref, [128, 8, 16], BF16))
                    w1 = s2.enter_context(nc.sbuf_tensor("e_w1_" + pref, [128, 8, NL, 64], BF16))
                    w2 = s2.enter_context(nc.sbuf_tensor("e_w2_" + pref, [128, 8, DM], BF16))
                    b2 = s2.enter_context(nc.sbuf_tensor("e_b2_" + pref, [16, DM], BF16))
                    selT = s2.enter_context(nc.sbuf_tensor("e_selT_" + pref, [16, seq], BF16))
                    nbb, _ = bcast_row(k, "e_nb_" + pref, D[f"L1_ffn_{pref}_node_b"][None, :], 15, s2)
                    b1b, _ = bcast_row(k, "e_b1_" + pref,
                                       D[f"L1_ffn_{pref}_b1"].rearrange("l h -> (l h)")[None, :], DM, s2)
                    iot, viot = bcast_row(k, "e_iota_" + pref, D["t_iota16"][None, :], 16, s2)
                    v1 = k.dma("gpsimd", out=nw[:], in_=D[f"L1_ffn_{pref}_node_w"][:, :])
                    v2 = []
                    w1r = D[f"L1_ffn_{pref}_w1"].rearrange("l (c p) h -> p c l h", p=128)
                    for c in range(8):
                        v2.append(k.dma("gpsimd", out=w1[:, c], in_=w1r[:, c]))
                    v3 = k.dma("gpsimd", out=w2[:],
                               in_=D[f"L1_ffn_{pref}_w2"].rearrange("l h n -> (l h) n")
                               .rearrange("(c p) n -> p c n", p=128))
                    v4 = k.dma("gpsimd", out=b2[:], in_=D[f"L1_ffn_{pref}_b2"][:, :])
                    psn = {j: s2.enter_context(nc.psum_tensor(f"e_psn{j}_" + pref, [128, 16], F32))
                           for j in range(2)}
                    psh = s2.enter_context(nc.psum_tensor("e_psh_" + pref, [128, 1024], F32))
                    pso = {j: s2.enter_context(nc.psum_tensor(f"e_pso{j}_" + pref, [128, 512], F32))
                           for j in range(2)}
                    ptp = s2.enter_context(nc.psum_tensor("e_ptp_" + pref, [128, 1024], BF16))
                    ptq = s2.enter_context(nc.psum_tensor("e_ptq_" + pref, [16, 128], BF16))
                    sc_sb = s2.enter_context(nc.sbuf_tensor("e_sc_" + pref, [128, 16], F32))
                    tr_sb = s2.enter_context(nc.sbuf_tensor("e_tr_" + pref, [128, 12], F32))
                    sel = s2.enter_context(nc.sbuf_tensor("e_sel_" + pref, [128, 16], F32))
                    selb = s2.enter_context(nc.sbuf_tensor("e_selb_" + pref, [128, 16], BF16))
                    h1 = s2.enter_context(nc.sbuf_tensor("e_h1_" + pref, [128, DM], F32))
                    hb = s2.enter_context(nc.sbuf_tensor("e_hb_" + pref, [128, DM], BF16))
                    hT = s2.enter_context(nc.sbuf_tensor("e_hT_" + pref, [128, 8, 512], BF16))
                    ob = s2.enter_context(nc.sbuf_tensor("e_ob_" + pref, [128, 2, 512], BF16))
                    k.wtd("tensor", [v1, v3, v4] + v2)
                    k.wt("tensor", "cV", viot)
                    vnt = 0
                    for c in range(8):
                        k.wt("tensor", "cV", vnt)
                        vtp = k.op("tensor", "transpose", out=ptp[:, 0:15],
                                   in_=nw[:, c * 128:(c + 1) * 128], identity=ident[0:15, 0:15])
                        k.wt("vector", "cT", vtp)
                        vnt = k.op("vector", "tensor_copy", out=nwT[:, c, 0:15], in_=ptp[:, 0:15])
                    k.wt("tensor", "cV", vnt)

                    st_sel = 0
                    vhb_used = 0
                    vh1_used = 0
                    vpsn_free = {0: 0, 1: 0}
                    vob_st = {0: None, 1: None}
                    pso_free = {0: 0, 1: 0}
                    vsel_used = 0
                    vhT_used = 0
                    for ch in range(seq // 512):
                        for ti in range(4):
                            tc = ch * 4 + ti
                            r0, r1_ = tc * 128, (tc + 1) * 128
                            pslot = tc % 2
                            k.wt("tensor", "cV", vpsn_free[pslot])
                            vm = None
                            for c in range(8):
                                vm = k.op("tensor", "matmul", out=psn[pslot][:, 0:15],
                                          lhsT=xnT[:, c, r0:r1_], rhs=nwT[:, c, 0:15],
                                          start=(c == 0), stop=(c == 7))
                            k.wt("tensor", "cV", vh1_used)
                            vm2 = None
                            for half in range(2):
                                for c in range(8):
                                    vm2 = k.op("tensor", "matmul", out=psh[:, half * 512:(half + 1) * 512],
                                               lhsT=xnT[:, c, r0:r1_],
                                               rhs=w1[:, c, half * 8:(half + 1) * 8, :].rearrange(
                                                   "p l h -> p (l h)"),
                                               start=(c == 0), stop=(c == 7))
                            k.wt("vector", "cT", vm)
                            vpsn_free[pslot] = k.op("vector", "tensor_tensor", out=sc_sb[:, 0:15],
                                                    in0=psn[pslot][:, 0:15],
                                                    in1=nbb[:, 0:15], op=ALU.add)
                            k.wt("vector", "cT", vm2)
                            vh1 = k.op("vector", "tensor_tensor", out=h1[:], in0=psh[:], in1=b1b[:],
                                       op=ALU.add)
                            vh1_used = vh1
                            k.op("vector", "tensor_scalar", out=tr_sb[:, 0:1], in0=sc_sb[:, 0:1],
                                 scalar1=0.0, scalar2=None, op0=ALU.is_ge)
                            k.op("vector", "tensor_tensor", out=tr_sb[:, 7:8], in0=sc_sb[:, 2:3],
                                 in1=sc_sb[:, 1:2], op=ALU.subtract)
                            k.op("vector", "tensor_tensor", out=tr_sb[:, 8:9], in0=tr_sb[:, 0:1],
                                 in1=tr_sb[:, 7:8], op=ALU.mult)
                            k.op("vector", "tensor_tensor", out=tr_sb[:, 9:10], in0=sc_sb[:, 1:2],
                                 in1=tr_sb[:, 8:9], op=ALU.add)
                            k.op("vector", "tensor_scalar", out=tr_sb[:, 1:2], in0=tr_sb[:, 9:10],
                                 scalar1=0.0, scalar2=None, op0=ALU.is_ge)
                            k.op("vector", "scalar_tensor_tensor", out=tr_sb[:, 4:5], in0=tr_sb[:, 0:1],
                                 scalar=2.0, in1=tr_sb[:, 1:2], op0=ALU.mult, op1=ALU.add)
                            bc4 = tr_sb[:, 4:5].broadcast_to([128, 4])
                            k.op("vector", "tensor_tensor", out=tr_sb[:, 7:11], in0=bc4,
                                 in1=iot[:, 0:4], op=ALU.is_equal)
                            k.op("vector", "tensor_tensor", out=tr_sb[:, 7:11],
                                 in0=tr_sb[:, 7:11], in1=sc_sb[:, 3:7], op=ALU.mult)
                            k.op("vector", "tensor_reduce", out=tr_sb[:, 9:10],
                                 in_=tr_sb[:, 7:11], axis=AX.X, op=ALU.add)
                            k.op("vector", "tensor_scalar", out=tr_sb[:, 2:3], in0=tr_sb[:, 9:10],
                                 scalar1=0.0, scalar2=None, op0=ALU.is_ge)
                            k.op("vector", "scalar_tensor_tensor", out=tr_sb[:, 5:6], in0=tr_sb[:, 4:5],
                                 scalar=2.0, in1=tr_sb[:, 2:3], op0=ALU.mult, op1=ALU.add)
                            k.wt("vector", "cT", vsel_used)
                            bc8 = tr_sb[:, 5:6].broadcast_to([128, 8])
                            k.op("vector", "tensor_tensor", out=sel[:, 0:8], in0=bc8,
                                 in1=iot[:, 0:8], op=ALU.is_equal)
                            k.op("vector", "tensor_tensor", out=sel[:, 8:16],
                                 in0=sel[:, 0:8], in1=sc_sb[:, 7:15], op=ALU.mult)
                            k.op("vector", "tensor_reduce", out=tr_sb[:, 9:10],
                                 in_=sel[:, 8:16], axis=AX.X, op=ALU.add)
                            k.op("vector", "tensor_scalar", out=tr_sb[:, 3:4], in0=tr_sb[:, 9:10],
                                 scalar1=0.0, scalar2=None, op0=ALU.is_ge)
                            k.op("vector", "scalar_tensor_tensor", out=tr_sb[:, 6:7], in0=tr_sb[:, 5:6],
                                 scalar=2.0, in1=tr_sb[:, 3:4], op0=ALU.mult, op1=ALU.add)
                            bc16 = tr_sb[:, 6:7].broadcast_to([128, 16])
                            vsel = k.op("vector", "tensor_tensor", out=sel[:], in0=bc16,
                                        in1=iot[:], op=ALU.is_equal)
                            vselb = k.op("vector", "tensor_copy", out=selb[:], in_=sel[:])
                            k.wt("tensor", "cV", vselb)
                            vtq = k.op("tensor", "transpose", out=ptq[:],
                                       in_=selb[:], identity=ident[:, :])
                            vsel_used = vtq
                            k.wt("vector", "cT", vtq)
                            k.op("vector", "tensor_copy", out=selT[:, r0:r1_], in_=ptq[:])
                            k.wt("vector", "cT", vhb_used)
                            vhb = None
                            for l in range(NL):
                                vhb = k.op("vector", "tensor_scalar", out=hb[:, l * 64:(l + 1) * 64],
                                           in0=h1[:, l * 64:(l + 1) * 64], scalar1=sel[:, l:l + 1],
                                           scalar2=0.0, op0=ALU.mult, op1=ALU.max,
                                           selfwait=(l == NL - 1))
                            vev = None
                            for c in range(8):
                                k.wt("tensor", "cV", vhb if c == 0 else vev)
                                k.wt("tensor", "cT", vhT_used) if c == 0 else None
                                vtp = k.op("tensor", "transpose",
                                           out=ptp[:, c * 128:(c + 1) * 128],
                                           in_=hb[:, c * 128:(c + 1) * 128], identity=ident[:])
                                vhb_used = vtp
                                k.wt("vector", "cT", vtp)
                                vev = k.op("vector", "tensor_copy",
                                           out=hT[:, c, ti * 128:(ti + 1) * 128],
                                           in_=ptp[:, c * 128:(c + 1) * 128])
                            k.wt("tensor", "cV", vev)
                        ch0 = ch * 512
                        if pref == "v":
                            for ti in range(4):
                                r0 = ch0 + ti * 128
                                for n0 in (0, 512):
                                    oslot = (ti * 2 + n0 // 512) % 2
                                    k.wt("tensor", "cV", pso_free[oslot])
                                    vmo = None
                                    for c in range(8):
                                        vmo = k.op("tensor", "matmul", out=pso[oslot][:],
                                                   lhsT=hT[:, c, ti * 128:(ti + 1) * 128],
                                                   rhs=w2[:, c, n0:n0 + 512],
                                                   start=(c == 0), stop=False)
                                    vmo = k.op("tensor", "matmul", out=pso[oslot][:],
                                               lhsT=selT[:, r0:r0 + 128],
                                               rhs=b2[:, n0:n0 + 512], start=False, stop=True)
                                    vhT_used = vmo
                                    k.wt("vector", "cT", vmo)
                                    k.wtd("vector", vob_st[oslot])
                                    vo = k.op("vector", "tensor_copy", out=ob[:, oslot, :],
                                              in_=pso[oslot][:], selfwait=False)
                                    pso_free[oslot] = vo
                                    k.wt("sync", "cV", vo)
                                    vob_st[oslot] = k.dma("sync", out=vb[r0:r0 + 128, n0:n0 + 512],
                                                          in_=ob[:, oslot, :])
                        else:
                            for oc in range(8):
                                oslot = oc % 2
                                k.wt("tensor", "cV", pso_free[oslot])
                                vmo = None
                                for c in range(8):
                                    vmo = k.op("tensor", "matmul", out=pso[oslot][:],
                                               lhsT=w2[:, c, oc * 128:(oc + 1) * 128],
                                               rhs=hT[:, c, :], start=(c == 0), stop=False)
                                vmo = k.op("tensor", "matmul", out=pso[oslot][:],
                                           lhsT=b2[:, oc * 128:(oc + 1) * 128],
                                           rhs=selT[:, ch0:ch0 + 512], start=False, stop=True)
                                vhT_used = vmo
                                k.wt("vector", "cT", vmo)
                                k.wtd("vector", vob_st[oslot])
                                if pref == "q":
                                    vo = k.op("vector", "tensor_scalar", out=ob[:, oslot, :],
                                              in0=pso[oslot][:], scalar1=1.0 / 32.0, scalar2=None,
                                              op0=ALU.mult, selfwait=False)
                                    pso_free[oslot] = vo
                                    k.wt("sync", "cV", vo)
                                    vob_st[oslot] = k.dma("sync", out=qTb[:, oc, ch0:ch0 + 512],
                                                          in_=ob[:, oslot, :])
                                else:
                                    vo = k.op("vector", "tensor_copy", out=kT[:, oc, ch0:ch0 + 512],
                                              in_=pso[oslot][:], selfwait=False)
                                    pso_free[oslot] = vo
                                    vob_st[oslot] = None
                    k.fence()

        with contextlib.ExitStack() as s3:
            k.fence()
            v_sb = s3.enter_context(nc.sbuf_tensor("e_v", [128, nt, 1032], BF16))
            vzero = k.op("vector", "memset", ap=v_sb[:, :, 1024:1032], constant=0.0)
            k.op("vector", "memset", ap=v_sb[:, :, 1024:1025], constant=1.0)
            vvl = k.dma("sync", out=v_sb[:, :, 0:1024], in_=vb.rearrange("(t p) n -> p t n", p=128))
            qTc = s3.enter_context(nc.sbuf_tensor("e_qTc", [128, 2, 8, 256], BF16))
            pts = s3.enter_context(nc.sbuf_tensor("e_pts", [128, 3, 256], BF16))
            mbias = s3.enter_context(nc.sbuf_tensor("e_mbias", [128, 1], F32))
            vmb = k.op("vector", "memset", ap=mbias[:], constant=-12.0)
            rv = s3.enter_context(nc.sbuf_tensor("e_rv", [128, 2], F32))
            xr = s3.enter_context(nc.sbuf_tensor("e_xr", [128, 2, DM], F32))
            xnew = s3.enter_context(nc.sbuf_tensor("e_xnew", [128, 2, DM], F32))
            sc = {j: s3.enter_context(nc.psum_tensor(f"e_sc{j}", [128, 256], F32))
                  for j in range(2)}
            y = {j: s3.enter_context(nc.psum_tensor(f"e_y{j}", [128, 1032], F32))
                 for j in range(2)}
            k.wtd("tensor", vvl)
            k.wt("scalar", "cV", vmb)
            k.wt("tensor", "cV", vzero)
            scfree = [None, None]
            ptsfree = [None, None, None]
            ptsi = [0]
            yfree = [None, None]
            qld = {0: None, 1: None}
            qfree = {0: 0, 1: 0}
            st_x = {0: None, 1: None}
            xprev = {0: 0, 1: 0}
            NCH = seq // 256
            for cc in range(NCH):
                b = cc % 2
                q0 = cc * 256
                k.wt("sync", "cT", qfree[b])
                qld[b] = k.dma("sync", out=qTc[:, b], in_=qTb[:, :, q0:q0 + 256])
                k.wtd("tensor", qld[b])

                def emit_pv(p):
                    ps2, kt2, gate2 = p
                    k.wt("tensor", *gate2)
                    vpv = None
                    for qs2 in range(2):
                        if kt2 == 0 and yfree[qs2]:
                            k.wt("tensor", *yfree[qs2])
                        for n0 in (0, 512, 1024):
                            n1 = min(n0 + 512, 1032)
                            vpv = k.op("tensor", "matmul", out=y[qs2][:, n0:n1],
                                       lhsT=pts[:, ps2, qs2 * 128:(qs2 + 1) * 128],
                                       rhs=v_sb[:, kt2, n0:n1],
                                       start=(kt2 == 0), stop=(kt2 == nt - 1),
                                       skip_group_check=True)
                    ptsfree[ps2] = ("cT", vpv)
                    return vpv

                pend = None
                for kt in range(nt):
                    ss = kt % 2
                    if scfree[ss]:
                        k.wt("tensor", *scfree[ss])
                    vmm = None
                    for c in range(8):
                        vmm = k.op("tensor", "matmul", out=sc[ss][:],
                                   lhsT=kT[:, c, kt * 128:(kt + 1) * 128],
                                   rhs=qTc[:, b, c, :], start=(c == 0), stop=(c == 7))
                    qfree[b] = vmm
                    ps = ptsi[0] % 3
                    ptsi[0] += 1
                    if ptsfree[ps]:
                        k.wt("scalar", *ptsfree[ps])
                    k.wt("scalar", "cT", vmm)
                    vexp = k.op("scalar", "activation", out=pts[:, ps, :], in_=sc[ss][:],
                                func=AF.Exp, bias=mbias[:], scale=1.0, selfwait=False)
                    scfree[ss] = ("cS", vexp)
                    if pend is not None:
                        emit_pv(pend)
                    pend = (ps, kt, ("cS", vexp))
                last_pv = emit_pv(pend)
                for qs2 in range(2):
                    tok0 = q0 + qs2 * 128
                    xb = qs2
                    k.wt("sync", "cV", xprev[xb])
                    vxl = k.dma("sync", out=xr[:, xb, :], in_=xin[tok0:tok0 + 128, :])
                    k.wt("vector", "cT", last_pv)
                    k.op("vector", "reciprocal", out=rv[:, qs2:qs2 + 1],
                         in_=y[qs2][:, 1024:1025])
                    k.wtd("vector", vxl)
                    k.wtd("vector", st_x[xb])
                    vfin = k.op("vector", "scalar_tensor_tensor", out=xnew[:, xb, :],
                                in0=y[qs2][:, 0:1024], scalar=rv[:, qs2:qs2 + 1],
                                in1=xr[:, xb, :], op0=ALU.mult, op1=ALU.add)
                    yfree[qs2] = ("cV", vfin)
                    xprev[xb] = vfin
                    k.wt("sync", "cV", vfin)
                    st_x[xb] = k.dma("sync", out=xout[tok0:tok0 + 128, :], in_=xnew[:, xb, :])
        k.fence()


WSPECS_L1F = []
for _p in ("q", "k", "v"):
    WSPECS_L1F += [
        (f"L1_ffn_{_p}_node_w", (15, DM), F32),
        (f"L1_ffn_{_p}_node_b", (15,), F32),
        (f"L1_ffn_{_p}_w1", (16, DM, 64), F32),
        (f"L1_ffn_{_p}_b1", (16, 64), F32),
        (f"L1_ffn_{_p}_w2", (16, 64, DM), F32),
        (f"L1_ffn_{_p}_b2", (16, DM), F32),
    ]


def tspecs_l1f(seq):
    return [
        ("t_ident", (128, 128), BF16),
        ("t_iota16", (16,), F32),
    ]


def build_l1f(seq):
    nc = bass.Bass("TRN2", target_bir_lowering=False)
    es = contextlib.ExitStack()
    with es:
        k = K(nc, es, seq)
        D = k.dram
        D["xin"] = nc.dram_tensor("xin", [seq, DM], F32, kind="ExternalInput").ap()
        for nm, shp, dt in WSPECS_L1F + tspecs_l1f(seq):
            D[nm] = nc.dram_tensor(nm, list(shp), dt, kind="ExternalInput").ap()
        D["hscr"] = nc.dram_tensor("hscr", [seq, 2048], BF16, kind="Internal").ap()
        D["qTb"] = nc.dram_tensor("qTb", [128, 8, seq], BF16, kind="Internal").ap()
        D["vb"] = nc.dram_tensor("vb", [seq, DM], BF16, kind="Internal").ap()
        D["xout"] = nc.dram_tensor("xout", [seq, DM], F32, kind="ExternalOutput").ap()
        emit_l1f(k, D["xin"], D["xout"])
        k.build()
    return nc


def all_tables(seq):
    t = host_tables(seq)
    tab = {
        "t_ident": t["ident"],
        "t_trmask": t["trmask"],
        "t_pvlr": t["pvlr"],
        "t_pvq4": t["pvq4"],
        "t_iota16": np.arange(16, dtype=np.float32),
    }
    qq2 = np.arange(128)[:, None]
    kk2 = np.arange(512)[None, :]
    tab["t_mlamask"] = np.stack([np.where(kk2 <= qq2 + 128 * e, 0.0, -30000.0)
                                 for e in range(4)]).astype(ml_dtypes.bfloat16)
    freqs = (1.0 / 100000.0 ** (np.arange(0, 32, 2, dtype=np.float64) / 32))
    ang = np.outer(np.arange(seq, dtype=np.float64), freqs)
    tab["t_mla_cos"] = np.cos(ang).astype(np.float32)
    tab["t_mla_sin"] = np.sin(ang).astype(np.float32)
    tl1, _ = l1a_host_tables(seq)
    tab.update(tl1)
    return tab


ALL_WSPECS = WSPECS_L0A + WSPECS_L0F + WSPECS_L1A + WSPECS_L1F


def all_tspecs(seq):
    seen = {}
    for nm, shp, dt in (list(TSPECS) + tspecs_l0f(seq) + tspecs_l1a(seq)
                        + tspecs_l1f(seq)):
        seen[nm] = (shp, dt)
    return [(nm, shp, dt) for nm, (shp, dt) in seen.items()]


def build_fused(seq):
    nc = bass.Bass("TRN2", target_bir_lowering=False)
    es = contextlib.ExitStack()
    with es:
        k = K(nc, es, seq)
        D = k.dram
        D["xin"] = nc.dram_tensor("xin", [seq, DM], F32, kind="ExternalInput").ap()
        for nm, shp, dt in ALL_WSPECS + all_tspecs(seq):
            D[nm] = nc.dram_tensor(nm, list(shp), dt, kind="ExternalInput").ap()
        D["hscr"] = nc.dram_tensor("hscr", [seq, 2048], BF16, kind="Internal").ap()
        D["zb"] = nc.dram_tensor("zb", [seq, DM], F32, kind="Internal").ap()
        D["r1b"] = nc.dram_tensor("r1b", [seq, DM], F32, kind="Internal").ap()
        D["macc"] = nc.dram_tensor("macc", [seq, DM], F32, kind="Internal").ap()
        D["hscr2"] = nc.dram_tensor("hscr2", [seq, 1024], BF16, kind="Internal").ap()
        D["vb"] = nc.dram_tensor("vb", [seq, 2048], BF16, kind="Internal").ap()
        vb_l1f = nc.dram_tensor("vb2", [seq, DM], BF16, kind="Internal").ap()
        D["qTb"] = nc.dram_tensor("qTb", [128, 8, seq], BF16, kind="Internal").ap()
        x1 = nc.dram_tensor("x1", [seq, DM], F32, kind="Internal").ap()
        x2 = nc.dram_tensor("x2", [seq, DM], F32, kind="Internal").ap()
        x3 = nc.dram_tensor("x3", [seq, DM], F32, kind="Internal").ap()
        D["xout"] = nc.dram_tensor("xout", [seq, DM], F32, kind="ExternalOutput").ap()
        emit_l0a(k, D["xin"], x1)
        emit_l0f(k, x1, x2)
        emit_l1a(k, x2, x3)
        D["vb"] = vb_l1f
        emit_l1f(k, x3, D["xout"])
        k.build()
    return nc


def kernel(**inputs):
    seq = SEQ_DEFAULT
    tab = all_tables(seq)
    x = np.ascontiguousarray(np.asarray(inputs["L0_attn_x"], dtype=np.float32))
    ncores = x.shape[0]
    nc = build_fused(seq)
    shared = {nm: np.ascontiguousarray(np.asarray(inputs[nm], dtype=np.float32))
              for nm, _, _ in ALL_WSPECS}
    for nm, _, _ in all_tspecs(seq):
        shared[nm] = tab[nm]
    in_maps = []
    for i in range(ncores):
        m = {"xin": np.ascontiguousarray(x[i])}
        m.update(shared)
        in_maps.append(m)
    res = run_bass_kernel_spmd(nc, in_maps, core_ids=list(range(ncores)))
    return np.stack([np.asarray(r["xout"], dtype=np.float32) for r in res.results],
                    axis=0)


SEQ_DEFAULT = 4096



# version 10
# speedup vs baseline: 1.0792x; 1.0792x over previous
import contextlib
import numpy as np
import ml_dtypes
import concourse.bass as bass
import concourse.mybir as mybir
from concourse.bass_utils import run_bass_kernel_spmd

F32 = mybir.dt.float32
BF16 = mybir.dt.bfloat16
AF = mybir.ActivationFunctionType
ALU = mybir.AluOpType
AX = mybir.AxisListType

NCORES = 8
DM = 1024

ENGS = ("tensor", "vector", "scalar", "sync", "gpsimd")
ENGSEM = {"tensor": "cT", "vector": "cV", "scalar": "cS", "sync": "cY", "gpsimd": "cG"}
DMASEM = {"sync": "dS", "gpsimd": "dG", "scalar": "dC", "vector": "dV", "tensor": "dT"}


class Sem:
    __slots__ = ("h", "n", "name")

    def __init__(self, h, name):
        self.h = h
        self.n = 0
        self.name = name

    def plan(self, kk):
        self.n += kk
        return self.n


class K:

    def __init__(self, nc, es, seq):
        self.nc = nc
        self.es = es
        self.seq = seq
        self.nt = seq // 128
        self.nq = seq // 512
        self.ops = {e: [] for e in ENGS}
        self.sems = {}
        for name in set(ENGSEM.values()):
            self.sems[name] = Sem(es.enter_context(nc.semaphore(name)), name)
        self.dpool = {}
        self.didx = {}
        for q, base in DMASEM.items():
            npool = 6 if q in ("sync", "gpsimd") else 2
            self.dpool[q] = []
            for j in range(npool):
                nm = f"{base}{j}"
                s = Sem(es.enter_context(nc.semaphore(nm)), nm)
                self.sems[nm] = s
                self.dpool[q].append(s)
            self.didx[q] = 0
        self.dram = {}
        self.out_tr = []
        self.out_sb2sb = []
        self.lastw = {}

    def emit(self, eng, f):
        self.ops[eng].append(f)

    def wt(self, eng, semname, v):
        if not v:
            return
        s = self.sems[semname]
        assert v <= s.n, f"wait on {semname}={v} > planned {s.n}"
        key = (eng, semname)
        if self.lastw.get(key, 0) >= v:
            return
        self.lastw[key] = v
        self.emit(eng, lambda e, s=s, v=v: e.wait_ge(s.h, v))

    _opid = [0]
    _oplog = {}

    def op(self, eng, meth, inc=1, selfwait=None, tag="", **kw):
        import sys as _sys
        s = self.sems[ENGSEM[eng]]
        v = s.plan(inc) if inc else None
        K._opid[0] += 1
        fr = _sys._getframe(1)
        K._oplog[K._opid[0]] = f"{meth} @ {fr.f_code.co_name}:{fr.f_lineno}"
        label = f"{meth}.{tag}.{K._opid[0]}"

        def f(e, meth=meth, kw=kw, s=s, inc=inc, label=label):
            sid, _ = self.nc.enter_named_scope(label, False)
            try:
                r = getattr(e, meth)(**kw)
                if inc:
                    r.then_inc(s.h, inc)
            finally:
                self.nc.leave_named_scope(label, sid, False)

        self.emit(eng, f)
        if selfwait is None:
            selfwait = eng in ("vector", "scalar", "gpsimd")
        if selfwait and inc:
            self.wt(eng, ENGSEM[eng], v)
        return v

    def dma(self, q, out, in_, sem=None, **kw):
        if sem is None:
            pool = self.dpool[q]
            s = pool[self.didx[q] % len(pool)]
            self.didx[q] += 1
        else:
            s = self.sems[sem]
        is_tr = bool(kw.get("transpose"))
        try:
            sp_in, sp_out = str(in_.space), str(out.space)
        except Exception:
            sp_in = sp_out = "?"
        is_sb2sb = (not is_tr) and ("SBUF" in sp_in) and ("SBUF" in sp_out)
        if is_tr and self.out_sb2sb:
            self.wtd(q, self.out_sb2sb)
            self.out_sb2sb = []
        if is_sb2sb and self.out_tr:
            self.wtd(q, self.out_tr)
            self.out_tr = []
        if s.n:
            self.wt(q, s.name, s.n)
        v = s.plan(16)

        def f(e, out=out, in_=in_, kw=kw, s=s):
            e.dma_start(out=out, in_=in_, **kw).then_inc(s.h, 16)

        self.emit(q, f)
        tok = (s.name, v)
        if is_tr:
            self.out_tr.append(tok)
        elif is_sb2sb:
            self.out_sb2sb.append(tok)
        return tok

    def wtd(self, eng, tok):
        if not tok:
            return
        if isinstance(tok, list):
            best = {}
            for t in tok:
                if t:
                    best[t[0]] = max(best.get(t[0], 0), t[1])
            for name, v in best.items():
                self.wt(eng, name, v)
            return
        self.wt(eng, tok[0], tok[1])

    def snap(self):
        return {n: s.n for n, s in self.sems.items()}

    def fence(self, engines=ENGS):
        snap = self.snap()
        for eng in engines:
            own = {ENGSEM[eng], DMASEM.get(eng)}
            for name, v in snap.items():
                if name in own or v == 0:
                    continue
                self.wt(eng, name, v)

    def wait_all(self, eng, include_own_dma=False):
        own = {ENGSEM[eng]}
        if not include_own_dma:
            own.add(DMASEM.get(eng))
        for name, s in self.sems.items():
            if name in own or s.n == 0:
                continue
            self.wt(eng, name, s.n)

    def build(self):
        with self.nc.Block() as blk:
            for eng in ENGS:
                def mk(eng):
                    def body(e):
                        for f in self.ops[eng]:
                            f(e)
                    return body
                getattr(blk, eng)(mk(eng))



def host_tables(seq):
    t = {}
    t["ident"] = np.eye(128, dtype=ml_dtypes.bfloat16)
    kk = np.arange(128)[:, None]
    qq = np.arange(512)[None, :]
    t["trmask"] = np.stack([(kk + 128 * e <= qq) for e in range(4)]).astype(ml_dtypes.bfloat16)
    jj = np.arange(128)[None, :]
    t["pvlr"] = np.broadcast_to(
        np.stack([(jj < 64), (jj >= 64)], axis=0)[None, :, 0, :], (128, 2, 128)
    ).astype(ml_dtypes.bfloat16)
    t["pvq4"] = np.broadcast_to(
        np.stack([(jj[0] // 32 == m) for m in range(4)], axis=0)[None, :, :],
        (128, 4, 128)).astype(ml_dtypes.bfloat16)
    return t



def bcast_row(k, name, src_dram, n, ph):
    nc = k.nc
    tile = ph.enter_context(nc.sbuf_tensor(name, [128, n], F32))
    with contextlib.ExitStack() as s:
        row = s.enter_context(nc.sbuf_tensor(name + "_row", [1, n], F32))
        ones = s.enter_context(nc.sbuf_tensor(name + "_ones", [1, 128], F32))
        ps = s.enter_context(nc.psum_tensor(name + "_ps", [128, n], F32))
        k.fence()
        vld = k.dma("sync", out=row[:], in_=src_dram)
        vo = k.op("vector", "memset", ap=ones[:], constant=1.0)
        k.wtd("tensor", vld)
        k.wt("tensor", "cV", k.sems["cV"].n)
        vmm = None
        for c0 in range(0, n, 512):
            c1 = min(c0 + 512, n)
            vmm = k.op("tensor", "matmul", out=ps[:, c0:c1], lhsT=ones[:, :],
                       rhs=row[:, c0:c1], start=True, stop=True)
        k.wt("vector", "cT", vmm)
        vcp = k.op("vector", "tensor_copy", out=tile[:], in_=ps[:])
    k.fence()
    return tile, vcp


def rsqrt_cols(k, buf, src_col, dst_col, eps):
    vv = k.op("vector", "tensor_scalar", out=buf[:, dst_col], in0=buf[:, src_col],
              scalar1=float(eps), scalar2=None, op0=ALU.add)
    k.wt("scalar", "cV", vv)
    vs = k.op("scalar", "activation", out=buf[:, dst_col], in_=buf[:, dst_col], func=AF.Sqrt)
    k.wt("vector", "cS", vs)
    vr = k.op("vector", "reciprocal", out=buf[:, dst_col], in_=buf[:, dst_col])
    k.wt("vector", "cV", vr)
    return vr


def layer_norm_tile(k, x_ap, out_ap, stats, mv, eps=1e-5, n=DM):
    nh = n // 512
    vs = None
    for hh in range(nh):
        vs = k.op("vector", "bn_stats", out=stats[:, hh * 6:(hh + 1) * 6], in_=x_ap[:, hh * 512:(hh + 1) * 512])
    k.wt("vector", "cV", vs)
    va = k.op("vector", "bn_aggr", out=mv[:, 0:2], in_=stats[:, 0:6 * nh])
    k.wt("vector", "cV", va)
    rsqrt_cols(k, mv, slice(1, 2), slice(2, 3), eps)
    return k.op("vector", "tensor_scalar", out=out_ap, in0=x_ap, scalar1=mv[:, 0:1],
                scalar2=mv[:, 2:3], op0=ALU.subtract, op1=ALU.mult)



def emit_l0a(k, xin, xout):
    nc = k.nc
    seq, nt, nq = k.seq, k.nt, k.nq
    D = k.dram
    H, HD, INNER = 8, 64, 512
    hscr = D["hscr"]

    with contextlib.ExitStack() as ph:
        k.fence()
        ones_col = ph.enter_context(nc.sbuf_tensor("a_ones_col", [128, 1], BF16))
        mneg8 = ph.enter_context(nc.sbuf_tensor("a_mneg8", [128, 1], F32))
        trmask = ph.enter_context(nc.sbuf_tensor("a_trmask", [128, 4, 512], BF16))
        ident = ph.enter_context(nc.sbuf_tensor("a_ident", [128, 128], BF16))
        k.op("vector", "memset", ap=ones_col[:], constant=1.0)
        k.op("vector", "memset", ap=mneg8[:], constant=-8.0)
        k.dma("sync", out=trmask[:], in_=D["t_trmask"].rearrange("e p q -> p e q"))
        k.dma("sync", out=ident[:], in_=D["t_ident"][:, :])

        with contextlib.ExitStack() as s1:
            lnw_bc, _ = bcast_row(k, "a_lnw", D["L0_attn_ln_w"][None, :], DM, s1)
            lnb_bc, vbc = bcast_row(k, "a_lnb", D["L0_attn_ln_b"][None, :], DM, s1)
            x_sb = s1.enter_context(nc.sbuf_tensor("a_x", [128, 2, DM], F32))
            xn1 = s1.enter_context(nc.sbuf_tensor("a_xn1", [128, DM], F32))
            scr1 = s1.enter_context(nc.sbuf_tensor("a_scr1", [128, DM], F32))
            xn2 = s1.enter_context(nc.sbuf_tensor("a_xn2", [128, DM], BF16))
            stats = s1.enter_context(nc.sbuf_tensor("a_stats", [128, 12], F32))
            mv = s1.enter_context(nc.sbuf_tensor("a_mv", [128, 4], F32))
            k.wt("vector", "cV", vbc)
            last_st = None
            xload = {0: None, 1: None}
            prev_use = {0: 0, 1: 0}
            for tc in range(nt):
                b = tc % 2
                k.wt("sync", "cV", prev_use[b])
                xload[b] = k.dma("sync", out=x_sb[:, b, :], in_=xin[tc * 128:(tc + 1) * 128, :])
                k.wtd("vector", xload[b])
                layer_norm_tile(k, x_sb[:, b, :], xn1[:], stats, mv)
                layer_norm_tile(k, xn1[:], scr1[:], stats, mv)
                k.op("vector", "tensor_tensor", out=xn1[:], in0=scr1[:], in1=lnw_bc[:], op=ALU.mult)
                k.wtd("vector", last_st)
                vn = k.op("vector", "tensor_tensor", out=xn2[:], in0=xn1[:], in1=lnb_bc[:], op=ALU.add)
                prev_use[b] = vn
                k.wt("sync", "cV", vn)
                last_st = k.dma("sync", out=hscr[tc * 128:(tc + 1) * 128, 0:DM], in_=xn2[:])

        v_sb = ph.enter_context(nc.sbuf_tensor("a_v", [128, nt, 4, 192], BF16))
        vones = k.op("vector", "memset", ap=v_sb[:, :, :, 64:128], constant=1.0)
        with contextlib.ExitStack() as s2:
            k.fence()
            xn2T = s2.enter_context(nc.sbuf_tensor("a_xn2T", [128, 8, seq], BF16))
            k.wtd("sync", last_st)
            vtr = []
            for c in range(8):
                vtr.append(k.dma("sync", out=xn2T[:, c, :], in_=hscr[:, c * 128:(c + 1) * 128],
                                 transpose=True))
            wqkv = s2.enter_context(nc.sbuf_tensor("a_wqkv", [128, 8, 1536], BF16))
            vwl = k.dma("gpsimd", out=wqkv[:], in_=D["L0_attn_w_qkv"].rearrange("(c p) n -> p c n", p=128))
            qk_bf = s2.enter_context(nc.sbuf_tensor("a_qkbf", [128, INNER], BF16))
            sq_scr = s2.enter_context(nc.sbuf_tensor("a_sq", [128, INNER], F32))
            hsums = s2.enter_context(nc.sbuf_tensor("a_hs", [128, 2 * H], F32))
            psA = s2.enter_context(nc.psum_tensor("a_psA", [128, 512], F32))
            psB = s2.enter_context(nc.psum_tensor("a_psB", [128, 512], F32))
            k.wtd("tensor", vtr)
            k.wtd("tensor", vwl)
            ps = {0: psA, 1: psB}
            freed = {0: 0, 1: 0}
            freed_eng = {0: "cV", 1: "cV"}
            vqk_store = None
            for tc in range(nt):
                for oi, o0 in enumerate((0, 512, 1024)):
                    b = (tc * 3 + oi) % 2
                    pp = ps[b]
                    k.wt("tensor", freed_eng[b], freed[b])
                    vm = None
                    for c in range(8):
                        vm = k.op("tensor", "matmul", out=pp[:],
                                  lhsT=xn2T[:, c, tc * 128:(tc + 1) * 128],
                                  rhs=wqkv[:, c, o0:o0 + 512],
                                  start=(c == 0), stop=(c == 7))
                    if o0 == 1024:
                        k.wt("scalar", "cT", vm)
                        ppv = pp[:].rearrange("p (q e) -> p q e", q=4)
                        k.op("scalar", "copy", out=v_sb[:, tc, :, 0:64],
                             in_=ppv[:, :, 0:64])
                        freed[b] = k.op("scalar", "copy", out=v_sb[:, tc, :, 128:192],
                                        in_=ppv[:, :, 64:128])
                        freed_eng[b] = "cS"
                    else:
                        k.wt("scalar", "cT", vm)
                        vsq = k.op("scalar", "activation", out=sq_scr[:], in_=pp[:], func=AF.Square)
                        k.wt("vector", "cS", vsq)
                        k.wtd("vector", vqk_store)
                        k.op("vector", "tensor_reduce", out=hsums[:, 0:H],
                             in_=sq_scr[:].rearrange("p (h d) -> p h d", h=H), axis=AX.X, op=ALU.add)
                        rsqrt_cols(k, hsums, slice(0, H), slice(H, 2 * H), 1e-12)
                        vn = None
                        for h in range(H):
                            vn = k.op("vector", "tensor_scalar", out=qk_bf[:, h * HD:(h + 1) * HD],
                                      in0=pp[:, h * HD:(h + 1) * HD], scalar1=hsums[:, H + h:H + h + 1],
                                      scalar2=None, op0=ALU.mult)
                        freed[b] = vn
                        freed_eng[b] = "cV"
                        k.wt("sync", "cV", vn)
                        vqk_store = k.dma("sync", out=hscr[tc * 128:(tc + 1) * 128, DM + o0:DM + o0 + 512],
                                          in_=qk_bf[:])

        k.fence()
        qT = ph.enter_context(nc.sbuf_tensor("a_qT", [128, 4, seq], BF16))
        kT = ph.enter_context(nc.sbuf_tensor("a_kT", [128, 4, seq], BF16))
        k.wtd("sync", vqk_store)
        k.fence(engines=("sync",))
        vq = []
        for c in range(4):
            vq.append(k.dma("sync", out=qT[:, c, :], in_=hscr[:, DM + c * 128:DM + (c + 1) * 128],
                            transpose=True))
            vq.append(k.dma("sync", out=kT[:, c, :], in_=hscr[:, DM + 512 + c * 128:DM + 512 + (c + 1) * 128],
                            transpose=True))
        wout = ph.enter_context(nc.sbuf_tensor("a_wout", [128, 4, DM], BF16))
        vwo = k.dma("gpsimd", out=wout[:], in_=D["L0_attn_w_out"].rearrange("(c p) n -> p c n", p=128))

        with contextlib.ExitStack() as s3:
            k.fence()
            big2 = s3.enter_context(nc.psum_tensor("a_big2", [128, 1024], F32))
            yt4 = [s3.enter_context(nc.psum_tensor(f"a_yt{t}", [128, 512], F32)) for t in range(4)]
            outps = s3.enter_context(nc.psum_tensor("a_outps", [128, 1024], F32))
            pts = s3.enter_context(nc.sbuf_tensor("a_pts", [128, 2, 512], BF16))
            rden = s3.enter_context(nc.sbuf_tensor("a_rden", [128, 512], F32))
            oT_sb = s3.enter_context(nc.sbuf_tensor("a_oT4", [128, 4, 512], BF16))
            xr_sb = s3.enter_context(nc.sbuf_tensor("a_xr", [128, 4, DM], F32))
            xnew = s3.enter_context(nc.sbuf_tensor("a_xnew", [128, DM], F32))

            k.wtd("tensor", vq)
            k.wtd("tensor", vwo)
            k.wait_all("tensor")
            exp_free = {0: 0, 1: 0}
            big_free = {0: ("cS", 0), 1: ("cS", 0)}
            banks_free = 0
            oT_free = 0
            out_free = {0: 0, 1: 0}
            st_out = None
            last_vadd = 0
            tig = [0]
            for qc in range(nq):
                k.wt("sync", "cV", last_vadd)
                vxl = k.dma("sync", out=xr_sb[:], in_=xin[qc * 512:(qc + 1) * 512, :].rearrange(
                    "(s p) d -> p s d", p=128))
                nkt = 4 * qc + 4
                for w in range(2):
                    k.wt("tensor", "cV", banks_free)
                    tiles = [(4 * w + j, kt) for j in range(4) for kt in range(nkt)]

                    def emit_pv_a(p2, h2, kt2, gate2, nkt=nkt, w=w):
                        j2 = h2 - 4 * w
                        side = h2 % 2
                        k.wt("tensor", gate2[0], gate2[1])
                        vpe = k.op("tensor", "matmul", out=yt4[j2][:],
                                   lhsT=v_sb[:, kt2, h2 // 2, side * 64:side * 64 + 128],
                                   rhs=pts[:, p2, :],
                                   start=(kt2 == 0), stop=(kt2 == nkt - 1),
                                   skip_group_check=True)
                        exp_free[p2] = vpe
                        return vpe

                    pend = None
                    last_pv = 0
                    for (h, kt) in tiles:
                        cih, off = h // 2, (h % 2) * 64
                        p = tig[0] % 2
                        tig[0] += 1
                        bslice = big2[:, p * 512:(p + 1) * 512]
                        k.wt("tensor", *big_free[p])
                        vmm = k.op("tensor", "matmul", out=bslice,
                                   lhsT=kT[off:off + 64, cih, kt * 128:(kt + 1) * 128],
                                   rhs=qT[off:off + 64, cih, qc * 512:(qc + 1) * 512],
                                   start=True, stop=True)
                        k.wt("scalar", "cT", vmm)
                        k.wt("scalar", "cT", exp_free[p])
                        vexp = k.op("scalar", "activation", out=pts[:, p, :], in_=bslice,
                                    func=AF.Exp, bias=mneg8[:], scale=8.0, selfwait=False)
                        big_free[p] = ("cS", vexp)
                        e = kt - 4 * qc
                        gate = ("cS", vexp)
                        if e >= 0:
                            k.wt("vector", "cS", vexp)
                            vmask = k.op("vector", "tensor_tensor", out=pts[:, p, :],
                                         in0=pts[:, p, :], in1=trmask[:, e, :],
                                         op=ALU.mult, selfwait=False)
                            gate = ("cV", vmask)
                        if pend is not None:
                            last_pv = emit_pv_a(*pend)
                        pend = (p, h, kt, gate)
                    last_pv = emit_pv_a(*pend)
                    k.wt("vector", "cT", last_pv)
                    vm1 = 0
                    for j in range(4):
                        h = 4 * w + j
                        pvr = (h % 2) * 64
                        dnr = 64 - pvr
                        k.op("vector", "reciprocal", out=rden[pvr:pvr + 64, :],
                             in_=yt4[j][dnr:dnr + 64, :])
                        k.wt("vector", "cT", oT_free)
                        vm1 = k.op("vector", "tensor_tensor",
                                   out=oT_sb[pvr:pvr + 64, 2 * w + j // 2, :],
                                   in0=yt4[j][pvr:pvr + 64, :],
                                   in1=rden[pvr:pvr + 64, :], op=ALU.mult)
                    banks_free = vm1
                k.wt("tensor", "cV", banks_free)
                vmm2 = 0
                for s in range(4):
                    pp_out = outps if s % 2 == 0 else big2
                    k.wt("tensor", "cV", out_free[s % 2])
                    if s % 2 == 1:
                        k.wt("tensor", *big_free[0])
                        k.wt("tensor", *big_free[1])
                    for c in range(4):
                        for n0 in (0, 512):
                            vmm2 = k.op("tensor", "matmul", out=pp_out[:, n0:n0 + 512],
                                        lhsT=oT_sb[:, c, s * 128:(s + 1) * 128],
                                        rhs=wout[:, c, n0:n0 + 512],
                                        start=(c == 0), stop=(c == 3))
                    k.wt("vector", "cT", vmm2)
                    k.wtd("vector", vxl)
                    k.wtd("vector", st_out)
                    vadd = k.op("vector", "tensor_tensor", out=xnew[:], in0=pp_out[:],
                                in1=xr_sb[:, s, :], op=ALU.add)
                    out_free[s % 2] = vadd
                    last_vadd = vadd
                    k.wt("sync", "cV", vadd)
                    st_out = k.dma("sync", out=xout[qc * 512 + s * 128:qc * 512 + (s + 1) * 128, :], in_=xnew[:])
                oT_free = vmm2
                big_free[0] = big_free[1] = ("cV", out_free[1])
        k.fence()



WSPECS_L0A = [
    ("L0_attn_ln_w", (DM,), F32),
    ("L0_attn_ln_b", (DM,), F32),
    ("L0_attn_w_qkv", (DM, 1536), F32),
    ("L0_attn_w_out", (512, DM), F32),
]
TSPECS = [
    ("t_ident", (128, 128), BF16),
    ("t_trmask", (4, 128, 512), BF16),
    ("t_pvlr", (128, 2, 128), BF16),
    ("t_pvq4", (128, 4, 128), BF16),
]


def build_l0a(seq):
    nc = bass.Bass("TRN2", target_bir_lowering=False)
    es = contextlib.ExitStack()
    with es:
        k = K(nc, es, seq)
        D = k.dram
        D["xin"] = nc.dram_tensor("xin", [seq, DM], F32, kind="ExternalInput").ap()
        for nm, shp, dt in WSPECS_L0A + TSPECS:
            D[nm] = nc.dram_tensor(nm, list(shp), dt, kind="ExternalInput").ap()
        D["hscr"] = nc.dram_tensor("hscr", [seq, 2048], BF16, kind="Internal").ap()
        D["xout"] = nc.dram_tensor("xout", [seq, DM], F32, kind="ExternalOutput").ap()
        emit_l0a(k, D["xin"], D["xout"])
        k.build()
    return nc





class _CloseBefore:

    def __init__(self, es):
        self.es = es

    def __enter__(self):
        return self

    def __exit__(self, *a):
        self.es.close()
        return False


def rms_tile(k, x_ap, out_bf, wbc, sq_scr, red, eps=1e-5, n=DM):
    h = n // 2
    k.op("vector", "tensor_tensor", out=sq_scr[:, 0:h], in0=x_ap[:, 0:h], in1=x_ap[:, 0:h],
         op=ALU.mult)
    k.op("vector", "tensor_reduce", out=red[:, 0:1], in_=sq_scr[:, 0:h], axis=AX.X, op=ALU.add)
    k.op("vector", "tensor_tensor", out=sq_scr[:, 0:h], in0=x_ap[:, h:n], in1=x_ap[:, h:n],
         op=ALU.mult)
    k.op("vector", "tensor_reduce", out=red[:, 2:3], in_=sq_scr[:, 0:h], axis=AX.X, op=ALU.add)
    k.op("vector", "tensor_tensor", out=red[:, 0:1], in0=red[:, 0:1], in1=red[:, 2:3], op=ALU.add)
    k.op("vector", "tensor_scalar", out=red[:, 0:1], in0=red[:, 0:1], scalar1=1.0 / n,
         scalar2=None, op0=ALU.mult)
    rsqrt_cols(k, red, slice(0, 1), slice(1, 2), eps)
    return k.op("vector", "scalar_tensor_tensor", out=out_bf[:], in0=x_ap,
                scalar=red[:, 1:2], in1=wbc[:], op0=ALU.mult, op1=ALU.mult)


def rope_pairs(k, qf_view, cos_bc, sin_bc, scrE, scrO, scrE2, scrO2):
    nh, ng = qf_view.shape[1], qf_view.shape[2]
    qe = qf_view[:, :, :, 0]
    qo = qf_view[:, :, :, 1]
    sE = scrE[:, 0:nh * ng].rearrange("p (h g) -> p h g", h=nh)
    sO = scrO[:, 0:nh * ng].rearrange("p (h g) -> p h g", h=nh)
    sE2 = scrE2[:, 0:nh * ng].rearrange("p (h g) -> p h g", h=nh)
    sO2 = scrO2[:, 0:nh * ng].rearrange("p (h g) -> p h g", h=nh)
    k.op("vector", "tensor_tensor", out=sE, in0=qe, in1=cos_bc, op=ALU.mult)
    k.op("vector", "tensor_tensor", out=sE2, in0=qo, in1=sin_bc, op=ALU.mult)
    k.op("vector", "tensor_tensor", out=sO, in0=qo, in1=cos_bc, op=ALU.mult)
    k.op("vector", "tensor_tensor", out=sO2, in0=qe, in1=sin_bc, op=ALU.mult)
    k.op("vector", "tensor_tensor", out=qe, in0=sE, in1=sE2, op=ALU.subtract)
    return k.op("vector", "tensor_tensor", out=qo, in0=sO, in1=sO2, op=ALU.add)


def emit_l0f(k, xin, xout):
    nc = k.nc
    seq, nt, nq = k.seq, k.nt, k.nq
    D = k.dram
    H = 16
    hscr, zb, r1b, macc = D["hscr"], D["zb"], D["r1b"], D["macc"]

    with contextlib.ExitStack() as ph:
        k.fence()
        ident = ph.enter_context(nc.sbuf_tensor("f_ident", [128, 128], BF16))
        k.dma("sync", out=ident[:], in_=D["t_ident"][:, :])
        red = ph.enter_context(nc.sbuf_tensor("f_red", [128, 4], F32))
        stats = ph.enter_context(nc.sbuf_tensor("f_stats", [128, 12], F32))
        mvst = ph.enter_context(nc.sbuf_tensor("f_mv", [128, 4], F32))

        with contextlib.ExitStack() as s1:
            k.fence()
            w1bc, vbc = bcast_row(k, "f_w1", D["L0_ffn_w_ln1"][None, :], DM, s1)
            x_sb = s1.enter_context(nc.sbuf_tensor("f_x", [128, 2, DM], F32))
            z_sb = s1.enter_context(nc.sbuf_tensor("f_z", [128, DM], F32))
            sq = s1.enter_context(nc.sbuf_tensor("f_sq", [128, DM], F32))
            tbf = s1.enter_context(nc.sbuf_tensor("f_tbf", [128, DM], BF16))
            k.wt("vector", "cV", vbc)
            last_st = None
            zst = None
            prev = {0: 0, 1: 0}
            for tc in range(nt):
                b = tc % 2
                k.wt("sync", "cV", prev[b])
                vl = k.dma("sync", out=x_sb[:, b, :], in_=xin[tc * 128:(tc + 1) * 128, :])
                k.wtd("vector", vl)
                k.wtd("vector", zst)
                prev[b] = layer_norm_tile(k, x_sb[:, b, :], z_sb[:], stats, mvst)
                k.wt("sync", "cV", prev[b])
                zst = k.dma("sync", out=zb[tc * 128:(tc + 1) * 128, :], in_=z_sb[:])
                k.wtd("vector", last_st)
                vn = rms_tile(k, z_sb[:], tbf, w1bc, sq, red)
                prev[b] = vn
                k.wt("sync", "cV", vn)
                last_st = k.dma("sync", out=hscr[tc * 128:(tc + 1) * 128, 0:DM], in_=tbf[:])

        with contextlib.ExitStack() as satt:
            kT = satt.enter_context(nc.sbuf_tensor("f_kT", [128, 8, seq], BF16))
            sproj = contextlib.ExitStack()
            satt.enter_context(_CloseBefore(sproj))
            latT = sproj.enter_context(nc.sbuf_tensor("f_latT", [128, seq], BF16))

            with contextlib.ExitStack() as s2:
                k.fence()
                tT = s2.enter_context(nc.sbuf_tensor("f_tT", [128, 8, seq], BF16))
                k.wtd("sync", last_st)
                vtr = []
                for c in range(8):
                    vtr.append(k.dma("sync", out=tT[:, c, :], in_=hscr[:, c * 128:(c + 1) * 128],
                                     transpose=True))
                wkva = s2.enter_context(nc.sbuf_tensor("f_wkva", [128, 8, 160], BF16))
                vw1 = k.dma("gpsimd", out=wkva[:], in_=D["L0_ffn_Wkva"].rearrange("(c p) n -> p c n", p=128))
                wq = s2.enter_context(nc.sbuf_tensor("f_wq", [128, 8, DM], BF16))
                vw2 = k.dma("gpsimd", out=wq[:], in_=D["L0_ffn_Wq"].rearrange("(c p) n -> p c n", p=128))
                cosP = s2.enter_context(nc.sbuf_tensor("f_cosP", [128, nt, 16], F32))
                sinP = s2.enter_context(nc.sbuf_tensor("f_sinP", [128, nt, 16], F32))
                vc1 = k.dma("sync", out=cosP[:], in_=D["t_mla_cos"].rearrange("(t p) g -> p t g", p=128))
                vc2 = k.dma("sync", out=sinP[:], in_=D["t_mla_sin"].rearrange("(t p) g -> p t g", p=128))
                psq = s2.enter_context(nc.psum_tensor("f_psq", [128, 1024], F32))
                psl = s2.enter_context(nc.psum_tensor("f_psl", [128, 512], F32))
                psr = s2.enter_context(nc.psum_tensor("f_psr", [128, 32], F32))
                qf = s2.enter_context(nc.sbuf_tensor("f_qf", [128, DM], F32))
                qb = s2.enter_context(nc.sbuf_tensor("f_qb", [128, DM], BF16))
                krf = s2.enter_context(nc.sbuf_tensor("f_krf", [128, 32], F32))
                krb = s2.enter_context(nc.sbuf_tensor("f_krb", [128, 32], BF16))
                scrE = s2.enter_context(nc.sbuf_tensor("f_scrE", [128, 256], F32))
                scrO = s2.enter_context(nc.sbuf_tensor("f_scrO", [128, 256], F32))
                scrE2 = s2.enter_context(nc.sbuf_tensor("f_scrE2", [128, 256], F32))
                scrO2 = s2.enter_context(nc.sbuf_tensor("f_scrO2", [128, 256], F32))
                k.wtd("tensor", vtr)
                k.wtd("tensor", [vw1, vw2])
                k.wtd("vector", [vc1, vc2])
                vlat_ev = None
                for ch in range(seq // 512):
                    vm = None
                    for c in range(8):
                        vm = k.op("tensor", "matmul", out=psl[:],
                                  lhsT=wkva[:, c, 0:128], rhs=tT[:, c, ch * 512:(ch + 1) * 512],
                                  start=(c == 0), stop=(c == 7))
                    k.wt("scalar", "cT", vm)
                    vlat_ev = k.op("scalar", "copy", out=latT[:, ch * 512:(ch + 1) * 512], in_=psl[:])
                    k.wt("tensor", "cS", vlat_ev)
                vst_q = None
                vst_kr = None
                for tc in range(nt):
                    cs_bc = cosP[:, tc, :].unsqueeze(1)
                    sn_bc = sinP[:, tc, :].unsqueeze(1)
                    vm = None
                    for c in range(8):
                        vm = k.op("tensor", "matmul", out=psr[:],
                                  lhsT=tT[:, c, tc * 128:(tc + 1) * 128], rhs=wkva[:, c, 128:160],
                                  start=(c == 0), stop=(c == 7))
                    k.wt("vector", "cT", vm)
                    k.wtd("vector", vst_kr)
                    k.op("vector", "tensor_copy", out=krf[:], in_=psr[:])
                    vr = rope_pairs(k, krf[:].rearrange("p (h g two) -> p h g two", h=1, two=2),
                                    cs_bc.broadcast_to([128, 1, 16]), sn_bc.broadcast_to([128, 1, 16]),
                                    scrE, scrO, scrE2, scrO2)
                    vkb = k.op("vector", "tensor_copy", out=krb[:], in_=krf[:])
                    k.wt("tensor", "cV", vr)
                    k.wt("sync", "cV", vkb)
                    vst_kr = k.dma("sync", out=hscr[tc * 128:(tc + 1) * 128, 0:32], in_=krb[:])
                    vm = None
                    for half in range(2):
                        for c in range(8):
                            vm = k.op("tensor", "matmul", out=psq[:, half * 512:(half + 1) * 512],
                                      lhsT=tT[:, c, tc * 128:(tc + 1) * 128],
                                      rhs=wq[:, c, half * 512:(half + 1) * 512],
                                      start=(c == 0), stop=(c == 7))
                    k.wt("vector", "cT", vm)
                    k.wtd("vector", vst_q)
                    k.op("vector", "tensor_scalar", out=qf[:], in0=psq[:], scalar1=0.125,
                         scalar2=None, op0=ALU.mult)
                    qv = qf[:].rearrange("p (h d) -> p h d", h=16)[:, :, 32:64]
                    vq2 = rope_pairs(k, qv.rearrange("p h (g two) -> p h g two", two=2),
                                     cs_bc.broadcast_to([128, 16, 16]), sn_bc.broadcast_to([128, 16, 16]),
                                     scrE, scrO, scrE2, scrO2)
                    vqb = k.op("vector", "tensor_copy", out=qb[:], in_=qf[:])
                    k.wt("tensor", "cV", vq2)
                    k.wt("sync", "cV", vqb)
                    vst_q = k.dma("sync", out=hscr[tc * 128:(tc + 1) * 128, DM:2 * DM], in_=qb[:])

            kv_aug = satt.enter_context(nc.sbuf_tensor("f_kvaug", [128, nt, 8, 96], BF16,
                                                       side="right"))
            with contextlib.ExitStack() as s3:
                k.fence()
                kvT = s3.enter_context(nc.sbuf_tensor("f_kvT", [128, 4, seq], BF16))
                wkvb = s3.enter_context(nc.sbuf_tensor("f_wkvb", [128, 512], BF16))
                vwb = k.dma("gpsimd", out=wkvb[:], in_=D["L0_ffn_Wkvb"][:, :])
                psA = s3.enter_context(nc.psum_tensor("f_psA", [128, 512], F32))
                psB = s3.enter_context(nc.psum_tensor("f_psB", [128, 512], F32))
                k.wtd("tensor", vwb)
                freed = {0: 0, 1: 0}
                for c4 in range(4):
                    for ch in range(seq // 512):
                        b = (c4 * (seq // 512) + ch) % 2
                        pp = psA if b == 0 else psB
                        k.wt("tensor", "cS", freed[b])
                        vm = k.op("tensor", "matmul", out=pp[:],
                                  lhsT=wkvb[:, c4 * 128:(c4 + 1) * 128],
                                  rhs=latT[:, ch * 512:(ch + 1) * 512], start=True, stop=True)
                        k.wt("scalar", "cT", vm)
                        freed[b] = k.op("scalar", "copy", out=kvT[:, c4, ch * 512:(ch + 1) * 512], in_=pp[:])
                k.fence(engines=("sync",))
                vasm = []
                for c in range(8):
                    for par, h in ((0, 2 * c), (64, 2 * c + 1)):
                        vasm.append(k.dma("sync", out=kT[par:par + 32, c, :],
                                          in_=kvT[(h % 4) * 32:(h % 4) * 32 + 32, h // 4, :]))
                krT = s3.enter_context(nc.sbuf_tensor("f_krT", [128, seq], BF16))
                vkt = k.dma("sync", out=krT[:], in_=hscr[:, 0:128], transpose=True)
                k.wtd("sync", vkt)
                for c in range(8):
                    for par in (32, 96):
                        vasm.append(k.dma("sync", out=kT[par:par + 32, c, :],
                                          in_=krT[0:32, :]))
                vko = k.op("vector", "memset", ap=kv_aug[:, :, :, 32:64], constant=1.0)
                k.wt("tensor", "cS", k.sems["cS"].n)
                freed = {0: 0, 1: 0}
                for tc in range(nt):
                    b = tc % 2
                    pp = psA if b == 0 else psB
                    k.wt("tensor", "cV", freed[b] if freed[b] else vko)
                    vm = k.op("tensor", "matmul", out=pp[:], lhsT=latT[:, tc * 128:(tc + 1) * 128],
                              rhs=wkvb[:], start=True, stop=True)
                    k.wt("vector", "cT", vm)
                    ppk = pp[:].rearrange("p (q e) -> p q e", q=8)
                    k.op("vector", "tensor_copy", out=kv_aug[:, tc, :, 0:32],
                         in_=ppk[:, :, 0:32])
                    freed[b] = k.op("vector", "tensor_copy",
                                    out=kv_aug[:, tc, :, 64:96],
                                    in_=ppk[:, :, 32:64])
                k.wtd("sync", vasm)

            sproj.close()
            with contextlib.ExitStack() as s4:
                k.fence()
                qT = s4.enter_context(nc.sbuf_tensor("f_qT", [128, 8, 2, 512], BF16))

                def load_qt(qc2):
                    toks = []
                    for c in range(8):
                        toks.append(k.dma("sync", out=qT[:, c, qc2 % 2, :],
                                          in_=hscr[qc2 * 512:(qc2 + 1) * 512,
                                                   DM + c * 128:DM + (c + 1) * 128],
                                          transpose=True))
                    return toks
                qload = {0: load_qt(0), 1: None}
                qk_last = {0: 0, 1: 0}
                wosub = s4.enter_context(nc.sbuf_tensor("f_wosub", [128, 4, DM], BF16))
                wo4 = D["L0_ffn_Wo"].rearrange("(h t d) n -> t h d n", t=2, d=32)[0]
                vwo = []
                for h in range(16):
                    vwo.append(k.dma("gpsimd", out=wosub[(h % 4) * 32:(h % 4) * 32 + 32, h // 4, :],
                                     in_=wo4[h]))
                trm = s4.enter_context(nc.sbuf_tensor("f_trm", [128, 4, 512], BF16))
                vmk = k.dma("sync", out=trm[:], in_=D["t_trmask"].rearrange("e p q -> p e q"))
                w2bc, _ = bcast_row(k, "f_w2", D["L0_ffn_w_ln2"][None, :], DM, s4)
                mbias = s4.enter_context(nc.sbuf_tensor("f_mbias", [128, 1], F32))
                vmb = k.op("vector", "memset", ap=mbias[:], constant=-12.0)
                sc = s4.enter_context(nc.psum_tensor("f_sc", [128, 3, 512], F32))
                ytf = [s4.enter_context(nc.psum_tensor(f"f_ytf{t}", [128, 512], F32))
                       for t in range(4)]
                pts = s4.enter_context(nc.sbuf_tensor("f_pts", [128, 3, 512], BF16))
                rdf = s4.enter_context(nc.sbuf_tensor("f_rdf", [128, 512], F32))
                oTf = s4.enter_context(nc.sbuf_tensor("f_oTf", [128, 4, 512], BF16))
                zt = s4.enter_context(nc.sbuf_tensor("f_zt", [128, 2, DM], F32))
                r1sb = s4.enter_context(nc.sbuf_tensor("f_r1", [128, DM], F32))
                usb = s4.enter_context(nc.sbuf_tensor("f_u", [128, DM], BF16))
                sqc = s4.enter_context(nc.sbuf_tensor("f_sqc", [128, 512], F32))
                k.wtd("tensor", vwo)
                k.wtd("vector", vmk)
                k.wt("scalar", "cV", vmb)

                scfree = [None, None, None]
                sci = [0]
                ptsfree = [None, None, None]
                ptsi = [0]
                fbanks_free = 0
                oTf_free = 0
                st_r1 = {0: None, 1: None}
                st_u = {0: None, 1: None}
                zprev = {0: 0, 1: 0}

                def sc_slot(eng):
                    s = sci[0] % 3
                    sci[0] += 1
                    if scfree[s]:
                        k.wt(eng, *scfree[s])
                    return s

                for qc in range(nq):
                    nkt = 4 * (qc + 1)
                    q0, q1 = qc * 512, (qc + 1) * 512
                    if qc + 1 < nq:
                        k.wt("sync", "cT", qk_last[(qc + 1) % 2])
                        qload[(qc + 1) % 2] = load_qt(qc + 1)
                    k.wtd("tensor", qload[qc % 2])
                    for hg in range(4):
                        k.wt("tensor", "cV", fbanks_free)
                        tiles = [(hg * 4 + hi, kt) for hi in range(4) for kt in range(nkt)]

                        def emit_pv(p, nkt=nkt):
                            ps2, h2, kt2, gate2 = p
                            j2 = h2 % 4
                            side = h2 % 2
                            k.wt("tensor", *gate2)
                            vpv = k.op("tensor", "matmul", out=ytf[j2][0:64, :],
                                       lhsT=kv_aug[:, kt2, h2 // 2,
                                                   side * 32:side * 32 + 64],
                                       rhs=pts[:, ps2, :],
                                       start=(kt2 == 0), stop=(kt2 == nkt - 1),
                                       skip_group_check=True)
                            ptsfree[ps2] = ("cT", vpv)
                            return vpv

                        pend = None
                        last_pv = 0
                        for (h, kt) in tiles:
                            cih, off = h // 2, (h % 2) * 64
                            ss = sc_slot("tensor")
                            vmm = k.op("tensor", "matmul", out=sc[:, ss, :],
                                       lhsT=kT[off:off + 64, cih, kt * 128:(kt + 1) * 128],
                                       rhs=qT[off:off + 64, cih, qc % 2, :],
                                       start=True, stop=True)
                            ps = ptsi[0] % 3
                            ptsi[0] += 1
                            if ptsfree[ps]:
                                k.wt("scalar", *ptsfree[ps])
                            k.wt("scalar", "cT", vmm)
                            vexp = k.op("scalar", "activation", out=pts[:, ps, :],
                                        in_=sc[:, ss, :], func=AF.Exp, bias=mbias[:],
                                        scale=1.0, selfwait=False)
                            scfree[ss] = ("cS", vexp)
                            e = kt - 4 * qc
                            gate = ("cS", vexp)
                            if e >= 0:
                                k.wt("vector", "cS", vexp)
                                vmsk = k.op("vector", "tensor_tensor", out=pts[:, ps, :],
                                            in0=pts[:, ps, :], in1=trm[:, e, :], op=ALU.mult,
                                            selfwait=False)
                                gate = ("cV", vmsk)
                            if pend is not None:
                                last_pv = emit_pv(pend)
                            pend = (ps, h, kt, gate)
                        last_pv = emit_pv(pend)
                        k.wt("vector", "cT", last_pv)
                        vm1 = 0
                        for j in range(4):
                            h = hg * 4 + j
                            pvr = (h % 2) * 32
                            dnr = 32 - pvr
                            k.op("vector", "reciprocal",
                                 out=rdf[j * 32:j * 32 + 32, :],
                                 in_=ytf[j][dnr:dnr + 32, :])
                            k.wt("vector", "cT", oTf_free)
                            vm1 = k.op("vector", "tensor_tensor",
                                       out=oTf[j * 32:j * 32 + 32, hg, :],
                                       in0=ytf[j][pvr:pvr + 32, :],
                                       in1=rdf[j * 32:j * 32 + 32, :],
                                       op=ALU.mult)
                        fbanks_free = vm1
                    qk_last[qc % 2] = last_pv
                    for qs in range(4):
                        b = qs % 2
                        tok0 = q0 + qs * 128
                        k.wt("sync", "cV", zprev[b])
                        vzl = k.dma("sync", out=zt[:, b, :], in_=zb[tok0:tok0 + 128, :])
                        if st_r1[1]:
                            k.wt("vector", *st_r1[1])
                        k.wt("tensor", "cV", fbanks_free)
                        sa = sc_slot("tensor")
                        sb2 = sc_slot("tensor")
                        vmo = None
                        for c in range(4):
                            k.op("tensor", "matmul", out=sc[:, sa, :],
                                 lhsT=oTf[:, c, qs * 128:(qs + 1) * 128],
                                 rhs=wosub[:, c, 0:512],
                                 start=(c == 0), stop=(c == 3), skip_group_check=True)
                            vmo = k.op("tensor", "matmul", out=sc[:, sb2, :],
                                       lhsT=oTf[:, c, qs * 128:(qs + 1) * 128],
                                       rhs=wosub[:, c, 512:1024],
                                       start=(c == 0), stop=(c == 3), skip_group_check=True)
                        oTf_free = vmo
                        k.wt("vector", "cT", vmo)
                        k.wtd("vector", vzl)
                        k.wtd("vector", st_r1[0])
                        k.op("vector", "tensor_tensor", out=r1sb[:, 0:512], in0=sc[:, sa, :],
                             in1=zt[:, b, 0:512], op=ALU.add, selfwait=False)
                        vr1 = k.op("vector", "tensor_tensor", out=r1sb[:, 512:1024],
                                   in0=sc[:, sb2, :], in1=zt[:, b, 512:1024], op=ALU.add)
                        zprev[b] = vr1
                        scfree[sa] = ("cV", vr1)
                        scfree[sb2] = ("cV", vr1)
                        k.wt("sync", "cV", vr1)
                        st_r1[0] = k.dma("sync", out=r1b[tok0:tok0 + 128, :], in_=r1sb[:])
                        k.wtd("vector", st_u[0])
                        vu = rms_tile(k, r1sb[:], usb, w2bc, sqc, red)
                        st_r1[1] = ("cV", vu)
                        k.wt("sync", "cV", vu)
                        st_u[0] = k.dma("sync", out=hscr[tok0:tok0 + 128, 0:DM], in_=usb[:])


        with contextlib.ExitStack() as s5:
            k.fence()
            uT = s5.enter_context(nc.sbuf_tensor("f_uT", [128, 8, seq], BF16))
            vut = []
            for c in range(8):
                vut.append(k.dma("sync", out=uT[:, c, :], in_=hscr[:, c * 128:(c + 1) * 128],
                                 transpose=True))
            wgate = s5.enter_context(nc.sbuf_tensor("f_wgate", [128, 8, 8], BF16))
            vwg = k.dma("gpsimd", out=wgate[:], in_=D["L0_ffn_Wgate"].rearrange("(c p) n -> p c n", p=128))
            ebias, vbc2 = bcast_row(k, "f_ebias", D["L0_ffn_expert_bias"][None, :], 8, s5)
            comb = s5.enter_context(nc.sbuf_tensor("f_comb", [128, nt, 8], F32))
            lg = s5.enter_context(nc.sbuf_tensor("f_lg", [128, 8], F32))
            cscr = s5.enter_context(nc.sbuf_tensor("f_cscr", [128, 8], F32))
            cscr2 = s5.enter_context(nc.sbuf_tensor("f_cscr2", [128, 8], F32))
            cred = s5.enter_context(nc.sbuf_tensor("f_cred", [128, 4], F32))
            psg = s5.enter_context(nc.psum_tensor("f_psg", [128, 512], F32))
            psu = s5.enter_context(nc.psum_tensor("f_psu", [128, 512], F32))
            ptp2 = s5.enter_context(nc.psum_tensor("f_ptp2", [128, 512], BF16))
            dn = {j: s5.enter_context(nc.psum_tensor(f"f_dn{j}", [128, 1024], F32))
                  for j in range(2)}
            gs = s5.enter_context(nc.sbuf_tensor("f_gs", [128, 512], F32))
            hb = s5.enter_context(nc.sbuf_tensor("f_hb", [128, 512], BF16))
            hT = s5.enter_context(nc.sbuf_tensor("f_hT", [128, 2, 512], BF16))
            nth = nt // 2
            acc = s5.enter_context(nc.sbuf_tensor("f_acc", [128, nth, DM], F32))
            wg_sb = s5.enter_context(nc.sbuf_tensor("f_wgsb", [128, 2, 8, 512], BF16))
            wu_sb = s5.enter_context(nc.sbuf_tensor("f_wusb", [128, 2, 8, 512], BF16))
            wd_sb = s5.enter_context(nc.sbuf_tensor("f_wdsb", [128, 2, 4, DM], BF16))
            xa = s5.enter_context(nc.sbuf_tensor("f_xa", [128, 2, DM], F32))
            xb2 = s5.enter_context(nc.sbuf_tensor("f_xb", [128, 2, DM], F32))
            xo = s5.enter_context(nc.sbuf_tensor("f_xo", [128, 2, DM], F32))
            k.wtd("tensor", vut)
            k.wtd("tensor", vwg)
            k.wt("vector", "cV", vbc2)
            for tc in range(nt):
                vm = None
                for c in range(8):
                    vm = k.op("tensor", "matmul", out=psg[:, 0:8], lhsT=uT[:, c, tc * 128:(tc + 1) * 128],
                              rhs=wgate[:, c, :], start=(c == 0), stop=(c == 7))
                k.wt("vector", "cT", vm)
                k.op("vector", "tensor_tensor", out=lg[:], in0=psg[:, 0:8], in1=ebias[:], op=ALU.add)
                vfree = k.sems["cV"].n
                k.wt("tensor", "cV", vfree)
                k.op("vector", "tensor_reduce", out=cred[:, 0:1], in_=lg[:], axis=AX.X, op=ALU.max)
                k.op("vector", "tensor_scalar", out=cscr[:], in0=lg[:], scalar1=cred[:, 0:1],
                     scalar2=None, op0=ALU.is_equal)
                k.op("vector", "scalar_tensor_tensor", out=cscr[:], in0=cscr[:], scalar=-1e30,
                     in1=lg[:], op0=ALU.mult, op1=ALU.add)
                k.op("vector", "tensor_reduce", out=cred[:, 1:2], in_=cscr[:], axis=AX.X, op=ALU.max)
                k.op("vector", "tensor_scalar", out=cscr2[:], in0=lg[:], scalar1=cred[:, 1:2],
                     scalar2=None, op0=ALU.is_ge)
                vng = k.op("vector", "tensor_scalar", out=cred[:, 2:3], in0=cred[:, 0:1],
                           scalar1=-1.0, scalar2=None, op0=ALU.mult)
                k.wt("scalar", "cV", vng)
                vex = k.op("scalar", "activation", out=cscr[:], in_=lg[:], func=AF.Exp,
                           bias=cred[:, 2:3], scale=1.0)
                k.wt("vector", "cS", vex)
                k.op("vector", "tensor_tensor", out=cscr[:], in0=cscr[:], in1=cscr2[:], op=ALU.mult)
                k.op("vector", "tensor_reduce", out=cred[:, 3:4], in_=cscr[:], axis=AX.X, op=ALU.add)
                k.op("vector", "reciprocal", out=cred[:, 3:4], in_=cred[:, 3:4])
                k.op("vector", "tensor_scalar", out=comb[:, tc, :], in0=cscr[:],
                     scalar1=cred[:, 3:4], scalar2=None, op0=ALU.mult)
            wtoks = {0: 0, 1: 0}
            st_out = {0: None, 1: None}
            xprev2 = {0: 0, 1: 0}
            vgu_done = k.sems["cV"].n
            silu_done = 0
            dn_free = {0: 0, 1: 0}
            hT_free = {0: 0, 1: 0}
            acc_rd2 = 0
            for half in range(2):
                t0h = half * nth
                for e in range(9):
                    b = e % 2
                    if e == 0:
                        srcs = (D["L0_ffn_sWg"][0], D["L0_ffn_sWu"][0], D["L0_ffn_sWd"][0])
                    else:
                        srcs = (D["L0_ffn_Wg"][e - 1], D["L0_ffn_Wu"][e - 1], D["L0_ffn_Wd"][e - 1])
                    k.wt("gpsimd", "cT", wtoks[b])
                    t1 = k.dma("gpsimd", out=wg_sb[:, b], in_=srcs[0].rearrange("(c p) n -> p c n", p=128))
                    t2 = k.dma("gpsimd", out=wu_sb[:, b], in_=srcs[1].rearrange("(c p) n -> p c n", p=128))
                    t3 = k.dma("gpsimd", out=wd_sb[:, b], in_=srcs[2].rearrange("(c p) n -> p c n", p=128))
                    k.wtd("tensor", [t1, t2, t3])
                    pend_dn = None

                    def emit_down(pd):
                        sl, tci, ve = pd
                        k.wt("tensor", "cV", ve)
                        k.wt("tensor", "cV", dn_free[sl])
                        vdm = None
                        for c in range(4):
                            for n0 in (0, 512):
                                vdm = k.op("tensor", "matmul", out=dn[sl][:, n0:n0 + 512],
                                           lhsT=hT[:, sl, c * 128:(c + 1) * 128],
                                           rhs=wd_sb[:, b, c, n0:n0 + 512],
                                           start=(c == 0), stop=(c == 3), skip_group_check=True)
                        hT_free[sl] = vdm
                        k.wt("vector", "cT", vdm)
                        if e == 0:
                            k.wt("vector", "cV", acc_rd2)
                            vcp = k.op("vector", "tensor_copy", out=acc[:, tci, :], in_=dn[sl][:],
                                       selfwait=False)
                        else:
                            vcp = k.op("vector", "scalar_tensor_tensor", out=acc[:, tci, :],
                                       in0=dn[sl][:], scalar=comb[:, t0h + tci, e - 1:e],
                                       in1=acc[:, tci, :], op0=ALU.mult, op1=ALU.add, selfwait=False)
                        dn_free[sl] = vcp
                        return vcp

                    for ti in range(nth):
                        tc = t0h + ti
                        sl = ti % 2
                        k.wt("tensor", "cV", vgu_done)
                        k.wt("tensor", "cS", silu_done)
                        vg = None
                        for c in range(8):
                            vg = k.op("tensor", "matmul", out=psg[:], lhsT=uT[:, c, tc * 128:(tc + 1) * 128],
                                      rhs=wg_sb[:, b, c, :], start=(c == 0), stop=(c == 7))
                        for c in range(8):
                            vg = k.op("tensor", "matmul", out=psu[:], lhsT=uT[:, c, tc * 128:(tc + 1) * 128],
                                      rhs=wu_sb[:, b, c, :], start=(c == 0), stop=(c == 7))
                        k.wt("scalar", "cT", vg)
                        vgs = k.op("scalar", "activation", out=gs[:], in_=psg[:], func=AF.Silu,
                                   selfwait=False)
                        silu_done = vgs
                        k.wt("vector", "cS", vgs)
                        vhb = k.op("vector", "tensor_tensor", out=hb[:], in0=psu[:], in1=gs[:],
                                   op=ALU.mult)
                        vgu_done = vhb
                        vev = None
                        for c in range(4):
                            k.wt("tensor", "cV", vhb if c == 0 else vev)
                            if c == 0:
                                k.wt("tensor", "cT", hT_free[sl])
                            vtp = k.op("tensor", "transpose", out=ptp2[:, c * 128:(c + 1) * 128],
                                       in_=hb[:, c * 128:(c + 1) * 128], identity=ident[:])
                            k.wt("vector", "cT", vtp)
                            vev = k.op("vector", "tensor_copy", out=hT[:, sl, c * 128:(c + 1) * 128],
                                       in_=ptp2[:, c * 128:(c + 1) * 128])
                        if pend_dn is not None:
                            emit_down(pend_dn)
                        pend_dn = (sl, ti, vev)
                    vlast_acc = emit_down(pend_dn)
                    wtoks[b] = k.sems["cT"].n
                for ti in range(nth):
                    tc = t0h + ti
                    fb = ti % 2
                    k.wt("sync", "cV", xprev2[fb])
                    va = k.dma("sync", out=xa[:, fb, :], in_=xin[tc * 128:(tc + 1) * 128, :])
                    vb_ = k.dma("sync", out=xb2[:, fb, :], in_=r1b[tc * 128:(tc + 1) * 128, :])
                    k.wtd("vector", [va, vb_])
                    k.wtd("vector", st_out[fb])
                    k.wt("vector", "cV", vlast_acc)
                    k.op("vector", "tensor_tensor", out=xo[:, fb, :], in0=xa[:, fb, :],
                         in1=xb2[:, fb, :], op=ALU.add)
                    vo2 = k.op("vector", "tensor_tensor", out=xo[:, fb, :], in0=xo[:, fb, :],
                               in1=acc[:, ti, :], op=ALU.add)
                    xprev2[fb] = vo2
                    acc_rd2 = vo2
                    k.wt("sync", "cV", vo2)
                    st_out[fb] = k.dma("sync", out=xout[tc * 128:(tc + 1) * 128, :], in_=xo[:, fb, :])
        k.fence()


WSPECS_L0F = [
    ("L0_ffn_w_ln1", (DM,), F32),
    ("L0_ffn_Wq", (DM, DM), F32),
    ("L0_ffn_Wkva", (DM, 160), F32),
    ("L0_ffn_Wkvb", (128, 512), F32),
    ("L0_ffn_Wo", (DM, DM), F32),
    ("L0_ffn_w_ln2", (DM,), F32),
    ("L0_ffn_Wgate", (DM, 8), F32),
    ("L0_ffn_expert_bias", (8,), F32),
    ("L0_ffn_Wg", (8, DM, 512), F32),
    ("L0_ffn_Wu", (8, DM, 512), F32),
    ("L0_ffn_Wd", (8, 512, DM), F32),
    ("L0_ffn_sWg", (1, DM, 512), F32),
    ("L0_ffn_sWu", (1, DM, 512), F32),
    ("L0_ffn_sWd", (1, 512, DM), F32),
]


def tspecs_l0f(seq):
    return [
        ("t_ident", (128, 128), BF16),
        ("t_trmask", (4, 128, 512), BF16),
        ("t_mla_cos", (seq, 16), F32),
        ("t_mla_sin", (seq, 16), F32),
    ]


def build_l0f(seq):
    nc = bass.Bass("TRN2", target_bir_lowering=False)
    es = contextlib.ExitStack()
    with es:
        k = K(nc, es, seq)
        D = k.dram
        D["xin"] = nc.dram_tensor("xin", [seq, DM], F32, kind="ExternalInput").ap()
        for nm, shp, dt in WSPECS_L0F + tspecs_l0f(seq):
            D[nm] = nc.dram_tensor(nm, list(shp), dt, kind="ExternalInput").ap()
        D["hscr"] = nc.dram_tensor("hscr", [seq, 2048], BF16, kind="Internal").ap()
        D["zb"] = nc.dram_tensor("zb", [seq, DM], F32, kind="Internal").ap()
        D["r1b"] = nc.dram_tensor("r1b", [seq, DM], F32, kind="Internal").ap()
        D["macc"] = nc.dram_tensor("macc", [seq, DM], F32, kind="Internal").ap()
        D["xout"] = nc.dram_tensor("xout", [seq, DM], F32, kind="ExternalOutput").ap()
        emit_l0f(k, D["xin"], D["xout"])
        k.build()
    return nc





def l1a_host_tables(seq):
    H = 8
    lam = np.log1p(-2.0 ** (-5.0 - np.arange(H, dtype=np.float64)))
    i = np.arange(seq, dtype=np.float64)
    c = (1.0 - np.exp((i[None, :] + 1.0) * lam[:, None])) / (1.0 - np.exp(lam))[:, None]
    rfq = np.exp((i % 512)[None, :] * lam[:, None]) / np.sqrt(c)
    rfk = np.exp(-(i % 128)[None, :] * lam[:, None]) / np.sqrt(128.0)
    lin = np.linspace(0.0, 1.0, 64)
    theta = 1.0 / 10000.0 ** lin
    ang = np.outer(i, theta)
    cosv, sinv = np.cos(ang), np.sin(ang)
    t = {}
    t["t_l1_cosq"] = (cosv[None] * rfq[:, :, None]).astype(np.float32)
    t["t_l1_sinq"] = (sinv[None] * rfq[:, :, None]).astype(np.float32)
    t["t_l1_cosk"] = (cosv[None] * rfk[:, :, None]).astype(np.float32)
    t["t_l1_sink"] = (sinv[None] * rfk[:, :, None]).astype(np.float32)
    nq = seq // 512
    ntk = seq // 128
    bsc = np.exp((512.0 * np.arange(nq)[None, :, None] - 128.0 * np.arange(ntk)[None, None, :])
                 * lam[:, None, None])
    return t, bsc


def emit_l1a(k, xin, xout):
    nc = k.nc
    seq, nt, nq = k.seq, k.nt, k.nq
    D = k.dram
    H = 8
    hscr, hscr2, vb = D["hscr"], D["hscr2"], D["vb"]
    _, bsc = l1a_host_tables(seq)
    EPS = 1e-5

    with contextlib.ExitStack() as ph:
        k.fence()
        ident = ph.enter_context(nc.sbuf_tensor("r_ident", [128, 128], BF16))
        ones_col = ph.enter_context(nc.sbuf_tensor("r_ones", [128, 1], BF16))
        trmask = ph.enter_context(nc.sbuf_tensor("r_trmask", [128, 4, 512], BF16))
        k.dma("sync", out=ident[:], in_=D["t_ident"][:, :])
        k.dma("sync", out=trmask[:], in_=D["t_trmask"].rearrange("e p q -> p e q"))
        k.op("vector", "memset", ap=ones_col[:], constant=1.0)
        stats = ph.enter_context(nc.sbuf_tensor("r_stats", [128, 12], F32))
        mvst = ph.enter_context(nc.sbuf_tensor("r_mv", [128, 4], F32))
        xnT = ph.enter_context(nc.sbuf_tensor("r_xnT", [128, 8, seq], BF16))

        with contextlib.ExitStack() as s1:
            k.fence()
            x_sb = s1.enter_context(nc.sbuf_tensor("r_x", [128, 2, DM], F32))
            xn = s1.enter_context(nc.sbuf_tensor("r_xn", [128, DM], BF16))
            last_st = None
            prev = {0: 0, 1: 0}
            for tc in range(nt):
                b = tc % 2
                k.wt("sync", "cV", prev[b])
                vl = k.dma("sync", out=x_sb[:, b, :], in_=xin[tc * 128:(tc + 1) * 128, :])
                k.wtd("vector", vl)
                k.wtd("vector", last_st)
                vn = layer_norm_tile(k, x_sb[:, b, :], xn[:], stats, mvst)
                prev[b] = vn
                k.wt("sync", "cV", vn)
                last_st = k.dma("sync", out=hscr[tc * 128:(tc + 1) * 128, 0:DM], in_=xn[:])
            k.wtd("sync", last_st)
            for c in range(8):
                k.dma("sync", out=xnT[:, c, :], in_=hscr[:, c * 128:(c + 1) * 128], transpose=True)
            k.fence(engines=("tensor",))

        with contextlib.ExitStack() as s2:
            k.fence()
            wqk = s2.enter_context(nc.sbuf_tensor("r_wqk", [128, 2, 8, 256], BF16))
            wv = s2.enter_context(nc.sbuf_tensor("r_wv", [128, 2, 8, 256], BF16))
            tblh = s2.enter_context(nc.sbuf_tensor("r_tbl", [128, nt, 4, 64], F32))
            psq = {tb: s2.enter_context(nc.psum_tensor(f"r_psq{tb}", [128, 256], F32))
                   for tb in range(2)}
            psv = {tb: s2.enter_context(nc.psum_tensor(f"r_psv{tb}", [128, 256], F32))
                   for tb in range(2)}
            qkf = s2.enter_context(nc.sbuf_tensor("r_qkf", [128, 2, 256], F32))
            qkb = s2.enter_context(nc.sbuf_tensor("r_qkb", [128, 2, 256], BF16))
            vbf = s2.enter_context(nc.sbuf_tensor("r_vbf", [128, 2, 256], BF16))
            scrE = s2.enter_context(nc.sbuf_tensor("r_scrE", [128, 128], F32))
            scrO = s2.enter_context(nc.sbuf_tensor("r_scrO", [128, 128], F32))
            scrE2 = s2.enter_context(nc.sbuf_tensor("r_scrE2", [128, 128], F32))
            scrO2 = s2.enter_context(nc.sbuf_tensor("r_scrO2", [128, 128], F32))
            wuse = {0: 0, 1: 0}
            st_qk = {0: None, 1: None}
            st_qk2 = {0: None, 1: None}
            st_v = {0: None, 1: None}
            qk_free = {0: 0, 1: 0}
            psq_free = {0: 0, 1: 0}
            psv_free = {0: 0, 1: 0}
            tbl_rd = 0
            tsrc = {0: "t_l1_cosq", 1: "t_l1_cosk", 2: "t_l1_sinq", 3: "t_l1_sink"}
            for h in range(H):
                b = h % 2
                k.wt("gpsimd", "cT", wuse[b])
                t1 = k.dma("gpsimd", out=wqk[:, b, :, 0:128],
                           in_=D["L1_attn_wq"].rearrange("(c p) n -> p c n", p=128)[:, :, h * 128:(h + 1) * 128])
                t2 = k.dma("gpsimd", out=wqk[:, b, :, 128:256],
                           in_=D["L1_attn_wk"].rearrange("(c p) n -> p c n", p=128)[:, :, h * 128:(h + 1) * 128])
                t3 = k.dma("gpsimd", out=wv[:, b],
                           in_=D["L1_attn_wv"].rearrange("(c p) n -> p c n", p=128)[:, :, h * 256:(h + 1) * 256])
                k.wt("sync", "cV", tbl_rd)
                tt = [k.dma("sync", out=tblh[:, :, j, :],
                            in_=D[tsrc[j]][h].rearrange("(t p) g -> p t g", p=128))
                      for j in range(4)]
                k.wtd("tensor", [t1, t2, t3])
                k.wtd("vector", tt)
                for tc in range(nt):
                    tb = tc % 2
                    r0, r1_ = tc * 128, (tc + 1) * 128
                    k.wt("tensor", "cV", psq_free[tb])
                    vm = None
                    for c in range(8):
                        vm = k.op("tensor", "matmul", out=psq[tb][:],
                                  lhsT=xnT[:, c, r0:r1_], rhs=wqk[:, b, c, :],
                                  start=(c == 0), stop=(c == 7))
                    k.wt("tensor", "cS", psv_free[tb])
                    vm2 = None
                    for c in range(8):
                        vm2 = k.op("tensor", "matmul", out=psv[tb][:],
                                   lhsT=xnT[:, c, r0:r1_], rhs=wv[:, b, c, :],
                                   start=(c == 0), stop=(c == 7))
                    wuse[b] = vm2
                    k.wt("vector", "cT", vm)
                    k.wt("vector", "cV", qk_free[tb])
                    vcp = k.op("vector", "tensor_copy", out=qkf[:, tb, :], in_=psq[tb][:])
                    psq_free[tb] = vcp
                    vr = rope_pairs(k, qkf[:, tb, :].rearrange("p (h2 g two) -> p h2 g two",
                                                               h2=2, two=2),
                                    tblh[:, tc, 0:2, :], tblh[:, tc, 2:4, :],
                                    scrE, scrO, scrE2, scrO2)
                    tbl_rd = vr
                    k.wtd("vector", st_qk[tb])
                    k.wtd("vector", st_qk2[tb])
                    vqk = k.op("vector", "tensor_copy", out=qkb[:, tb, :], in_=qkf[:, tb, :])
                    qk_free[tb] = vqk
                    k.wt("sync", "cV", vqk)
                    st_qk[tb] = k.dma("sync", out=hscr[r0:r1_, DM + h * 128:DM + (h + 1) * 128],
                                      in_=qkb[:, tb, 0:128])
                    st_qk2[tb] = k.dma("sync", out=hscr2[r0:r1_, h * 128:(h + 1) * 128],
                                       in_=qkb[:, tb, 128:256])
                    k.wt("scalar", "cT", vm2)
                    k.wtd("scalar", st_v[tb])
                    vv = k.op("scalar", "copy", out=vbf[:, tb, :], in_=psv[tb][:],
                              selfwait=False)
                    psv_free[tb] = vv
                    k.wt("sync", "cS", vv)
                    st_v[tb] = k.dma("sync", out=vb[r0:r1_, h * 256:(h + 1) * 256], in_=vbf[:, tb, :])
            k.wtd("sync", [x for d in (st_qk, st_qk2, st_v) for x in d.values() if x])

        with contextlib.ExitStack() as s3:
            k.fence()
            wg_sb = s3.enter_context(nc.sbuf_tensor("r_wg", [128, 2, 8, 256], BF16))
            wo_sb = s3.enter_context(nc.sbuf_tensor("r_wo", [128, 2, 2, DM], BF16))
            kTp = s3.enter_context(nc.sbuf_tensor("r_kT", [128, 2, seq], BF16))
            qTp = s3.enter_context(nc.sbuf_tensor("r_qT", [128, 2, 2, 512], BF16))
            v_sb = s3.enter_context(nc.sbuf_tensor("r_v", [128, 2, nt, 256], BF16))
            ssb = s3.enter_context(nc.sbuf_tensor("r_ssb", [128, 2, 512], BF16))
            osb = s3.enter_context(nc.sbuf_tensor("r_osb", [128, 4, DM], BF16))
            dsb = s3.enter_context(nc.sbuf_tensor("r_dsb", [128, 4, 4], F32))
            gsb = s3.enter_context(nc.sbuf_tensor("r_gsb", [128, 2, 8, 256], BF16))
            densb = s3.enter_context(nc.sbuf_tensor("r_den", [128, 8], F32))
            dtmp = s3.enter_context(nc.sbuf_tensor("r_dtmp", [128, 8, 4], F32))
            sqs = s3.enter_context(nc.sbuf_tensor("r_sqs", [128, 1024], F32))
            ysub = s3.enter_context(nc.sbuf_tensor("r_ysub", [128, 256], BF16))
            yT = s3.enter_context(nc.sbuf_tensor("r_yT", [128, 256], BF16))
            acc = s3.enter_context(nc.sbuf_tensor("r_acc", [128, 8, DM], F32))
            xr = s3.enter_context(nc.sbuf_tensor("r_xr", [128, 2, DM], F32))
            st2 = s3.enter_context(nc.psum_tensor("r_st2", [128, 1024], F32))
            denp = s3.enter_context(nc.psum_tensor("r_denp", [128, 512], F32))
            outp = s3.enter_context(nc.psum_tensor("r_outp", [128, 1024], F32))
            ptp = s3.enter_context(nc.psum_tensor("r_ptp", [128, 256], BF16))
            wop = s3.enter_context(nc.psum_tensor("r_wop", [128, 1024], F32))

            items = [(pp, h) for pp in range(nq // 2) for h in range(H)]
            NI = len(items)
            ld = {0: None, 1: None}
            free_w = {0: 0, 1: 0}
            free_kq = {0: 0, 1: 0}
            free_v = {0: 0, 1: 0}
            slot_free = [0, 0, 0, 0]
            wgr = D["L1_attn_wg"].rearrange("(c p) n -> p c n", p=128)

            def emit_loads(i):
                pp, h = items[i]
                b = i % 2
                ntk_hi = 8 * pp + 8
                k.wt("gpsimd", "cT", free_w[b])
                t_wg = k.dma("gpsimd", out=wg_sb[:, b], in_=wgr[:, :, h * 256:(h + 1) * 256])
                t_wo = k.dma("gpsimd", out=wo_sb[:, b],
                             in_=D["L1_attn_wo"][h * 256:(h + 1) * 256].rearrange("(c p) n -> p c n", p=128))
                k.wt("sync", "cT", max(free_kq[b], free_v[b]))
                t_k = k.dma("sync", out=kTp[:, b, 0:ntk_hi * 128],
                            in_=hscr2[0:ntk_hi * 128, h * 128:(h + 1) * 128], transpose=True)
                qts = [t_k]
                for qi in range(2):
                    qc = 2 * pp + qi
                    qts.append(k.dma("sync", out=qTp[:, b, qi],
                                     in_=hscr[qc * 512:(qc + 1) * 512, DM + h * 128:DM + (h + 1) * 128],
                                     transpose=True))
                t_v = k.dma("sync", out=v_sb[:, b, 0:ntk_hi, :],
                            in_=vb[0:ntk_hi * 128, h * 256:(h + 1) * 256].rearrange(
                                "(t p) d -> p t d", p=128))
                ld[b] = {"w": [t_wg, t_wo], "kq": qts, "v": t_v}

            gsb_free = {0: 0, 1: 0}
            ev_done = {0: ("cS", 0), 1: ("cS", 0)}
            pv_done = {0: 0, 1: 0}
            outp_free = 0
            den_free = {0: 0, 1: 0}
            pcount = 0
            st_x = {0: None, 1: None}

            def emit_A(i):
                nonlocal outp_free, pcount
                pp, h = items[i]
                b = i % 2
                k.wtd("tensor", ld[b]["kq"])
                k.wtd("tensor", ld[b]["v"])
                k.wtd("tensor", ld[b]["w"])
                gp_units = [(qi, s) for qi in range(2) for s in range(4)]
                gpi = [0]

                def emit_gp():
                    nonlocal pcount
                    if gpi[0] >= len(gp_units):
                        return
                    qi, s = gp_units[gpi[0]]
                    u = qi * 4 + s
                    gpi[0] += 1
                    qc = 2 * pp + qi
                    p = pcount % 2
                    pcount += 1
                    sem, val = ev_done[p]
                    if val:
                        k.wt("tensor", sem, val)
                    vgp = None
                    for c in range(8):
                        vgp = k.op("tensor", "matmul", out=st2[:, p * 512:p * 512 + 256],
                                   lhsT=xnT[:, c, qc * 512 + s * 128:qc * 512 + (s + 1) * 128],
                                   rhs=wg_sb[:, b, c, :], start=(c == 0), stop=(c == 7),
                                   skip_group_check=True)
                    k.wt("scalar", "cT", vgp)
                    k.wt("scalar", "cV", gsb_free[b])
                    vgs = k.op("scalar", "activation", out=gsb[:, b, u, :],
                               in_=st2[:, p * 512:p * 512 + 256], func=AF.Silu,
                               selfwait=False)
                    ev_done[p] = ("cS", vgs)

                for qi in range(2):
                    qc = 2 * pp + qi
                    slot = b * 2 + qi
                    keep = [kt for kt in range(4 * qc + 4) if bsc[h, qc, kt] >= 1e-25]
                    first_den = [True]
                    fbank = {0: True, 1: True}
                    vpv = [0]
                    pend = None

                    def emit_pvden(pd, qi=qi):
                        p2, kt2, gsem2, vev2 = pd
                        k.wt("tensor", gsem2, vev2)
                        if fbank[0]:
                            k.wt("tensor", "cS", outp_free)
                            k.wt("tensor", "cS", den_free[qi])
                        for s in range(4):
                            k.op("tensor", "matmul", out=denp[:, qi * 4 + s:qi * 4 + s + 1],
                                 lhsT=ssb[:, p2, s * 128:(s + 1) * 128], rhs=ones_col[:],
                                 start=first_den[0], stop=False, skip_group_check=True)
                            first_den[0] = False
                            vpv[0] = k.op("tensor", "matmul", out=outp[:, s * 256:(s + 1) * 256],
                                          lhsT=ssb[:, p2, s * 128:(s + 1) * 128],
                                          rhs=v_sb[:, b, kt2, :],
                                          start=fbank[s // 2], stop=False, skip_group_check=True)
                            fbank[s // 2] = False
                        pv_done[p2] = vpv[0]

                    for bi, kt in enumerate(keep):
                        p = pcount % 2
                        pcount += 1
                        sl = st2[:, p * 512:(p + 1) * 512]
                        sem, val = ev_done[p]
                        if val:
                            k.wt("tensor", sem, val)
                        vmm = k.op("tensor", "matmul", out=sl,
                                   lhsT=kTp[:, b, kt * 128:(kt + 1) * 128],
                                   rhs=qTp[:, b, qi], start=True, stop=True)
                        scale = float(bsc[h, qc, kt])
                        e = kt - 4 * qc
                        if e >= 0:
                            k.wt("vector", "cT", vmm)
                            k.wt("vector", "cT", pv_done[p])
                            vev = k.op("vector", "scalar_tensor_tensor", out=ssb[:, p, :], in0=sl,
                                       scalar=scale, in1=trmask[:, e, :], op0=ALU.mult, op1=ALU.mult,
                                       selfwait=False)
                            ev_done[p] = ("cV", vev)
                            gsem = "cV"
                        else:
                            k.wt("scalar", "cT", vmm)
                            k.wt("scalar", "cT", pv_done[p])
                            vev = k.op("scalar", "activation", out=ssb[:, p, :], in_=sl,
                                       func=AF.Copy, scale=scale, selfwait=False)
                            ev_done[p] = ("cS", vev)
                            gsem = "cS"
                        if pend is not None:
                            emit_pvden(pend)
                        pend = (p, kt, gsem, vev)
                        if bi % 3 == 2:
                            emit_pvden(pend)
                            pend = None
                            emit_gp()
                    if pend is not None:
                        emit_pvden(pend)
                    while gpi[0] < (qi + 1) * 4:
                        emit_gp()
                    k.wt("scalar", "cT", vpv[0])
                    k.wt("scalar", "cV", slot_free[slot])
                    k.op("scalar", "activation", out=dsb[:, slot, :], in_=denp[:, qi * 4:qi * 4 + 4],
                         func=AF.Abs)
                    den_free[qi] = k.sems["cS"].n
                    outp_free = k.op("scalar", "copy", out=osb[:, slot, :], in_=outp[:])
                free_kq[b] = k.sems["cT"].n
                free_v[b] = k.sems["cT"].n
                return outp_free

            acc_rd = 0

            def emit_B(j, ready_cs):
                pp, h = items[j]
                bj = j % 2
                vwo_last = 0
                for qi in range(2):
                    qc = 2 * pp + qi
                    slot = bj * 2 + qi
                    k.wt("vector", "cS", ready_cs)
                    vcl = k.op("vector", "tensor_scalar", out=densb[:, qi * 4:qi * 4 + 4],
                               in0=dsb[:, slot, :], scalar1=1.0, scalar2=None, op0=ALU.max)
                    vsq = k.op("scalar", "activation", out=sqs[:], in_=osb[:, slot, :],
                               func=AF.Square)
                    k.wt("vector", "cS", vsq)
                    k.op("vector", "tensor_reduce", out=dtmp[:, qi, 0:4],
                         in_=sqs[:].rearrange("p (u d) -> p u d", u=4), axis=AX.X, op=ALU.add)
                    k.op("vector", "tensor_scalar", out=dtmp[:, qi, 0:4], in0=dtmp[:, qi, 0:4],
                         scalar1=1.0 / 256, scalar2=None, op0=ALU.mult)
                    k.op("vector", "tensor_tensor", out=dtmp[:, qi + 2, 0:4],
                         in0=densb[:, qi * 4:qi * 4 + 4], in1=densb[:, qi * 4:qi * 4 + 4],
                         op=ALU.mult)
                    vd2 = k.op("vector", "scalar_tensor_tensor", out=dtmp[:, qi + 4, 0:4],
                               in0=dtmp[:, qi + 2, 0:4], scalar=EPS, in1=dtmp[:, qi, 0:4],
                               op0=ALU.mult, op1=ALU.add)
                    k.wt("scalar", "cV", vd2)
                    vln = k.op("scalar", "activation", out=dtmp[:, qi + 6, 0:4],
                               in_=dtmp[:, qi + 4, 0:4], func=AF.Ln)
                    k.wt("scalar", "cS", vln)
                    vrs = k.op("scalar", "activation", out=dtmp[:, qi + 6, 0:4],
                               in_=dtmp[:, qi + 6, 0:4], func=AF.Exp, scale=-0.5)
                    k.wt("vector", "cS", vrs)
                    vy_slot = 0
                    for s in range(4):
                        u = qi * 4 + s
                        t = qi * 4 + s
                        vy = k.op("vector", "scalar_tensor_tensor", out=ysub[:],
                                  in0=osb[:, slot, s * 256:(s + 1) * 256],
                                  scalar=dtmp[:, qi + 6, s:s + 1],
                                  in1=gsb[:, bj, u, :], op0=ALU.mult, op1=ALU.mult)
                        vy_slot = vy
                        vevt = None
                        for cc in range(2):
                            k.wt("tensor", "cV", vy if cc == 0 else vevt)
                            vtp = k.op("tensor", "transpose", out=ptp[:, cc * 128:(cc + 1) * 128],
                                       in_=ysub[:, cc * 128:(cc + 1) * 128], identity=ident[:])
                            k.wt("vector", "cT", vtp)
                            vevt = k.op("vector", "tensor_copy", out=yT[:, cc * 128:(cc + 1) * 128],
                                        in_=ptp[:, cc * 128:(cc + 1) * 128])
                        k.wt("tensor", "cV", vevt)
                        if h == 0:
                            k.wt("tensor", "cV", acc_rd)
                        vmo = None
                        for cc in range(2):
                            for n0 in (0, 512):
                                vmo = k.op("tensor", "matmul", out=wop[:, n0:n0 + 512],
                                           lhsT=yT[:, cc * 128:(cc + 1) * 128],
                                           rhs=wo_sb[:, bj, cc, n0:n0 + 512],
                                           start=(cc == 0), stop=(cc == 1))
                        vwo_last = vmo
                        k.wt("vector", "cT", vmo)
                        if h == 0:
                            k.op("vector", "tensor_copy", out=acc[:, t, :], in_=wop[:])
                        else:
                            k.op("vector", "tensor_tensor", out=acc[:, t, :], in0=acc[:, t, :],
                                 in1=wop[:], op=ALU.add)
                    slot_free[slot] = max(vy_slot, vcl)
                    gsb_free[bj] = max(gsb_free[bj], vy_slot)
                free_w[bj] = vwo_last

            def emit_final(pp):
                nonlocal acc_rd
                for t in range(8):
                    fb = t % 2
                    qc = 2 * pp + t // 4
                    s = t % 4
                    k.wtd("sync", st_x[fb])
                    vxl = k.dma("sync", out=xr[:, fb, :],
                                in_=xin[qc * 512 + s * 128:qc * 512 + (s + 1) * 128, :])
                    k.wtd("vector", vxl)
                    k.wtd("vector", st_x[fb])
                    vadd = k.op("vector", "tensor_tensor", out=xr[:, fb, :], in0=acc[:, t, :],
                                in1=xr[:, fb, :], op=ALU.add)
                    acc_rd = vadd
                    k.wt("sync", "cV", vadd)
                    st_x[fb] = k.dma("sync", out=xout[qc * 512 + s * 128:qc * 512 + (s + 1) * 128, :],
                                     in_=xr[:, fb, :])

            emit_loads(0)
            prev_ready = None
            for i in range(NI):
                ready = emit_A(i)
                if i > 0:
                    emit_B(i - 1, prev_ready)
                    if items[i - 1][1] == H - 1:
                        emit_final(items[i - 1][0])
                prev_ready = ready
                if i + 1 < NI:
                    emit_loads(i + 1)
            emit_B(NI - 1, prev_ready)
            emit_final(items[NI - 1][0])
        k.fence()


WSPECS_L1A = [
    ("L1_attn_wq", (DM, DM), F32),
    ("L1_attn_wk", (DM, DM), F32),
    ("L1_attn_wv", (DM, 2048), F32),
    ("L1_attn_wg", (DM, 2048), F32),
    ("L1_attn_wo", (2048, DM), F32),
]


def tspecs_l1a(seq):
    return [
        ("t_ident", (128, 128), BF16),
        ("t_trmask", (4, 128, 512), BF16),
        ("t_l1_cosq", (8, seq, 64), F32),
        ("t_l1_sinq", (8, seq, 64), F32),
        ("t_l1_cosk", (8, seq, 64), F32),
        ("t_l1_sink", (8, seq, 64), F32),
    ]


def build_l1a(seq):
    nc = bass.Bass("TRN2", target_bir_lowering=False)
    es = contextlib.ExitStack()
    with es:
        k = K(nc, es, seq)
        D = k.dram
        D["xin"] = nc.dram_tensor("xin", [seq, DM], F32, kind="ExternalInput").ap()
        for nm, shp, dt in WSPECS_L1A + tspecs_l1a(seq):
            D[nm] = nc.dram_tensor(nm, list(shp), dt, kind="ExternalInput").ap()
        D["hscr"] = nc.dram_tensor("hscr", [seq, 2048], BF16, kind="Internal").ap()
        D["hscr2"] = nc.dram_tensor("hscr2", [seq, 1024], BF16, kind="Internal").ap()
        D["vb"] = nc.dram_tensor("vb", [seq, 2048], BF16, kind="Internal").ap()
        D["xout"] = nc.dram_tensor("xout", [seq, DM], F32, kind="ExternalOutput").ap()
        emit_l1a(k, D["xin"], D["xout"])
        k.build()
    return nc



def emit_l1f(k, xin, xout):
    nc = k.nc
    seq, nt, nq = k.seq, k.nt, k.nq
    D = k.dram
    NL = 16
    hscr, qTb, vb = D["hscr"], D["qTb"], D["vb"]

    with contextlib.ExitStack() as ph:
        k.fence()
        ident = ph.enter_context(nc.sbuf_tensor("e_ident", [128, 128], BF16))
        k.dma("sync", out=ident[:], in_=D["t_ident"][:, :])
        stats = ph.enter_context(nc.sbuf_tensor("e_stats", [128, 12], F32))
        mvst = ph.enter_context(nc.sbuf_tensor("e_mv", [128, 4], F32))
        kT = ph.enter_context(nc.sbuf_tensor("e_kT", [128, 8, seq], BF16))

        with contextlib.ExitStack() as sff:
            xnT = sff.enter_context(nc.sbuf_tensor("e_xnT", [128, 8, seq], BF16))

            with contextlib.ExitStack() as s1:
                k.fence()
                x_sb = s1.enter_context(nc.sbuf_tensor("e_x", [128, 2, DM], F32))
                xn = s1.enter_context(nc.sbuf_tensor("e_xn", [128, DM], BF16))
                last_st = None
                prev = {0: 0, 1: 0}
                for tc in range(nt):
                    b = tc % 2
                    k.wt("sync", "cV", prev[b])
                    vl = k.dma("sync", out=x_sb[:, b, :], in_=xin[tc * 128:(tc + 1) * 128, :])
                    k.wtd("vector", vl)
                    k.wtd("vector", last_st)
                    vn = layer_norm_tile(k, x_sb[:, b, :], xn[:], stats, mvst)
                    prev[b] = vn
                    k.wt("sync", "cV", vn)
                    last_st = k.dma("sync", out=hscr[tc * 128:(tc + 1) * 128, 0:DM], in_=xn[:])
                k.wtd("sync", last_st)
                vxt = []
                for c in range(8):
                    vxt.append(k.dma("sync", out=xnT[:, c, :], in_=hscr[:, c * 128:(c + 1) * 128],
                                     transpose=True))
                k.wtd("tensor", vxt)

            for pref in ("q", "v", "k"):
                with contextlib.ExitStack() as s2:
                    k.fence()
                    nw = s2.enter_context(nc.sbuf_tensor("e_nw_" + pref, [15, DM], BF16))
                    nwT = s2.enter_context(nc.sbuf_tensor("e_nwT_" + pref, [128, 8, 16], BF16))
                    w1 = s2.enter_context(nc.sbuf_tensor("e_w1_" + pref, [128, 8, NL, 64], BF16))
                    w2 = s2.enter_context(nc.sbuf_tensor("e_w2_" + pref, [128, 8, DM], BF16))
                    b2 = s2.enter_context(nc.sbuf_tensor("e_b2_" + pref, [16, DM], BF16))
                    selT = s2.enter_context(nc.sbuf_tensor("e_selT_" + pref, [16, seq], BF16))
                    nbb, _ = bcast_row(k, "e_nb_" + pref, D[f"L1_ffn_{pref}_node_b"][None, :], 15, s2)
                    b1b, _ = bcast_row(k, "e_b1_" + pref,
                                       D[f"L1_ffn_{pref}_b1"].rearrange("l h -> (l h)")[None, :], DM, s2)
                    iot, viot = bcast_row(k, "e_iota_" + pref, D["t_iota16"][None, :], 16, s2)
                    v1 = k.dma("gpsimd", out=nw[:], in_=D[f"L1_ffn_{pref}_node_w"][:, :])
                    v2 = []
                    w1r = D[f"L1_ffn_{pref}_w1"].rearrange("l (c p) h -> p c l h", p=128)
                    for c in range(8):
                        v2.append(k.dma("gpsimd", out=w1[:, c], in_=w1r[:, c]))
                    v3 = k.dma("gpsimd", out=w2[:],
                               in_=D[f"L1_ffn_{pref}_w2"].rearrange("l h n -> (l h) n")
                               .rearrange("(c p) n -> p c n", p=128))
                    v4 = k.dma("gpsimd", out=b2[:], in_=D[f"L1_ffn_{pref}_b2"][:, :])
                    psn = {j: s2.enter_context(nc.psum_tensor(f"e_psn{j}_" + pref, [128, 16], F32))
                           for j in range(2)}
                    psh = s2.enter_context(nc.psum_tensor("e_psh_" + pref, [128, 1024], F32))
                    pso = {j: s2.enter_context(nc.psum_tensor(f"e_pso{j}_" + pref, [128, 512], F32))
                           for j in range(2)}
                    ptp = s2.enter_context(nc.psum_tensor("e_ptp_" + pref, [128, 1024], BF16))
                    ptq = s2.enter_context(nc.psum_tensor("e_ptq_" + pref, [16, 128], BF16))
                    sc_sb = s2.enter_context(nc.sbuf_tensor("e_sc_" + pref, [128, 16], F32))
                    tr_sb = s2.enter_context(nc.sbuf_tensor("e_tr_" + pref, [128, 12], F32))
                    sel = s2.enter_context(nc.sbuf_tensor("e_sel_" + pref, [128, 16], F32))
                    selb = s2.enter_context(nc.sbuf_tensor("e_selb_" + pref, [128, 16], BF16))
                    h1 = s2.enter_context(nc.sbuf_tensor("e_h1_" + pref, [128, DM], F32))
                    hb = s2.enter_context(nc.sbuf_tensor("e_hb_" + pref, [128, DM], BF16))
                    hT = s2.enter_context(nc.sbuf_tensor("e_hT_" + pref, [128, 8, 512], BF16))
                    ob = s2.enter_context(nc.sbuf_tensor("e_ob_" + pref, [128, 2, 512], BF16))
                    k.wtd("tensor", [v1, v3, v4] + v2)
                    k.wt("tensor", "cV", viot)
                    vnt = 0
                    for c in range(8):
                        k.wt("tensor", "cV", vnt)
                        vtp = k.op("tensor", "transpose", out=ptp[:, 0:15],
                                   in_=nw[:, c * 128:(c + 1) * 128], identity=ident[0:15, 0:15])
                        k.wt("vector", "cT", vtp)
                        vnt = k.op("vector", "tensor_copy", out=nwT[:, c, 0:15], in_=ptp[:, 0:15])
                    k.wt("tensor", "cV", vnt)

                    st_sel = 0
                    vhb_used = 0
                    vh1_used = 0
                    vpsn_free = {0: 0, 1: 0}
                    vob_st = {0: None, 1: None}
                    pso_free = {0: 0, 1: 0}
                    vsel_used = 0
                    vhT_used = 0
                    for ch in range(seq // 512):
                        for ti in range(4):
                            tc = ch * 4 + ti
                            r0, r1_ = tc * 128, (tc + 1) * 128
                            pslot = tc % 2
                            k.wt("tensor", "cV", vpsn_free[pslot])
                            vm = None
                            for c in range(8):
                                vm = k.op("tensor", "matmul", out=psn[pslot][:, 0:15],
                                          lhsT=xnT[:, c, r0:r1_], rhs=nwT[:, c, 0:15],
                                          start=(c == 0), stop=(c == 7))
                            k.wt("tensor", "cV", vh1_used)
                            vm2 = None
                            for half in range(2):
                                for c in range(8):
                                    vm2 = k.op("tensor", "matmul", out=psh[:, half * 512:(half + 1) * 512],
                                               lhsT=xnT[:, c, r0:r1_],
                                               rhs=w1[:, c, half * 8:(half + 1) * 8, :].rearrange(
                                                   "p l h -> p (l h)"),
                                               start=(c == 0), stop=(c == 7))
                            k.wt("vector", "cT", vm)
                            vpsn_free[pslot] = k.op("vector", "tensor_tensor", out=sc_sb[:, 0:15],
                                                    in0=psn[pslot][:, 0:15],
                                                    in1=nbb[:, 0:15], op=ALU.add)
                            k.wt("vector", "cT", vm2)
                            vh1 = k.op("vector", "tensor_tensor", out=h1[:], in0=psh[:], in1=b1b[:],
                                       op=ALU.add)
                            vh1_used = vh1
                            k.op("vector", "tensor_scalar", out=tr_sb[:, 0:1], in0=sc_sb[:, 0:1],
                                 scalar1=0.0, scalar2=None, op0=ALU.is_ge)
                            k.op("vector", "tensor_tensor", out=tr_sb[:, 7:8], in0=sc_sb[:, 2:3],
                                 in1=sc_sb[:, 1:2], op=ALU.subtract)
                            k.op("vector", "tensor_tensor", out=tr_sb[:, 8:9], in0=tr_sb[:, 0:1],
                                 in1=tr_sb[:, 7:8], op=ALU.mult)
                            k.op("vector", "tensor_tensor", out=tr_sb[:, 9:10], in0=sc_sb[:, 1:2],
                                 in1=tr_sb[:, 8:9], op=ALU.add)
                            k.op("vector", "tensor_scalar", out=tr_sb[:, 1:2], in0=tr_sb[:, 9:10],
                                 scalar1=0.0, scalar2=None, op0=ALU.is_ge)
                            k.op("vector", "scalar_tensor_tensor", out=tr_sb[:, 4:5], in0=tr_sb[:, 0:1],
                                 scalar=2.0, in1=tr_sb[:, 1:2], op0=ALU.mult, op1=ALU.add)
                            bc4 = tr_sb[:, 4:5].broadcast_to([128, 4])
                            k.op("vector", "tensor_tensor", out=tr_sb[:, 7:11], in0=bc4,
                                 in1=iot[:, 0:4], op=ALU.is_equal)
                            k.op("vector", "tensor_tensor", out=tr_sb[:, 7:11],
                                 in0=tr_sb[:, 7:11], in1=sc_sb[:, 3:7], op=ALU.mult)
                            k.op("vector", "tensor_reduce", out=tr_sb[:, 9:10],
                                 in_=tr_sb[:, 7:11], axis=AX.X, op=ALU.add)
                            k.op("vector", "tensor_scalar", out=tr_sb[:, 2:3], in0=tr_sb[:, 9:10],
                                 scalar1=0.0, scalar2=None, op0=ALU.is_ge)
                            k.op("vector", "scalar_tensor_tensor", out=tr_sb[:, 5:6], in0=tr_sb[:, 4:5],
                                 scalar=2.0, in1=tr_sb[:, 2:3], op0=ALU.mult, op1=ALU.add)
                            k.wt("vector", "cT", vsel_used)
                            bc8 = tr_sb[:, 5:6].broadcast_to([128, 8])
                            k.op("vector", "tensor_tensor", out=sel[:, 0:8], in0=bc8,
                                 in1=iot[:, 0:8], op=ALU.is_equal)
                            k.op("vector", "tensor_tensor", out=sel[:, 8:16],
                                 in0=sel[:, 0:8], in1=sc_sb[:, 7:15], op=ALU.mult)
                            k.op("vector", "tensor_reduce", out=tr_sb[:, 9:10],
                                 in_=sel[:, 8:16], axis=AX.X, op=ALU.add)
                            k.op("vector", "tensor_scalar", out=tr_sb[:, 3:4], in0=tr_sb[:, 9:10],
                                 scalar1=0.0, scalar2=None, op0=ALU.is_ge)
                            k.op("vector", "scalar_tensor_tensor", out=tr_sb[:, 6:7], in0=tr_sb[:, 5:6],
                                 scalar=2.0, in1=tr_sb[:, 3:4], op0=ALU.mult, op1=ALU.add)
                            bc16 = tr_sb[:, 6:7].broadcast_to([128, 16])
                            vsel = k.op("vector", "tensor_tensor", out=sel[:], in0=bc16,
                                        in1=iot[:], op=ALU.is_equal)
                            vselb = k.op("vector", "tensor_copy", out=selb[:], in_=sel[:])
                            k.wt("tensor", "cV", vselb)
                            vtq = k.op("tensor", "transpose", out=ptq[:],
                                       in_=selb[:], identity=ident[:, :])
                            vsel_used = vtq
                            k.wt("vector", "cT", vtq)
                            k.op("vector", "tensor_copy", out=selT[:, r0:r1_], in_=ptq[:])
                            k.wt("vector", "cT", vhb_used)
                            vhb = None
                            for l in range(NL):
                                vhb = k.op("vector", "tensor_scalar", out=hb[:, l * 64:(l + 1) * 64],
                                           in0=h1[:, l * 64:(l + 1) * 64], scalar1=sel[:, l:l + 1],
                                           scalar2=0.0, op0=ALU.mult, op1=ALU.max,
                                           selfwait=(l == NL - 1))
                            vev = None
                            for c in range(8):
                                k.wt("tensor", "cV", vhb if c == 0 else vev)
                                k.wt("tensor", "cT", vhT_used) if c == 0 else None
                                vtp = k.op("tensor", "transpose",
                                           out=ptp[:, c * 128:(c + 1) * 128],
                                           in_=hb[:, c * 128:(c + 1) * 128], identity=ident[:])
                                vhb_used = vtp
                                k.wt("vector", "cT", vtp)
                                vev = k.op("vector", "tensor_copy",
                                           out=hT[:, c, ti * 128:(ti + 1) * 128],
                                           in_=ptp[:, c * 128:(c + 1) * 128])
                            k.wt("tensor", "cV", vev)
                        ch0 = ch * 512
                        if pref == "v":
                            for ti in range(4):
                                r0 = ch0 + ti * 128
                                for n0 in (0, 512):
                                    oslot = (ti * 2 + n0 // 512) % 2
                                    k.wt("tensor", "cV", pso_free[oslot])
                                    vmo = None
                                    for c in range(8):
                                        vmo = k.op("tensor", "matmul", out=pso[oslot][:],
                                                   lhsT=hT[:, c, ti * 128:(ti + 1) * 128],
                                                   rhs=w2[:, c, n0:n0 + 512],
                                                   start=(c == 0), stop=False)
                                    vmo = k.op("tensor", "matmul", out=pso[oslot][:],
                                               lhsT=selT[:, r0:r0 + 128],
                                               rhs=b2[:, n0:n0 + 512], start=False, stop=True)
                                    vhT_used = vmo
                                    k.wt("vector", "cT", vmo)
                                    k.wtd("vector", vob_st[oslot])
                                    vo = k.op("vector", "tensor_copy", out=ob[:, oslot, :],
                                              in_=pso[oslot][:], selfwait=False)
                                    pso_free[oslot] = vo
                                    k.wt("sync", "cV", vo)
                                    vob_st[oslot] = k.dma("sync", out=vb[r0:r0 + 128, n0:n0 + 512],
                                                          in_=ob[:, oslot, :])
                        else:
                            for oc in range(8):
                                oslot = oc % 2
                                k.wt("tensor", "cV", pso_free[oslot])
                                vmo = None
                                for c in range(8):
                                    vmo = k.op("tensor", "matmul", out=pso[oslot][:],
                                               lhsT=w2[:, c, oc * 128:(oc + 1) * 128],
                                               rhs=hT[:, c, :], start=(c == 0), stop=False)
                                vmo = k.op("tensor", "matmul", out=pso[oslot][:],
                                           lhsT=b2[:, oc * 128:(oc + 1) * 128],
                                           rhs=selT[:, ch0:ch0 + 512], start=False, stop=True)
                                vhT_used = vmo
                                k.wt("vector", "cT", vmo)
                                k.wtd("vector", vob_st[oslot])
                                if pref == "q":
                                    vo = k.op("vector", "tensor_scalar", out=ob[:, oslot, :],
                                              in0=pso[oslot][:], scalar1=1.0 / 32.0, scalar2=None,
                                              op0=ALU.mult, selfwait=False)
                                    pso_free[oslot] = vo
                                    k.wt("sync", "cV", vo)
                                    vob_st[oslot] = k.dma("sync", out=qTb[:, oc, ch0:ch0 + 512],
                                                          in_=ob[:, oslot, :])
                                else:
                                    vo = k.op("vector", "tensor_copy", out=kT[:, oc, ch0:ch0 + 512],
                                              in_=pso[oslot][:], selfwait=False)
                                    pso_free[oslot] = vo
                                    vob_st[oslot] = None
                    k.fence()

        with contextlib.ExitStack() as s3:
            k.fence()
            v_sb = s3.enter_context(nc.sbuf_tensor("e_v", [128, nt, 1032], BF16))
            vzero = k.op("vector", "memset", ap=v_sb[:, :, 1024:1032], constant=0.0)
            k.op("vector", "memset", ap=v_sb[:, :, 1024:1025], constant=1.0)
            vvl = k.dma("sync", out=v_sb[:, :, 0:1024], in_=vb.rearrange("(t p) n -> p t n", p=128))
            qTc = s3.enter_context(nc.sbuf_tensor("e_qTc", [128, 2, 8, 256], BF16))
            pts = s3.enter_context(nc.sbuf_tensor("e_pts", [128, 3, 256], BF16))
            mbias = s3.enter_context(nc.sbuf_tensor("e_mbias", [128, 1], F32))
            vmb = k.op("vector", "memset", ap=mbias[:], constant=-12.0)
            rv = s3.enter_context(nc.sbuf_tensor("e_rv", [128, 2], F32))
            xr = s3.enter_context(nc.sbuf_tensor("e_xr", [128, 2, DM], F32))
            xnew = s3.enter_context(nc.sbuf_tensor("e_xnew", [128, 2, DM], F32))
            sc = {j: s3.enter_context(nc.psum_tensor(f"e_sc{j}", [128, 256], F32))
                  for j in range(2)}
            y = {j: s3.enter_context(nc.psum_tensor(f"e_y{j}", [128, 1032], F32))
                 for j in range(2)}
            k.wtd("tensor", vvl)
            k.wt("scalar", "cV", vmb)
            k.wt("tensor", "cV", vzero)
            scfree = [None, None]
            ptsfree = [None, None, None]
            ptsi = [0]
            yfree = [None, None]
            qld = {0: None, 1: None}
            qfree = {0: 0, 1: 0}
            st_x = {0: None, 1: None}
            xprev = {0: 0, 1: 0}
            NCH = seq // 256
            for cc in range(NCH):
                b = cc % 2
                q0 = cc * 256
                k.wt("sync", "cT", qfree[b])
                qld[b] = k.dma("sync", out=qTc[:, b], in_=qTb[:, :, q0:q0 + 256])
                k.wtd("tensor", qld[b])

                def emit_pv(p):
                    ps2, kt2, gate2 = p
                    k.wt("tensor", *gate2)
                    vpv = None
                    for qs2 in range(2):
                        if kt2 == 0 and yfree[qs2]:
                            k.wt("tensor", *yfree[qs2])
                        for n0 in (0, 512, 1024):
                            n1 = min(n0 + 512, 1032)
                            vpv = k.op("tensor", "matmul", out=y[qs2][:, n0:n1],
                                       lhsT=pts[:, ps2, qs2 * 128:(qs2 + 1) * 128],
                                       rhs=v_sb[:, kt2, n0:n1],
                                       start=(kt2 == 0), stop=(kt2 == nt - 1),
                                       skip_group_check=True)
                    ptsfree[ps2] = ("cT", vpv)
                    return vpv

                pend = None
                for kt in range(nt):
                    ss = kt % 2
                    if scfree[ss]:
                        k.wt("tensor", *scfree[ss])
                    vmm = None
                    for c in range(8):
                        vmm = k.op("tensor", "matmul", out=sc[ss][:],
                                   lhsT=kT[:, c, kt * 128:(kt + 1) * 128],
                                   rhs=qTc[:, b, c, :], start=(c == 0), stop=(c == 7))
                    qfree[b] = vmm
                    ps = ptsi[0] % 3
                    ptsi[0] += 1
                    if ptsfree[ps]:
                        k.wt("scalar", *ptsfree[ps])
                    k.wt("scalar", "cT", vmm)
                    vexp = k.op("scalar", "activation", out=pts[:, ps, :], in_=sc[ss][:],
                                func=AF.Exp, bias=mbias[:], scale=1.0, selfwait=False)
                    scfree[ss] = ("cS", vexp)
                    if pend is not None:
                        emit_pv(pend)
                    pend = (ps, kt, ("cS", vexp))
                last_pv = emit_pv(pend)
                for qs2 in range(2):
                    tok0 = q0 + qs2 * 128
                    xb = qs2
                    k.wt("sync", "cV", xprev[xb])
                    vxl = k.dma("sync", out=xr[:, xb, :], in_=xin[tok0:tok0 + 128, :])
                    k.wt("vector", "cT", last_pv)
                    k.op("vector", "reciprocal", out=rv[:, qs2:qs2 + 1],
                         in_=y[qs2][:, 1024:1025])
                    k.wtd("vector", vxl)
                    k.wtd("vector", st_x[xb])
                    vfin = k.op("vector", "scalar_tensor_tensor", out=xnew[:, xb, :],
                                in0=y[qs2][:, 0:1024], scalar=rv[:, qs2:qs2 + 1],
                                in1=xr[:, xb, :], op0=ALU.mult, op1=ALU.add)
                    yfree[qs2] = ("cV", vfin)
                    xprev[xb] = vfin
                    k.wt("sync", "cV", vfin)
                    st_x[xb] = k.dma("sync", out=xout[tok0:tok0 + 128, :], in_=xnew[:, xb, :])
        k.fence()


WSPECS_L1F = []
for _p in ("q", "k", "v"):
    WSPECS_L1F += [
        (f"L1_ffn_{_p}_node_w", (15, DM), F32),
        (f"L1_ffn_{_p}_node_b", (15,), F32),
        (f"L1_ffn_{_p}_w1", (16, DM, 64), F32),
        (f"L1_ffn_{_p}_b1", (16, 64), F32),
        (f"L1_ffn_{_p}_w2", (16, 64, DM), F32),
        (f"L1_ffn_{_p}_b2", (16, DM), F32),
    ]


def tspecs_l1f(seq):
    return [
        ("t_ident", (128, 128), BF16),
        ("t_iota16", (16,), F32),
    ]


def build_l1f(seq):
    nc = bass.Bass("TRN2", target_bir_lowering=False)
    es = contextlib.ExitStack()
    with es:
        k = K(nc, es, seq)
        D = k.dram
        D["xin"] = nc.dram_tensor("xin", [seq, DM], F32, kind="ExternalInput").ap()
        for nm, shp, dt in WSPECS_L1F + tspecs_l1f(seq):
            D[nm] = nc.dram_tensor(nm, list(shp), dt, kind="ExternalInput").ap()
        D["hscr"] = nc.dram_tensor("hscr", [seq, 2048], BF16, kind="Internal").ap()
        D["qTb"] = nc.dram_tensor("qTb", [128, 8, seq], BF16, kind="Internal").ap()
        D["vb"] = nc.dram_tensor("vb", [seq, DM], BF16, kind="Internal").ap()
        D["xout"] = nc.dram_tensor("xout", [seq, DM], F32, kind="ExternalOutput").ap()
        emit_l1f(k, D["xin"], D["xout"])
        k.build()
    return nc


def all_tables(seq):
    t = host_tables(seq)
    tab = {
        "t_ident": t["ident"],
        "t_trmask": t["trmask"],
        "t_pvlr": t["pvlr"],
        "t_pvq4": t["pvq4"],
        "t_iota16": np.arange(16, dtype=np.float32),
    }
    qq2 = np.arange(128)[:, None]
    kk2 = np.arange(512)[None, :]
    tab["t_mlamask"] = np.stack([np.where(kk2 <= qq2 + 128 * e, 0.0, -30000.0)
                                 for e in range(4)]).astype(ml_dtypes.bfloat16)
    freqs = (1.0 / 100000.0 ** (np.arange(0, 32, 2, dtype=np.float64) / 32))
    ang = np.outer(np.arange(seq, dtype=np.float64), freqs)
    tab["t_mla_cos"] = np.cos(ang).astype(np.float32)
    tab["t_mla_sin"] = np.sin(ang).astype(np.float32)
    tl1, _ = l1a_host_tables(seq)
    tab.update(tl1)
    return tab


ALL_WSPECS = WSPECS_L0A + WSPECS_L0F + WSPECS_L1A + WSPECS_L1F


def all_tspecs(seq):
    seen = {}
    for nm, shp, dt in (list(TSPECS) + tspecs_l0f(seq) + tspecs_l1a(seq)
                        + tspecs_l1f(seq)):
        seen[nm] = (shp, dt)
    return [(nm, shp, dt) for nm, (shp, dt) in seen.items()]


def build_fused(seq):
    nc = bass.Bass("TRN2", target_bir_lowering=False)
    es = contextlib.ExitStack()
    with es:
        k = K(nc, es, seq)
        D = k.dram
        D["xin"] = nc.dram_tensor("xin", [seq, DM], F32, kind="ExternalInput").ap()
        for nm, shp, dt in ALL_WSPECS + all_tspecs(seq):
            D[nm] = nc.dram_tensor(nm, list(shp), dt, kind="ExternalInput").ap()
        D["hscr"] = nc.dram_tensor("hscr", [seq, 2048], BF16, kind="Internal").ap()
        D["zb"] = nc.dram_tensor("zb", [seq, DM], F32, kind="Internal").ap()
        D["r1b"] = nc.dram_tensor("r1b", [seq, DM], F32, kind="Internal").ap()
        D["macc"] = nc.dram_tensor("macc", [seq, DM], F32, kind="Internal").ap()
        D["hscr2"] = nc.dram_tensor("hscr2", [seq, 1024], BF16, kind="Internal").ap()
        D["vb"] = nc.dram_tensor("vb", [seq, 2048], BF16, kind="Internal").ap()
        vb_l1f = nc.dram_tensor("vb2", [seq, DM], BF16, kind="Internal").ap()
        D["qTb"] = nc.dram_tensor("qTb", [128, 8, seq], BF16, kind="Internal").ap()
        x1 = nc.dram_tensor("x1", [seq, DM], F32, kind="Internal").ap()
        x2 = nc.dram_tensor("x2", [seq, DM], F32, kind="Internal").ap()
        x3 = nc.dram_tensor("x3", [seq, DM], F32, kind="Internal").ap()
        D["xout"] = nc.dram_tensor("xout", [seq, DM], F32, kind="ExternalOutput").ap()
        emit_l0a(k, D["xin"], x1)
        emit_l0f(k, x1, x2)
        emit_l1a(k, x2, x3)
        D["vb"] = vb_l1f
        emit_l1f(k, x3, D["xout"])
        k.build()
    return nc


def kernel(**inputs):
    seq = SEQ_DEFAULT
    tab = all_tables(seq)
    x = np.ascontiguousarray(np.asarray(inputs["L0_attn_x"], dtype=np.float32))
    ncores = x.shape[0]
    nc = build_fused(seq)
    shared = {nm: np.ascontiguousarray(np.asarray(inputs[nm], dtype=np.float32))
              for nm, _, _ in ALL_WSPECS}
    for nm, _, _ in all_tspecs(seq):
        shared[nm] = tab[nm]
    in_maps = []
    for i in range(ncores):
        m = {"xin": np.ascontiguousarray(x[i])}
        m.update(shared)
        in_maps.append(m)
    res = run_bass_kernel_spmd(nc, in_maps, core_ids=list(range(ncores)))
    return np.stack([np.asarray(r["xout"], dtype=np.float32) for r in res.results],
                    axis=0)


SEQ_DEFAULT = 4096



# version 11
# speedup vs baseline: 1.1077x; 1.0264x over previous
import contextlib
import numpy as np
import ml_dtypes
import concourse.bass as bass
import concourse.mybir as mybir
from concourse.bass_utils import run_bass_kernel_spmd

F32 = mybir.dt.float32
BF16 = mybir.dt.bfloat16
AF = mybir.ActivationFunctionType
ALU = mybir.AluOpType
AX = mybir.AxisListType

NCORES = 8
DM = 1024

ENGS = ("tensor", "vector", "scalar", "sync", "gpsimd")
ENGSEM = {"tensor": "cT", "vector": "cV", "scalar": "cS", "sync": "cY", "gpsimd": "cG"}
DMASEM = {"sync": "dS", "gpsimd": "dG", "scalar": "dC", "vector": "dV", "tensor": "dT"}


class Sem:
    __slots__ = ("h", "n", "name")

    def __init__(self, h, name):
        self.h = h
        self.n = 0
        self.name = name

    def plan(self, kk):
        self.n += kk
        return self.n


class K:

    def __init__(self, nc, es, seq):
        self.nc = nc
        self.es = es
        self.seq = seq
        self.nt = seq // 128
        self.nq = seq // 512
        self.ops = {e: [] for e in ENGS}
        self.sems = {}
        for name in set(ENGSEM.values()):
            self.sems[name] = Sem(es.enter_context(nc.semaphore(name)), name)
        self.dpool = {}
        self.didx = {}
        for q, base in DMASEM.items():
            npool = 6 if q in ("sync", "gpsimd") else 2
            self.dpool[q] = []
            for j in range(npool):
                nm = f"{base}{j}"
                s = Sem(es.enter_context(nc.semaphore(nm)), nm)
                self.sems[nm] = s
                self.dpool[q].append(s)
            self.didx[q] = 0
        self.dram = {}
        self.out_tr = []
        self.out_sb2sb = []
        self.lastw = {}

    def emit(self, eng, f):
        self.ops[eng].append(f)

    def wt(self, eng, semname, v):
        if not v:
            return
        s = self.sems[semname]
        assert v <= s.n, f"wait on {semname}={v} > planned {s.n}"
        key = (eng, semname)
        if self.lastw.get(key, 0) >= v:
            return
        self.lastw[key] = v
        self.emit(eng, lambda e, s=s, v=v: e.wait_ge(s.h, v))

    _opid = [0]
    _oplog = {}

    def op(self, eng, meth, inc=1, selfwait=None, tag="", **kw):
        import sys as _sys
        s = self.sems[ENGSEM[eng]]
        v = s.plan(inc) if inc else None
        K._opid[0] += 1
        fr = _sys._getframe(1)
        K._oplog[K._opid[0]] = f"{meth} @ {fr.f_code.co_name}:{fr.f_lineno}"
        label = f"{meth}.{tag}.{K._opid[0]}"

        def f(e, meth=meth, kw=kw, s=s, inc=inc, label=label):
            sid, _ = self.nc.enter_named_scope(label, False)
            try:
                r = getattr(e, meth)(**kw)
                if inc:
                    r.then_inc(s.h, inc)
            finally:
                self.nc.leave_named_scope(label, sid, False)

        self.emit(eng, f)
        if selfwait is None:
            selfwait = eng in ("vector", "scalar", "gpsimd")
        if selfwait and inc:
            self.wt(eng, ENGSEM[eng], v)
        return v

    def dma(self, q, out, in_, sem=None, **kw):
        if sem is None:
            pool = self.dpool[q]
            s = pool[self.didx[q] % len(pool)]
            self.didx[q] += 1
        else:
            s = self.sems[sem]
        is_tr = bool(kw.get("transpose"))
        try:
            sp_in, sp_out = str(in_.space), str(out.space)
        except Exception:
            sp_in = sp_out = "?"
        is_sb2sb = (not is_tr) and ("SBUF" in sp_in) and ("SBUF" in sp_out)
        if is_tr and self.out_sb2sb:
            self.wtd(q, self.out_sb2sb)
            self.out_sb2sb = []
        if is_sb2sb and self.out_tr:
            self.wtd(q, self.out_tr)
            self.out_tr = []
        if s.n:
            self.wt(q, s.name, s.n)
        v = s.plan(16)

        def f(e, out=out, in_=in_, kw=kw, s=s):
            e.dma_start(out=out, in_=in_, **kw).then_inc(s.h, 16)

        self.emit(q, f)
        tok = (s.name, v)
        if is_tr:
            self.out_tr.append(tok)
        elif is_sb2sb:
            self.out_sb2sb.append(tok)
        return tok

    def wtd(self, eng, tok):
        if not tok:
            return
        if isinstance(tok, list):
            best = {}
            for t in tok:
                if t:
                    best[t[0]] = max(best.get(t[0], 0), t[1])
            for name, v in best.items():
                self.wt(eng, name, v)
            return
        self.wt(eng, tok[0], tok[1])

    def snap(self):
        return {n: s.n for n, s in self.sems.items()}

    def fence(self, engines=ENGS):
        snap = self.snap()
        for eng in engines:
            own = {ENGSEM[eng], DMASEM.get(eng)}
            for name, v in snap.items():
                if name in own or v == 0:
                    continue
                self.wt(eng, name, v)

    def wait_all(self, eng, include_own_dma=False):
        own = {ENGSEM[eng]}
        if not include_own_dma:
            own.add(DMASEM.get(eng))
        for name, s in self.sems.items():
            if name in own or s.n == 0:
                continue
            self.wt(eng, name, s.n)

    def build(self):
        with self.nc.Block() as blk:
            for eng in ENGS:
                def mk(eng):
                    def body(e):
                        for f in self.ops[eng]:
                            f(e)
                    return body
                getattr(blk, eng)(mk(eng))



def host_tables(seq):
    t = {}
    t["ident"] = np.eye(128, dtype=ml_dtypes.bfloat16)
    kk = np.arange(128)[:, None]
    qq = np.arange(512)[None, :]
    t["trmask"] = np.stack([(kk + 128 * e <= qq) for e in range(4)]).astype(ml_dtypes.bfloat16)
    jj = np.arange(128)[None, :]
    t["pvlr"] = np.broadcast_to(
        np.stack([(jj < 64), (jj >= 64)], axis=0)[None, :, 0, :], (128, 2, 128)
    ).astype(ml_dtypes.bfloat16)
    t["pvq4"] = np.broadcast_to(
        np.stack([(jj[0] // 32 == m) for m in range(4)], axis=0)[None, :, :],
        (128, 4, 128)).astype(ml_dtypes.bfloat16)
    return t



def bcast_row(k, name, src_dram, n, ph):
    nc = k.nc
    tile = ph.enter_context(nc.sbuf_tensor(name, [128, n], F32))
    with contextlib.ExitStack() as s:
        row = s.enter_context(nc.sbuf_tensor(name + "_row", [1, n], F32))
        ones = s.enter_context(nc.sbuf_tensor(name + "_ones", [1, 128], F32))
        ps = s.enter_context(nc.psum_tensor(name + "_ps", [128, n], F32))
        k.fence()
        vld = k.dma("sync", out=row[:], in_=src_dram)
        vo = k.op("vector", "memset", ap=ones[:], constant=1.0)
        k.wtd("tensor", vld)
        k.wt("tensor", "cV", k.sems["cV"].n)
        vmm = None
        for c0 in range(0, n, 512):
            c1 = min(c0 + 512, n)
            vmm = k.op("tensor", "matmul", out=ps[:, c0:c1], lhsT=ones[:, :],
                       rhs=row[:, c0:c1], start=True, stop=True)
        k.wt("vector", "cT", vmm)
        vcp = k.op("vector", "tensor_copy", out=tile[:], in_=ps[:])
    k.fence()
    return tile, vcp


def rsqrt_cols(k, buf, src_col, dst_col, eps):
    vv = k.op("vector", "tensor_scalar", out=buf[:, dst_col], in0=buf[:, src_col],
              scalar1=float(eps), scalar2=None, op0=ALU.add)
    k.wt("scalar", "cV", vv)
    vs = k.op("scalar", "activation", out=buf[:, dst_col], in_=buf[:, dst_col], func=AF.Sqrt)
    k.wt("vector", "cS", vs)
    vr = k.op("vector", "reciprocal", out=buf[:, dst_col], in_=buf[:, dst_col])
    k.wt("vector", "cV", vr)
    return vr


def layer_norm_tile(k, x_ap, out_ap, stats, mv, eps=1e-5, n=DM):
    nh = n // 512
    vs = None
    for hh in range(nh):
        vs = k.op("vector", "bn_stats", out=stats[:, hh * 6:(hh + 1) * 6], in_=x_ap[:, hh * 512:(hh + 1) * 512])
    k.wt("vector", "cV", vs)
    va = k.op("vector", "bn_aggr", out=mv[:, 0:2], in_=stats[:, 0:6 * nh])
    k.wt("vector", "cV", va)
    rsqrt_cols(k, mv, slice(1, 2), slice(2, 3), eps)
    return k.op("vector", "tensor_scalar", out=out_ap, in0=x_ap, scalar1=mv[:, 0:1],
                scalar2=mv[:, 2:3], op0=ALU.subtract, op1=ALU.mult)



def emit_l0a(k, xin, xout):
    nc = k.nc
    seq, nt, nq = k.seq, k.nt, k.nq
    D = k.dram
    H, HD, INNER = 8, 64, 512
    hscr = D["hscr"]

    with contextlib.ExitStack() as ph:
        k.fence()
        ones_col = ph.enter_context(nc.sbuf_tensor("a_ones_col", [128, 1], BF16))
        mneg8 = ph.enter_context(nc.sbuf_tensor("a_mneg8", [128, 1], F32))
        trmask = ph.enter_context(nc.sbuf_tensor("a_trmask", [128, 4, 512], BF16))
        ident = ph.enter_context(nc.sbuf_tensor("a_ident", [128, 128], BF16))
        k.op("vector", "memset", ap=ones_col[:], constant=1.0)
        k.op("vector", "memset", ap=mneg8[:], constant=-8.0)
        k.dma("sync", out=trmask[:], in_=D["t_trmask"].rearrange("e p q -> p e q"))
        k.dma("sync", out=ident[:], in_=D["t_ident"][:, :])

        with contextlib.ExitStack() as s1:
            lnw_bc, _ = bcast_row(k, "a_lnw", D["L0_attn_ln_w"][None, :], DM, s1)
            lnb_bc, vbc = bcast_row(k, "a_lnb", D["L0_attn_ln_b"][None, :], DM, s1)
            x_sb = s1.enter_context(nc.sbuf_tensor("a_x", [128, 2, DM], F32))
            xn1 = s1.enter_context(nc.sbuf_tensor("a_xn1", [128, DM], F32))
            scr1 = s1.enter_context(nc.sbuf_tensor("a_scr1", [128, DM], F32))
            xn2 = s1.enter_context(nc.sbuf_tensor("a_xn2", [128, DM], BF16))
            stats = s1.enter_context(nc.sbuf_tensor("a_stats", [128, 12], F32))
            mv = s1.enter_context(nc.sbuf_tensor("a_mv", [128, 4], F32))
            k.wt("vector", "cV", vbc)
            last_st = None
            xload = {0: None, 1: None}
            prev_use = {0: 0, 1: 0}
            for tc in range(nt):
                b = tc % 2
                k.wt("sync", "cV", prev_use[b])
                xload[b] = k.dma("sync", out=x_sb[:, b, :], in_=xin[tc * 128:(tc + 1) * 128, :])
                k.wtd("vector", xload[b])
                layer_norm_tile(k, x_sb[:, b, :], xn1[:], stats, mv)
                layer_norm_tile(k, xn1[:], scr1[:], stats, mv)
                k.op("vector", "tensor_tensor", out=xn1[:], in0=scr1[:], in1=lnw_bc[:], op=ALU.mult)
                k.wtd("vector", last_st)
                vn = k.op("vector", "tensor_tensor", out=xn2[:], in0=xn1[:], in1=lnb_bc[:], op=ALU.add)
                prev_use[b] = vn
                k.wt("sync", "cV", vn)
                last_st = k.dma("sync", out=hscr[tc * 128:(tc + 1) * 128, 0:DM], in_=xn2[:])

        v_sb = ph.enter_context(nc.sbuf_tensor("a_v", [128, nt, 4, 192], BF16))
        vones = k.op("vector", "memset", ap=v_sb[:, :, :, 64:128], constant=1.0)
        with contextlib.ExitStack() as s2:
            k.fence()
            xn2T = s2.enter_context(nc.sbuf_tensor("a_xn2T", [128, 8, seq], BF16))
            k.wtd("sync", last_st)
            vtr = []
            for c in range(8):
                vtr.append(k.dma("sync", out=xn2T[:, c, :], in_=hscr[:, c * 128:(c + 1) * 128],
                                 transpose=True))
            wqkv = s2.enter_context(nc.sbuf_tensor("a_wqkv", [128, 8, 1536], BF16))
            vwl = k.dma("gpsimd", out=wqkv[:], in_=D["L0_attn_w_qkv"].rearrange("(c p) n -> p c n", p=128))
            qk_bf = s2.enter_context(nc.sbuf_tensor("a_qkbf", [128, INNER], BF16))
            sq_scr = s2.enter_context(nc.sbuf_tensor("a_sq", [128, INNER], F32))
            hsums = s2.enter_context(nc.sbuf_tensor("a_hs", [128, 2 * H], F32))
            psA = s2.enter_context(nc.psum_tensor("a_psA", [128, 512], F32))
            psB = s2.enter_context(nc.psum_tensor("a_psB", [128, 512], F32))
            k.wtd("tensor", vtr)
            k.wtd("tensor", vwl)
            ps = {0: psA, 1: psB}
            freed = {0: 0, 1: 0}
            freed_eng = {0: "cV", 1: "cV"}
            vqk_store = None
            for tc in range(nt):
                for oi, o0 in enumerate((0, 512, 1024)):
                    b = (tc * 3 + oi) % 2
                    pp = ps[b]
                    k.wt("tensor", freed_eng[b], freed[b])
                    vm = None
                    for c in range(8):
                        vm = k.op("tensor", "matmul", out=pp[:],
                                  lhsT=xn2T[:, c, tc * 128:(tc + 1) * 128],
                                  rhs=wqkv[:, c, o0:o0 + 512],
                                  start=(c == 0), stop=(c == 7))
                    if o0 == 1024:
                        k.wt("scalar", "cT", vm)
                        ppv = pp[:].rearrange("p (q e) -> p q e", q=4)
                        k.op("scalar", "copy", out=v_sb[:, tc, :, 0:64],
                             in_=ppv[:, :, 0:64])
                        freed[b] = k.op("scalar", "copy", out=v_sb[:, tc, :, 128:192],
                                        in_=ppv[:, :, 64:128])
                        freed_eng[b] = "cS"
                    else:
                        k.wt("scalar", "cT", vm)
                        vsq = k.op("scalar", "activation", out=sq_scr[:], in_=pp[:], func=AF.Square)
                        k.wt("vector", "cS", vsq)
                        k.wtd("vector", vqk_store)
                        k.op("vector", "tensor_reduce", out=hsums[:, 0:H],
                             in_=sq_scr[:].rearrange("p (h d) -> p h d", h=H), axis=AX.X, op=ALU.add)
                        rsqrt_cols(k, hsums, slice(0, H), slice(H, 2 * H), 1e-12)
                        vn = None
                        for h in range(H):
                            vn = k.op("vector", "tensor_scalar", out=qk_bf[:, h * HD:(h + 1) * HD],
                                      in0=pp[:, h * HD:(h + 1) * HD], scalar1=hsums[:, H + h:H + h + 1],
                                      scalar2=None, op0=ALU.mult)
                        freed[b] = vn
                        freed_eng[b] = "cV"
                        k.wt("sync", "cV", vn)
                        vqk_store = k.dma("sync", out=hscr[tc * 128:(tc + 1) * 128, DM + o0:DM + o0 + 512],
                                          in_=qk_bf[:])

        k.fence()
        qT = ph.enter_context(nc.sbuf_tensor("a_qT", [128, 4, seq], BF16))
        kT = ph.enter_context(nc.sbuf_tensor("a_kT", [128, 4, seq], BF16))
        k.wtd("sync", vqk_store)
        k.fence(engines=("sync",))
        vq = []
        for c in range(4):
            vq.append(k.dma("sync", out=qT[:, c, :], in_=hscr[:, DM + c * 128:DM + (c + 1) * 128],
                            transpose=True))
            vq.append(k.dma("sync", out=kT[:, c, :], in_=hscr[:, DM + 512 + c * 128:DM + 512 + (c + 1) * 128],
                            transpose=True))
        wout = ph.enter_context(nc.sbuf_tensor("a_wout", [128, 4, DM], BF16))
        vwo = k.dma("gpsimd", out=wout[:], in_=D["L0_attn_w_out"].rearrange("(c p) n -> p c n", p=128))

        with contextlib.ExitStack() as s3:
            k.fence()
            big2 = s3.enter_context(nc.psum_tensor("a_big2", [128, 2048], F32))
            yt4 = [s3.enter_context(nc.psum_tensor(f"a_yt{t}", [128, 512], F32)) for t in range(4)]
            pts = s3.enter_context(nc.sbuf_tensor("a_pts", [128, 4, 512], BF16))
            rden = s3.enter_context(nc.sbuf_tensor("a_rden", [128, 512], F32))
            oT_sb = s3.enter_context(nc.sbuf_tensor("a_oT4", [128, 4, 512], BF16))
            xr_sb = s3.enter_context(nc.sbuf_tensor("a_xr", [128, 4, DM], F32))
            xnew = s3.enter_context(nc.sbuf_tensor("a_xnew", [128, DM], F32))

            k.wtd("tensor", vq)
            k.wtd("tensor", vwo)
            k.wait_all("tensor")
            exp_free = {0: 0, 1: 0}
            big_free = {0: ("cS", 0), 1: ("cS", 0)}
            banks_free = 0
            oT_free = 0
            out_free = {0: 0, 1: 0}
            st_out = None
            last_vadd = 0
            tig = [0]
            for qc in range(nq):
                k.wt("sync", "cV", last_vadd)
                vxl = k.dma("sync", out=xr_sb[:], in_=xin[qc * 512:(qc + 1) * 512, :].rearrange(
                    "(s p) d -> p s d", p=128))
                nkt = 4 * qc + 4
                for w in range(2):
                    k.wt("tensor", "cV", banks_free)
                    tiles = [(4 * w + j, kt) for j in range(4) for kt in range(nkt)]
                    pairs = [tiles[i:i + 2] for i in range(0, len(tiles), 2)]

                    def emit_pv_a(p2, pair2, gate2, nkt=nkt, w=w):
                        k.wt("tensor", gate2[0], gate2[1])
                        vpe = None
                        for u, (h2, kt2) in enumerate(pair2):
                            j2 = h2 - 4 * w
                            side = h2 % 2
                            vpe = k.op("tensor", "matmul", out=yt4[j2][:],
                                       lhsT=v_sb[:, kt2, h2 // 2, side * 64:side * 64 + 128],
                                       rhs=pts[:, 2 * p2 + u, :],
                                       start=(kt2 == 0), stop=(kt2 == nkt - 1),
                                       skip_group_check=True)
                        exp_free[p2] = vpe
                        return vpe

                    pend = None
                    last_pv = 0
                    for pair in pairs:
                        p = tig[0] % 2
                        tig[0] += 1
                        k.wt("tensor", *big_free[p])
                        vmm = None
                        for u, (h, kt) in enumerate(pair):
                            cih, off = h // 2, (h % 2) * 64
                            vmm = k.op("tensor", "matmul",
                                       out=big2[:, (2 * p + u) * 512:(2 * p + u + 1) * 512],
                                       lhsT=kT[off:off + 64, cih, kt * 128:(kt + 1) * 128],
                                       rhs=qT[off:off + 64, cih, qc * 512:(qc + 1) * 512],
                                       start=True, stop=True)
                        k.wt("scalar", "cT", vmm)
                        k.wt("scalar", "cT", exp_free[p])
                        vexp = k.op("scalar", "activation",
                                    out=pts[:, 2 * p:2 * p + 2, :],
                                    in_=big2[:, 2 * p * 512:(2 * p + 2) * 512],
                                    func=AF.Exp, bias=mneg8[:], scale=8.0, selfwait=False)
                        big_free[p] = ("cS", vexp)
                        gate = ("cS", vexp)
                        vmask = None
                        for u, (h, kt) in enumerate(pair):
                            e = kt - 4 * qc
                            if e >= 0:
                                k.wt("vector", "cS", vexp)
                                vmask = k.op("vector", "tensor_tensor",
                                             out=pts[:, 2 * p + u, :],
                                             in0=pts[:, 2 * p + u, :],
                                             in1=trmask[:, e, :],
                                             op=ALU.mult, selfwait=False)
                        if vmask is not None:
                            gate = ("cV", vmask)
                        if pend is not None:
                            last_pv = emit_pv_a(*pend)
                        pend = (p, pair, gate)
                    last_pv = emit_pv_a(*pend)
                    k.wt("vector", "cT", last_pv)
                    vm1 = 0
                    for j in range(4):
                        h = 4 * w + j
                        pvr = (h % 2) * 64
                        dnr = 64 - pvr
                        k.op("vector", "reciprocal", out=rden[pvr:pvr + 64, :],
                             in_=yt4[j][dnr:dnr + 64, :])
                        k.wt("vector", "cT", oT_free)
                        vm1 = k.op("vector", "tensor_tensor",
                                   out=oT_sb[pvr:pvr + 64, 2 * w + j // 2, :],
                                   in0=yt4[j][pvr:pvr + 64, :],
                                   in1=rden[pvr:pvr + 64, :], op=ALU.mult)
                    banks_free = vm1
                k.wt("tensor", "cV", banks_free)
                vmm2 = 0
                for s in range(4):
                    par = s % 2
                    k.wt("tensor", "cV", out_free[par])
                    for c in range(4):
                        for n0 in (0, 512):
                            vmm2 = k.op("tensor", "matmul",
                                        out=yt4[2 * par + n0 // 512][:],
                                        lhsT=oT_sb[:, c, s * 128:(s + 1) * 128],
                                        rhs=wout[:, c, n0:n0 + 512],
                                        start=(c == 0), stop=(c == 3),
                                        skip_group_check=True)
                    k.wt("vector", "cT", vmm2)
                    k.wtd("vector", vxl)
                    k.wtd("vector", st_out)
                    k.op("vector", "tensor_tensor", out=xnew[:, 0:512],
                         in0=yt4[2 * par][:], in1=xr_sb[:, s, 0:512], op=ALU.add,
                         selfwait=False)
                    vadd = k.op("vector", "tensor_tensor", out=xnew[:, 512:1024],
                                in0=yt4[2 * par + 1][:], in1=xr_sb[:, s, 512:1024],
                                op=ALU.add)
                    out_free[par] = vadd
                    last_vadd = vadd
                    k.wt("sync", "cV", vadd)
                    st_out = k.dma("sync", out=xout[qc * 512 + s * 128:qc * 512 + (s + 1) * 128, :], in_=xnew[:])
                oT_free = vmm2
                banks_free = out_free[1]
        k.fence()



WSPECS_L0A = [
    ("L0_attn_ln_w", (DM,), F32),
    ("L0_attn_ln_b", (DM,), F32),
    ("L0_attn_w_qkv", (DM, 1536), F32),
    ("L0_attn_w_out", (512, DM), F32),
]
TSPECS = [
    ("t_ident", (128, 128), BF16),
    ("t_trmask", (4, 128, 512), BF16),
    ("t_pvlr", (128, 2, 128), BF16),
    ("t_pvq4", (128, 4, 128), BF16),
]


def build_l0a(seq):
    nc = bass.Bass("TRN2", target_bir_lowering=False)
    es = contextlib.ExitStack()
    with es:
        k = K(nc, es, seq)
        D = k.dram
        D["xin"] = nc.dram_tensor("xin", [seq, DM], F32, kind="ExternalInput").ap()
        for nm, shp, dt in WSPECS_L0A + TSPECS:
            D[nm] = nc.dram_tensor(nm, list(shp), dt, kind="ExternalInput").ap()
        D["hscr"] = nc.dram_tensor("hscr", [seq, 2048], BF16, kind="Internal").ap()
        D["xout"] = nc.dram_tensor("xout", [seq, DM], F32, kind="ExternalOutput").ap()
        emit_l0a(k, D["xin"], D["xout"])
        k.build()
    return nc





class _CloseBefore:

    def __init__(self, es):
        self.es = es

    def __enter__(self):
        return self

    def __exit__(self, *a):
        self.es.close()
        return False


def rms_tile(k, x_ap, out_bf, wbc, sq_scr, red, eps=1e-5, n=DM):
    h = n // 2
    k.op("vector", "tensor_tensor", out=sq_scr[:, 0:h], in0=x_ap[:, 0:h], in1=x_ap[:, 0:h],
         op=ALU.mult)
    k.op("vector", "tensor_reduce", out=red[:, 0:1], in_=sq_scr[:, 0:h], axis=AX.X, op=ALU.add)
    k.op("vector", "tensor_tensor", out=sq_scr[:, 0:h], in0=x_ap[:, h:n], in1=x_ap[:, h:n],
         op=ALU.mult)
    k.op("vector", "tensor_reduce", out=red[:, 2:3], in_=sq_scr[:, 0:h], axis=AX.X, op=ALU.add)
    k.op("vector", "tensor_tensor", out=red[:, 0:1], in0=red[:, 0:1], in1=red[:, 2:3], op=ALU.add)
    k.op("vector", "tensor_scalar", out=red[:, 0:1], in0=red[:, 0:1], scalar1=1.0 / n,
         scalar2=None, op0=ALU.mult)
    rsqrt_cols(k, red, slice(0, 1), slice(1, 2), eps)
    return k.op("vector", "scalar_tensor_tensor", out=out_bf[:], in0=x_ap,
                scalar=red[:, 1:2], in1=wbc[:], op0=ALU.mult, op1=ALU.mult)


def rope_pairs(k, qf_view, cos_bc, sin_bc, scrE, scrO, scrE2, scrO2):
    nh, ng = qf_view.shape[1], qf_view.shape[2]
    qe = qf_view[:, :, :, 0]
    qo = qf_view[:, :, :, 1]
    sE = scrE[:, 0:nh * ng].rearrange("p (h g) -> p h g", h=nh)
    sO = scrO[:, 0:nh * ng].rearrange("p (h g) -> p h g", h=nh)
    sE2 = scrE2[:, 0:nh * ng].rearrange("p (h g) -> p h g", h=nh)
    sO2 = scrO2[:, 0:nh * ng].rearrange("p (h g) -> p h g", h=nh)
    k.op("vector", "tensor_tensor", out=sE, in0=qe, in1=cos_bc, op=ALU.mult)
    k.op("vector", "tensor_tensor", out=sE2, in0=qo, in1=sin_bc, op=ALU.mult)
    k.op("vector", "tensor_tensor", out=sO, in0=qo, in1=cos_bc, op=ALU.mult)
    k.op("vector", "tensor_tensor", out=sO2, in0=qe, in1=sin_bc, op=ALU.mult)
    k.op("vector", "tensor_tensor", out=qe, in0=sE, in1=sE2, op=ALU.subtract)
    return k.op("vector", "tensor_tensor", out=qo, in0=sO, in1=sO2, op=ALU.add)


def emit_l0f(k, xin, xout):
    nc = k.nc
    seq, nt, nq = k.seq, k.nt, k.nq
    D = k.dram
    H = 16
    hscr, zb, r1b, macc = D["hscr"], D["zb"], D["r1b"], D["macc"]

    with contextlib.ExitStack() as ph:
        k.fence()
        ident = ph.enter_context(nc.sbuf_tensor("f_ident", [128, 128], BF16))
        k.dma("sync", out=ident[:], in_=D["t_ident"][:, :])
        red = ph.enter_context(nc.sbuf_tensor("f_red", [128, 4], F32))
        stats = ph.enter_context(nc.sbuf_tensor("f_stats", [128, 12], F32))
        mvst = ph.enter_context(nc.sbuf_tensor("f_mv", [128, 4], F32))

        with contextlib.ExitStack() as s1:
            k.fence()
            w1bc, vbc = bcast_row(k, "f_w1", D["L0_ffn_w_ln1"][None, :], DM, s1)
            x_sb = s1.enter_context(nc.sbuf_tensor("f_x", [128, 2, DM], F32))
            z_sb = s1.enter_context(nc.sbuf_tensor("f_z", [128, DM], F32))
            sq = s1.enter_context(nc.sbuf_tensor("f_sq", [128, DM], F32))
            tbf = s1.enter_context(nc.sbuf_tensor("f_tbf", [128, DM], BF16))
            k.wt("vector", "cV", vbc)
            last_st = None
            zst = None
            prev = {0: 0, 1: 0}
            for tc in range(nt):
                b = tc % 2
                k.wt("sync", "cV", prev[b])
                vl = k.dma("sync", out=x_sb[:, b, :], in_=xin[tc * 128:(tc + 1) * 128, :])
                k.wtd("vector", vl)
                k.wtd("vector", zst)
                prev[b] = layer_norm_tile(k, x_sb[:, b, :], z_sb[:], stats, mvst)
                k.wt("sync", "cV", prev[b])
                zst = k.dma("sync", out=zb[tc * 128:(tc + 1) * 128, :], in_=z_sb[:])
                k.wtd("vector", last_st)
                vn = rms_tile(k, z_sb[:], tbf, w1bc, sq, red)
                prev[b] = vn
                k.wt("sync", "cV", vn)
                last_st = k.dma("sync", out=hscr[tc * 128:(tc + 1) * 128, 0:DM], in_=tbf[:])

        with contextlib.ExitStack() as satt:
            kT = satt.enter_context(nc.sbuf_tensor("f_kT", [128, 8, seq], BF16))
            sproj = contextlib.ExitStack()
            satt.enter_context(_CloseBefore(sproj))
            latT = sproj.enter_context(nc.sbuf_tensor("f_latT", [128, seq], BF16))

            with contextlib.ExitStack() as s2:
                k.fence()
                tT = s2.enter_context(nc.sbuf_tensor("f_tT", [128, 8, seq], BF16))
                k.wtd("sync", last_st)
                vtr = []
                for c in range(8):
                    vtr.append(k.dma("sync", out=tT[:, c, :], in_=hscr[:, c * 128:(c + 1) * 128],
                                     transpose=True))
                wkva = s2.enter_context(nc.sbuf_tensor("f_wkva", [128, 8, 160], BF16))
                vw1 = k.dma("gpsimd", out=wkva[:], in_=D["L0_ffn_Wkva"].rearrange("(c p) n -> p c n", p=128))
                wq = s2.enter_context(nc.sbuf_tensor("f_wq", [128, 8, DM], BF16))
                vw2 = k.dma("gpsimd", out=wq[:], in_=D["L0_ffn_Wq"].rearrange("(c p) n -> p c n", p=128))
                cosP = s2.enter_context(nc.sbuf_tensor("f_cosP", [128, nt, 16], F32))
                sinP = s2.enter_context(nc.sbuf_tensor("f_sinP", [128, nt, 16], F32))
                vc1 = k.dma("sync", out=cosP[:], in_=D["t_mla_cos"].rearrange("(t p) g -> p t g", p=128))
                vc2 = k.dma("sync", out=sinP[:], in_=D["t_mla_sin"].rearrange("(t p) g -> p t g", p=128))
                psq = s2.enter_context(nc.psum_tensor("f_psq", [128, 1024], F32))
                psl = s2.enter_context(nc.psum_tensor("f_psl", [128, 512], F32))
                psr = s2.enter_context(nc.psum_tensor("f_psr", [128, 32], F32))
                qf = s2.enter_context(nc.sbuf_tensor("f_qf", [128, DM], F32))
                qb = s2.enter_context(nc.sbuf_tensor("f_qb", [128, DM], BF16))
                krf = s2.enter_context(nc.sbuf_tensor("f_krf", [128, 32], F32))
                krb = s2.enter_context(nc.sbuf_tensor("f_krb", [128, 32], BF16))
                scrE = s2.enter_context(nc.sbuf_tensor("f_scrE", [128, 256], F32))
                scrO = s2.enter_context(nc.sbuf_tensor("f_scrO", [128, 256], F32))
                scrE2 = s2.enter_context(nc.sbuf_tensor("f_scrE2", [128, 256], F32))
                scrO2 = s2.enter_context(nc.sbuf_tensor("f_scrO2", [128, 256], F32))
                k.wtd("tensor", vtr)
                k.wtd("tensor", [vw1, vw2])
                k.wtd("vector", [vc1, vc2])
                vlat_ev = None
                for ch in range(seq // 512):
                    vm = None
                    for c in range(8):
                        vm = k.op("tensor", "matmul", out=psl[:],
                                  lhsT=wkva[:, c, 0:128], rhs=tT[:, c, ch * 512:(ch + 1) * 512],
                                  start=(c == 0), stop=(c == 7))
                    k.wt("scalar", "cT", vm)
                    vlat_ev = k.op("scalar", "copy", out=latT[:, ch * 512:(ch + 1) * 512], in_=psl[:])
                    k.wt("tensor", "cS", vlat_ev)
                vst_q = None
                vst_kr = None
                for tc in range(nt):
                    cs_bc = cosP[:, tc, :].unsqueeze(1)
                    sn_bc = sinP[:, tc, :].unsqueeze(1)
                    vm = None
                    for c in range(8):
                        vm = k.op("tensor", "matmul", out=psr[:],
                                  lhsT=tT[:, c, tc * 128:(tc + 1) * 128], rhs=wkva[:, c, 128:160],
                                  start=(c == 0), stop=(c == 7))
                    k.wt("vector", "cT", vm)
                    k.wtd("vector", vst_kr)
                    k.op("vector", "tensor_copy", out=krf[:], in_=psr[:])
                    vr = rope_pairs(k, krf[:].rearrange("p (h g two) -> p h g two", h=1, two=2),
                                    cs_bc.broadcast_to([128, 1, 16]), sn_bc.broadcast_to([128, 1, 16]),
                                    scrE, scrO, scrE2, scrO2)
                    vkb = k.op("vector", "tensor_copy", out=krb[:], in_=krf[:])
                    k.wt("tensor", "cV", vr)
                    k.wt("sync", "cV", vkb)
                    vst_kr = k.dma("sync", out=hscr[tc * 128:(tc + 1) * 128, 0:32], in_=krb[:])
                    vm = None
                    for half in range(2):
                        for c in range(8):
                            vm = k.op("tensor", "matmul", out=psq[:, half * 512:(half + 1) * 512],
                                      lhsT=tT[:, c, tc * 128:(tc + 1) * 128],
                                      rhs=wq[:, c, half * 512:(half + 1) * 512],
                                      start=(c == 0), stop=(c == 7))
                    k.wt("vector", "cT", vm)
                    k.wtd("vector", vst_q)
                    k.op("vector", "tensor_scalar", out=qf[:], in0=psq[:], scalar1=0.125,
                         scalar2=None, op0=ALU.mult)
                    qv = qf[:].rearrange("p (h d) -> p h d", h=16)[:, :, 32:64]
                    vq2 = rope_pairs(k, qv.rearrange("p h (g two) -> p h g two", two=2),
                                     cs_bc.broadcast_to([128, 16, 16]), sn_bc.broadcast_to([128, 16, 16]),
                                     scrE, scrO, scrE2, scrO2)
                    vqb = k.op("vector", "tensor_copy", out=qb[:], in_=qf[:])
                    k.wt("tensor", "cV", vq2)
                    k.wt("sync", "cV", vqb)
                    vst_q = k.dma("sync", out=hscr[tc * 128:(tc + 1) * 128, DM:2 * DM], in_=qb[:])

            kv_aug = satt.enter_context(nc.sbuf_tensor("f_kvaug", [128, nt, 8, 96], BF16,
                                                       side="right"))
            with contextlib.ExitStack() as s3:
                k.fence()
                kvT = s3.enter_context(nc.sbuf_tensor("f_kvT", [128, 4, seq], BF16))
                wkvb = s3.enter_context(nc.sbuf_tensor("f_wkvb", [128, 512], BF16))
                vwb = k.dma("gpsimd", out=wkvb[:], in_=D["L0_ffn_Wkvb"][:, :])
                psA = s3.enter_context(nc.psum_tensor("f_psA", [128, 512], F32))
                psB = s3.enter_context(nc.psum_tensor("f_psB", [128, 512], F32))
                k.wtd("tensor", vwb)
                freed = {0: 0, 1: 0}
                for c4 in range(4):
                    for ch in range(seq // 512):
                        b = (c4 * (seq // 512) + ch) % 2
                        pp = psA if b == 0 else psB
                        k.wt("tensor", "cS", freed[b])
                        vm = k.op("tensor", "matmul", out=pp[:],
                                  lhsT=wkvb[:, c4 * 128:(c4 + 1) * 128],
                                  rhs=latT[:, ch * 512:(ch + 1) * 512], start=True, stop=True)
                        k.wt("scalar", "cT", vm)
                        freed[b] = k.op("scalar", "copy", out=kvT[:, c4, ch * 512:(ch + 1) * 512], in_=pp[:])
                k.fence(engines=("sync",))
                vasm = []
                for c in range(8):
                    for par, h in ((0, 2 * c), (64, 2 * c + 1)):
                        vasm.append(k.dma("sync", out=kT[par:par + 32, c, :],
                                          in_=kvT[(h % 4) * 32:(h % 4) * 32 + 32, h // 4, :]))
                krT = s3.enter_context(nc.sbuf_tensor("f_krT", [128, seq], BF16))
                vkt = k.dma("sync", out=krT[:], in_=hscr[:, 0:128], transpose=True)
                k.wtd("sync", vkt)
                for c in range(8):
                    for par in (32, 96):
                        vasm.append(k.dma("sync", out=kT[par:par + 32, c, :],
                                          in_=krT[0:32, :]))
                vko = k.op("vector", "memset", ap=kv_aug[:, :, :, 32:64], constant=1.0)
                k.wt("tensor", "cS", k.sems["cS"].n)
                freed = {0: 0, 1: 0}
                for tc in range(nt):
                    b = tc % 2
                    pp = psA if b == 0 else psB
                    k.wt("tensor", "cV", freed[b] if freed[b] else vko)
                    vm = k.op("tensor", "matmul", out=pp[:], lhsT=latT[:, tc * 128:(tc + 1) * 128],
                              rhs=wkvb[:], start=True, stop=True)
                    k.wt("vector", "cT", vm)
                    ppk = pp[:].rearrange("p (q e) -> p q e", q=8)
                    k.op("vector", "tensor_copy", out=kv_aug[:, tc, :, 0:32],
                         in_=ppk[:, :, 0:32])
                    freed[b] = k.op("vector", "tensor_copy",
                                    out=kv_aug[:, tc, :, 64:96],
                                    in_=ppk[:, :, 32:64])
                k.wtd("sync", vasm)

            sproj.close()
            with contextlib.ExitStack() as s4:
                k.fence()
                qT = s4.enter_context(nc.sbuf_tensor("f_qT", [128, 8, 2, 512], BF16))

                def load_qt(qc2):
                    toks = []
                    for c in range(8):
                        toks.append(k.dma("sync", out=qT[:, c, qc2 % 2, :],
                                          in_=hscr[qc2 * 512:(qc2 + 1) * 512,
                                                   DM + c * 128:DM + (c + 1) * 128],
                                          transpose=True))
                    return toks
                qload = {0: load_qt(0), 1: None}
                qk_last = {0: 0, 1: 0}
                wosub = s4.enter_context(nc.sbuf_tensor("f_wosub", [128, 4, DM], BF16))
                wo4 = D["L0_ffn_Wo"].rearrange("(h t d) n -> t h d n", t=2, d=32)[0]
                vwo = []
                for h in range(16):
                    vwo.append(k.dma("gpsimd", out=wosub[(h % 4) * 32:(h % 4) * 32 + 32, h // 4, :],
                                     in_=wo4[h]))
                trm = s4.enter_context(nc.sbuf_tensor("f_trm", [128, 4, 512], BF16))
                vmk = k.dma("sync", out=trm[:], in_=D["t_trmask"].rearrange("e p q -> p e q"))
                w2bc, _ = bcast_row(k, "f_w2", D["L0_ffn_w_ln2"][None, :], DM, s4)
                mbias = s4.enter_context(nc.sbuf_tensor("f_mbias", [128, 1], F32))
                vmb = k.op("vector", "memset", ap=mbias[:], constant=-12.0)
                sc = s4.enter_context(nc.psum_tensor("f_sc", [128, 4, 512], F32))
                ytf = [s4.enter_context(nc.psum_tensor(f"f_ytf{t}", [128, 512], F32))
                       for t in range(4)]
                pts = s4.enter_context(nc.sbuf_tensor("f_pts", [128, 4, 512], BF16))
                rdf = s4.enter_context(nc.sbuf_tensor("f_rdf", [128, 512], F32))
                oTf = s4.enter_context(nc.sbuf_tensor("f_oTf", [128, 4, 512], BF16))
                zt = s4.enter_context(nc.sbuf_tensor("f_zt", [128, 2, DM], F32))
                r1sb = s4.enter_context(nc.sbuf_tensor("f_r1", [128, DM], F32))
                usb = s4.enter_context(nc.sbuf_tensor("f_u", [128, DM], BF16))
                sqc = s4.enter_context(nc.sbuf_tensor("f_sqc", [128, 512], F32))
                k.wtd("tensor", vwo)
                k.wtd("vector", vmk)
                k.wt("scalar", "cV", vmb)

                scfree2 = [None, None]
                ptsfree2 = [None, None]
                spi = [0]
                fbanks_free = 0
                oTf_free = 0
                st_r1 = {0: None, 1: None}
                st_u = {0: None, 1: None}
                zprev = {0: 0, 1: 0}

                def sc_pair(eng):
                    p = spi[0] % 2
                    spi[0] += 1
                    if scfree2[p]:
                        k.wt(eng, *scfree2[p])
                    return p

                for qc in range(nq):
                    nkt = 4 * (qc + 1)
                    q0, q1 = qc * 512, (qc + 1) * 512
                    if qc + 1 < nq:
                        k.wt("sync", "cT", qk_last[(qc + 1) % 2])
                        qload[(qc + 1) % 2] = load_qt(qc + 1)
                    k.wtd("tensor", qload[qc % 2])
                    for hg in range(4):
                        k.wt("tensor", "cV", fbanks_free)
                        tiles = [(hg * 4 + hi, kt) for hi in range(4) for kt in range(nkt)]
                        pairs = [tiles[i:i + 2] for i in range(0, len(tiles), 2)]

                        def emit_pv(p, nkt=nkt):
                            pp2, pair2, gate2 = p
                            k.wt("tensor", *gate2)
                            vpv = None
                            for u, (h2, kt2) in enumerate(pair2):
                                j2 = h2 % 4
                                side = h2 % 2
                                vpv = k.op("tensor", "matmul", out=ytf[j2][0:64, :],
                                           lhsT=kv_aug[:, kt2, h2 // 2,
                                                       side * 32:side * 32 + 64],
                                           rhs=pts[:, 2 * pp2 + u, :],
                                           start=(kt2 == 0), stop=(kt2 == nkt - 1),
                                           skip_group_check=True)
                            ptsfree2[pp2] = ("cT", vpv)
                            return vpv

                        pend = None
                        last_pv = 0
                        for pair in pairs:
                            pp = sc_pair("tensor")
                            vmm = None
                            for u, (h, kt) in enumerate(pair):
                                cih, off = h // 2, (h % 2) * 64
                                vmm = k.op("tensor", "matmul", out=sc[:, 2 * pp + u, :],
                                           lhsT=kT[off:off + 64, cih, kt * 128:(kt + 1) * 128],
                                           rhs=qT[off:off + 64, cih, qc % 2, :],
                                           start=True, stop=True)
                            if ptsfree2[pp]:
                                k.wt("scalar", *ptsfree2[pp])
                            k.wt("scalar", "cT", vmm)
                            nu = len(pair)
                            vexp = k.op("scalar", "activation",
                                        out=pts[:, 2 * pp:2 * pp + nu, :],
                                        in_=sc[:, 2 * pp:2 * pp + nu, :],
                                        func=AF.Exp, bias=mbias[:],
                                        scale=1.0, selfwait=False)
                            scfree2[pp] = ("cS", vexp)
                            gate = ("cS", vexp)
                            vmsk = None
                            for u, (h, kt) in enumerate(pair):
                                e = kt - 4 * qc
                                if e >= 0:
                                    k.wt("vector", "cS", vexp)
                                    vmsk = k.op("vector", "tensor_tensor",
                                                out=pts[:, 2 * pp + u, :],
                                                in0=pts[:, 2 * pp + u, :],
                                                in1=trm[:, e, :], op=ALU.mult,
                                                selfwait=False)
                            if vmsk is not None:
                                gate = ("cV", vmsk)
                            if pend is not None:
                                last_pv = emit_pv(pend)
                            pend = (pp, pair, gate)
                        last_pv = emit_pv(pend)
                        k.wt("vector", "cT", last_pv)
                        vm1 = 0
                        for j in range(4):
                            h = hg * 4 + j
                            pvr = (h % 2) * 32
                            dnr = 32 - pvr
                            k.op("vector", "reciprocal",
                                 out=rdf[j * 32:j * 32 + 32, :],
                                 in_=ytf[j][dnr:dnr + 32, :])
                            k.wt("vector", "cT", oTf_free)
                            vm1 = k.op("vector", "tensor_tensor",
                                       out=oTf[j * 32:j * 32 + 32, hg, :],
                                       in0=ytf[j][pvr:pvr + 32, :],
                                       in1=rdf[j * 32:j * 32 + 32, :],
                                       op=ALU.mult)
                        fbanks_free = vm1
                    qk_last[qc % 2] = last_pv
                    for qs in range(4):
                        b = qs % 2
                        tok0 = q0 + qs * 128
                        k.wt("sync", "cV", zprev[b])
                        vzl = k.dma("sync", out=zt[:, b, :], in_=zb[tok0:tok0 + 128, :])
                        if st_r1[1]:
                            k.wt("vector", *st_r1[1])
                        k.wt("tensor", "cV", fbanks_free)
                        pe = sc_pair("tensor")
                        sa, sb2 = 2 * pe, 2 * pe + 1
                        vmo = None
                        for c in range(4):
                            k.op("tensor", "matmul", out=sc[:, sa, :],
                                 lhsT=oTf[:, c, qs * 128:(qs + 1) * 128],
                                 rhs=wosub[:, c, 0:512],
                                 start=(c == 0), stop=(c == 3), skip_group_check=True)
                            vmo = k.op("tensor", "matmul", out=sc[:, sb2, :],
                                       lhsT=oTf[:, c, qs * 128:(qs + 1) * 128],
                                       rhs=wosub[:, c, 512:1024],
                                       start=(c == 0), stop=(c == 3), skip_group_check=True)
                        oTf_free = vmo
                        k.wt("vector", "cT", vmo)
                        k.wtd("vector", vzl)
                        k.wtd("vector", st_r1[0])
                        k.op("vector", "tensor_tensor", out=r1sb[:, 0:512], in0=sc[:, sa, :],
                             in1=zt[:, b, 0:512], op=ALU.add, selfwait=False)
                        vr1 = k.op("vector", "tensor_tensor", out=r1sb[:, 512:1024],
                                   in0=sc[:, sb2, :], in1=zt[:, b, 512:1024], op=ALU.add)
                        zprev[b] = vr1
                        scfree2[pe] = ("cV", vr1)
                        k.wt("sync", "cV", vr1)
                        st_r1[0] = k.dma("sync", out=r1b[tok0:tok0 + 128, :], in_=r1sb[:])
                        k.wtd("vector", st_u[0])
                        vu = rms_tile(k, r1sb[:], usb, w2bc, sqc, red)
                        st_r1[1] = ("cV", vu)
                        k.wt("sync", "cV", vu)
                        st_u[0] = k.dma("sync", out=hscr[tok0:tok0 + 128, 0:DM], in_=usb[:])


        with contextlib.ExitStack() as s5:
            k.fence()
            uT = s5.enter_context(nc.sbuf_tensor("f_uT", [128, 8, seq], BF16))
            vut = []
            for c in range(8):
                vut.append(k.dma("sync", out=uT[:, c, :], in_=hscr[:, c * 128:(c + 1) * 128],
                                 transpose=True))
            wgate = s5.enter_context(nc.sbuf_tensor("f_wgate", [128, 8, 8], BF16))
            vwg = k.dma("gpsimd", out=wgate[:], in_=D["L0_ffn_Wgate"].rearrange("(c p) n -> p c n", p=128))
            ebias, vbc2 = bcast_row(k, "f_ebias", D["L0_ffn_expert_bias"][None, :], 8, s5)
            comb = s5.enter_context(nc.sbuf_tensor("f_comb", [128, nt, 8], F32))
            lg = s5.enter_context(nc.sbuf_tensor("f_lg", [128, 8], F32))
            cscr = s5.enter_context(nc.sbuf_tensor("f_cscr", [128, 8], F32))
            cscr2 = s5.enter_context(nc.sbuf_tensor("f_cscr2", [128, 8], F32))
            cred = s5.enter_context(nc.sbuf_tensor("f_cred", [128, 4], F32))
            psg = s5.enter_context(nc.psum_tensor("f_psg", [128, 512], F32))
            psu = s5.enter_context(nc.psum_tensor("f_psu", [128, 512], F32))
            ptp2 = s5.enter_context(nc.psum_tensor("f_ptp2", [128, 512], BF16))
            dn = {j: s5.enter_context(nc.psum_tensor(f"f_dn{j}", [128, 1024], F32))
                  for j in range(2)}
            gs = s5.enter_context(nc.sbuf_tensor("f_gs", [128, 512], F32))
            hb = s5.enter_context(nc.sbuf_tensor("f_hb", [128, 512], BF16))
            hT = s5.enter_context(nc.sbuf_tensor("f_hT", [128, 2, 512], BF16))
            nth = nt // 2
            acc = s5.enter_context(nc.sbuf_tensor("f_acc", [128, nth, DM], F32))
            wg_sb = s5.enter_context(nc.sbuf_tensor("f_wgsb", [128, 2, 8, 512], BF16))
            wu_sb = s5.enter_context(nc.sbuf_tensor("f_wusb", [128, 2, 8, 512], BF16))
            wd_sb = s5.enter_context(nc.sbuf_tensor("f_wdsb", [128, 2, 4, DM], BF16))
            xa = s5.enter_context(nc.sbuf_tensor("f_xa", [128, 2, DM], F32))
            xb2 = s5.enter_context(nc.sbuf_tensor("f_xb", [128, 2, DM], F32))
            xo = s5.enter_context(nc.sbuf_tensor("f_xo", [128, 2, DM], F32))
            k.wtd("tensor", vut)
            k.wtd("tensor", vwg)
            k.wt("vector", "cV", vbc2)
            for tc in range(nt):
                vm = None
                for c in range(8):
                    vm = k.op("tensor", "matmul", out=psg[:, 0:8], lhsT=uT[:, c, tc * 128:(tc + 1) * 128],
                              rhs=wgate[:, c, :], start=(c == 0), stop=(c == 7))
                k.wt("vector", "cT", vm)
                k.op("vector", "tensor_tensor", out=lg[:], in0=psg[:, 0:8], in1=ebias[:], op=ALU.add)
                vfree = k.sems["cV"].n
                k.wt("tensor", "cV", vfree)
                k.op("vector", "tensor_reduce", out=cred[:, 0:1], in_=lg[:], axis=AX.X, op=ALU.max)
                k.op("vector", "tensor_scalar", out=cscr[:], in0=lg[:], scalar1=cred[:, 0:1],
                     scalar2=None, op0=ALU.is_equal)
                k.op("vector", "scalar_tensor_tensor", out=cscr[:], in0=cscr[:], scalar=-1e30,
                     in1=lg[:], op0=ALU.mult, op1=ALU.add)
                k.op("vector", "tensor_reduce", out=cred[:, 1:2], in_=cscr[:], axis=AX.X, op=ALU.max)
                k.op("vector", "tensor_scalar", out=cscr2[:], in0=lg[:], scalar1=cred[:, 1:2],
                     scalar2=None, op0=ALU.is_ge)
                vng = k.op("vector", "tensor_scalar", out=cred[:, 2:3], in0=cred[:, 0:1],
                           scalar1=-1.0, scalar2=None, op0=ALU.mult)
                k.wt("scalar", "cV", vng)
                vex = k.op("scalar", "activation", out=cscr[:], in_=lg[:], func=AF.Exp,
                           bias=cred[:, 2:3], scale=1.0)
                k.wt("vector", "cS", vex)
                k.op("vector", "tensor_tensor", out=cscr[:], in0=cscr[:], in1=cscr2[:], op=ALU.mult)
                k.op("vector", "tensor_reduce", out=cred[:, 3:4], in_=cscr[:], axis=AX.X, op=ALU.add)
                k.op("vector", "reciprocal", out=cred[:, 3:4], in_=cred[:, 3:4])
                k.op("vector", "tensor_scalar", out=comb[:, tc, :], in0=cscr[:],
                     scalar1=cred[:, 3:4], scalar2=None, op0=ALU.mult)
            wtoks = {0: 0, 1: 0}
            st_out = {0: None, 1: None}
            xprev2 = {0: 0, 1: 0}
            vgu_done = k.sems["cV"].n
            silu_done = 0
            dn_free = {0: 0, 1: 0}
            hT_free = {0: 0, 1: 0}
            acc_rd2 = 0
            for half in range(2):
                t0h = half * nth
                for e in range(9):
                    b = e % 2
                    if e == 0:
                        srcs = (D["L0_ffn_sWg"][0], D["L0_ffn_sWu"][0], D["L0_ffn_sWd"][0])
                    else:
                        srcs = (D["L0_ffn_Wg"][e - 1], D["L0_ffn_Wu"][e - 1], D["L0_ffn_Wd"][e - 1])
                    k.wt("gpsimd", "cT", wtoks[b])
                    t1 = k.dma("gpsimd", out=wg_sb[:, b], in_=srcs[0].rearrange("(c p) n -> p c n", p=128))
                    t2 = k.dma("gpsimd", out=wu_sb[:, b], in_=srcs[1].rearrange("(c p) n -> p c n", p=128))
                    t3 = k.dma("gpsimd", out=wd_sb[:, b], in_=srcs[2].rearrange("(c p) n -> p c n", p=128))
                    k.wtd("tensor", [t1, t2, t3])
                    pend_dn = None

                    def emit_down(pd):
                        sl, tci, ve = pd
                        k.wt("tensor", "cV", ve)
                        k.wt("tensor", "cV", dn_free[sl])
                        vdm = None
                        for c in range(4):
                            for n0 in (0, 512):
                                vdm = k.op("tensor", "matmul", out=dn[sl][:, n0:n0 + 512],
                                           lhsT=hT[:, sl, c * 128:(c + 1) * 128],
                                           rhs=wd_sb[:, b, c, n0:n0 + 512],
                                           start=(c == 0), stop=(c == 3), skip_group_check=True)
                        hT_free[sl] = vdm
                        k.wt("vector", "cT", vdm)
                        if e == 0:
                            k.wt("vector", "cV", acc_rd2)
                            vcp = k.op("vector", "tensor_copy", out=acc[:, tci, :], in_=dn[sl][:],
                                       selfwait=False)
                        else:
                            vcp = k.op("vector", "scalar_tensor_tensor", out=acc[:, tci, :],
                                       in0=dn[sl][:], scalar=comb[:, t0h + tci, e - 1:e],
                                       in1=acc[:, tci, :], op0=ALU.mult, op1=ALU.add, selfwait=False)
                        dn_free[sl] = vcp
                        return vcp

                    for ti in range(nth):
                        tc = t0h + ti
                        sl = ti % 2
                        k.wt("tensor", "cV", vgu_done)
                        k.wt("tensor", "cS", silu_done)
                        vg = None
                        for c in range(8):
                            vg = k.op("tensor", "matmul", out=psg[:], lhsT=uT[:, c, tc * 128:(tc + 1) * 128],
                                      rhs=wg_sb[:, b, c, :], start=(c == 0), stop=(c == 7))
                        for c in range(8):
                            vg = k.op("tensor", "matmul", out=psu[:], lhsT=uT[:, c, tc * 128:(tc + 1) * 128],
                                      rhs=wu_sb[:, b, c, :], start=(c == 0), stop=(c == 7))
                        k.wt("scalar", "cT", vg)
                        vgs = k.op("scalar", "activation", out=gs[:], in_=psg[:], func=AF.Silu,
                                   selfwait=False)
                        silu_done = vgs
                        k.wt("vector", "cS", vgs)
                        vhb = k.op("vector", "tensor_tensor", out=hb[:], in0=psu[:], in1=gs[:],
                                   op=ALU.mult)
                        vgu_done = vhb
                        vev = None
                        for c in range(4):
                            k.wt("tensor", "cV", vhb if c == 0 else vev)
                            if c == 0:
                                k.wt("tensor", "cT", hT_free[sl])
                            vtp = k.op("tensor", "transpose", out=ptp2[:, c * 128:(c + 1) * 128],
                                       in_=hb[:, c * 128:(c + 1) * 128], identity=ident[:])
                            k.wt("vector", "cT", vtp)
                            vev = k.op("vector", "tensor_copy", out=hT[:, sl, c * 128:(c + 1) * 128],
                                       in_=ptp2[:, c * 128:(c + 1) * 128])
                        if pend_dn is not None:
                            emit_down(pend_dn)
                        pend_dn = (sl, ti, vev)
                    vlast_acc = emit_down(pend_dn)
                    wtoks[b] = k.sems["cT"].n
                for ti in range(nth):
                    tc = t0h + ti
                    fb = ti % 2
                    k.wt("sync", "cV", xprev2[fb])
                    va = k.dma("sync", out=xa[:, fb, :], in_=xin[tc * 128:(tc + 1) * 128, :])
                    vb_ = k.dma("sync", out=xb2[:, fb, :], in_=r1b[tc * 128:(tc + 1) * 128, :])
                    k.wtd("vector", [va, vb_])
                    k.wtd("vector", st_out[fb])
                    k.wt("vector", "cV", vlast_acc)
                    k.op("vector", "tensor_tensor", out=xo[:, fb, :], in0=xa[:, fb, :],
                         in1=xb2[:, fb, :], op=ALU.add)
                    vo2 = k.op("vector", "tensor_tensor", out=xo[:, fb, :], in0=xo[:, fb, :],
                               in1=acc[:, ti, :], op=ALU.add)
                    xprev2[fb] = vo2
                    acc_rd2 = vo2
                    k.wt("sync", "cV", vo2)
                    st_out[fb] = k.dma("sync", out=xout[tc * 128:(tc + 1) * 128, :], in_=xo[:, fb, :])
        k.fence()


WSPECS_L0F = [
    ("L0_ffn_w_ln1", (DM,), F32),
    ("L0_ffn_Wq", (DM, DM), F32),
    ("L0_ffn_Wkva", (DM, 160), F32),
    ("L0_ffn_Wkvb", (128, 512), F32),
    ("L0_ffn_Wo", (DM, DM), F32),
    ("L0_ffn_w_ln2", (DM,), F32),
    ("L0_ffn_Wgate", (DM, 8), F32),
    ("L0_ffn_expert_bias", (8,), F32),
    ("L0_ffn_Wg", (8, DM, 512), F32),
    ("L0_ffn_Wu", (8, DM, 512), F32),
    ("L0_ffn_Wd", (8, 512, DM), F32),
    ("L0_ffn_sWg", (1, DM, 512), F32),
    ("L0_ffn_sWu", (1, DM, 512), F32),
    ("L0_ffn_sWd", (1, 512, DM), F32),
]


def tspecs_l0f(seq):
    return [
        ("t_ident", (128, 128), BF16),
        ("t_trmask", (4, 128, 512), BF16),
        ("t_mla_cos", (seq, 16), F32),
        ("t_mla_sin", (seq, 16), F32),
    ]


def build_l0f(seq):
    nc = bass.Bass("TRN2", target_bir_lowering=False)
    es = contextlib.ExitStack()
    with es:
        k = K(nc, es, seq)
        D = k.dram
        D["xin"] = nc.dram_tensor("xin", [seq, DM], F32, kind="ExternalInput").ap()
        for nm, shp, dt in WSPECS_L0F + tspecs_l0f(seq):
            D[nm] = nc.dram_tensor(nm, list(shp), dt, kind="ExternalInput").ap()
        D["hscr"] = nc.dram_tensor("hscr", [seq, 2048], BF16, kind="Internal").ap()
        D["zb"] = nc.dram_tensor("zb", [seq, DM], F32, kind="Internal").ap()
        D["r1b"] = nc.dram_tensor("r1b", [seq, DM], F32, kind="Internal").ap()
        D["macc"] = nc.dram_tensor("macc", [seq, DM], F32, kind="Internal").ap()
        D["xout"] = nc.dram_tensor("xout", [seq, DM], F32, kind="ExternalOutput").ap()
        emit_l0f(k, D["xin"], D["xout"])
        k.build()
    return nc





def l1a_host_tables(seq):
    H = 8
    lam = np.log1p(-2.0 ** (-5.0 - np.arange(H, dtype=np.float64)))
    i = np.arange(seq, dtype=np.float64)
    c = (1.0 - np.exp((i[None, :] + 1.0) * lam[:, None])) / (1.0 - np.exp(lam))[:, None]
    rfq = np.exp((i % 512)[None, :] * lam[:, None]) / np.sqrt(c)
    rfk = np.exp(-(i % 128)[None, :] * lam[:, None]) / np.sqrt(128.0)
    lin = np.linspace(0.0, 1.0, 64)
    theta = 1.0 / 10000.0 ** lin
    ang = np.outer(i, theta)
    cosv, sinv = np.cos(ang), np.sin(ang)
    t = {}
    t["t_l1_cosq"] = (cosv[None] * rfq[:, :, None]).astype(np.float32)
    t["t_l1_sinq"] = (sinv[None] * rfq[:, :, None]).astype(np.float32)
    t["t_l1_cosk"] = (cosv[None] * rfk[:, :, None]).astype(np.float32)
    t["t_l1_sink"] = (sinv[None] * rfk[:, :, None]).astype(np.float32)
    nq = seq // 512
    ntk = seq // 128
    bsc = np.exp((512.0 * np.arange(nq)[None, :, None] - 128.0 * np.arange(ntk)[None, None, :])
                 * lam[:, None, None])
    return t, bsc


def emit_l1a(k, xin, xout):
    nc = k.nc
    seq, nt, nq = k.seq, k.nt, k.nq
    D = k.dram
    H = 8
    hscr, hscr2, vb = D["hscr"], D["hscr2"], D["vb"]
    _, bsc = l1a_host_tables(seq)
    EPS = 1e-5

    with contextlib.ExitStack() as ph:
        k.fence()
        ident = ph.enter_context(nc.sbuf_tensor("r_ident", [128, 128], BF16))
        ones_col = ph.enter_context(nc.sbuf_tensor("r_ones", [128, 1], BF16))
        trmask = ph.enter_context(nc.sbuf_tensor("r_trmask", [128, 4, 512], BF16))
        k.dma("sync", out=ident[:], in_=D["t_ident"][:, :])
        k.dma("sync", out=trmask[:], in_=D["t_trmask"].rearrange("e p q -> p e q"))
        k.op("vector", "memset", ap=ones_col[:], constant=1.0)
        stats = ph.enter_context(nc.sbuf_tensor("r_stats", [128, 12], F32))
        mvst = ph.enter_context(nc.sbuf_tensor("r_mv", [128, 4], F32))
        xnT = ph.enter_context(nc.sbuf_tensor("r_xnT", [128, 8, seq], BF16))

        with contextlib.ExitStack() as s1:
            k.fence()
            x_sb = s1.enter_context(nc.sbuf_tensor("r_x", [128, 2, DM], F32))
            xn = s1.enter_context(nc.sbuf_tensor("r_xn", [128, DM], BF16))
            last_st = None
            prev = {0: 0, 1: 0}
            for tc in range(nt):
                b = tc % 2
                k.wt("sync", "cV", prev[b])
                vl = k.dma("sync", out=x_sb[:, b, :], in_=xin[tc * 128:(tc + 1) * 128, :])
                k.wtd("vector", vl)
                k.wtd("vector", last_st)
                vn = layer_norm_tile(k, x_sb[:, b, :], xn[:], stats, mvst)
                prev[b] = vn
                k.wt("sync", "cV", vn)
                last_st = k.dma("sync", out=hscr[tc * 128:(tc + 1) * 128, 0:DM], in_=xn[:])
            k.wtd("sync", last_st)
            for c in range(8):
                k.dma("sync", out=xnT[:, c, :], in_=hscr[:, c * 128:(c + 1) * 128], transpose=True)
            k.fence(engines=("tensor",))

        with contextlib.ExitStack() as s2:
            k.fence()
            wqk = s2.enter_context(nc.sbuf_tensor("r_wqk", [128, 2, 8, 256], BF16))
            wv = s2.enter_context(nc.sbuf_tensor("r_wv", [128, 2, 8, 256], BF16))
            tblh = s2.enter_context(nc.sbuf_tensor("r_tbl", [128, nt, 4, 64], F32))
            psq = {tb: s2.enter_context(nc.psum_tensor(f"r_psq{tb}", [128, 256], F32))
                   for tb in range(2)}
            psv = {tb: s2.enter_context(nc.psum_tensor(f"r_psv{tb}", [128, 256], F32))
                   for tb in range(2)}
            qkf = s2.enter_context(nc.sbuf_tensor("r_qkf", [128, 2, 256], F32))
            qkb = s2.enter_context(nc.sbuf_tensor("r_qkb", [128, 2, 256], BF16))
            vbf = s2.enter_context(nc.sbuf_tensor("r_vbf", [128, 2, 256], BF16))
            scrE = s2.enter_context(nc.sbuf_tensor("r_scrE", [128, 128], F32))
            scrO = s2.enter_context(nc.sbuf_tensor("r_scrO", [128, 128], F32))
            scrE2 = s2.enter_context(nc.sbuf_tensor("r_scrE2", [128, 128], F32))
            scrO2 = s2.enter_context(nc.sbuf_tensor("r_scrO2", [128, 128], F32))
            wuse = {0: 0, 1: 0}
            st_qk = {0: None, 1: None}
            st_qk2 = {0: None, 1: None}
            st_v = {0: None, 1: None}
            qk_free = {0: 0, 1: 0}
            psq_free = {0: 0, 1: 0}
            psv_free = {0: 0, 1: 0}
            tbl_rd = 0
            tsrc = {0: "t_l1_cosq", 1: "t_l1_cosk", 2: "t_l1_sinq", 3: "t_l1_sink"}
            for h in range(H):
                b = h % 2
                k.wt("gpsimd", "cT", wuse[b])
                t1 = k.dma("gpsimd", out=wqk[:, b, :, 0:128],
                           in_=D["L1_attn_wq"].rearrange("(c p) n -> p c n", p=128)[:, :, h * 128:(h + 1) * 128])
                t2 = k.dma("gpsimd", out=wqk[:, b, :, 128:256],
                           in_=D["L1_attn_wk"].rearrange("(c p) n -> p c n", p=128)[:, :, h * 128:(h + 1) * 128])
                t3 = k.dma("gpsimd", out=wv[:, b],
                           in_=D["L1_attn_wv"].rearrange("(c p) n -> p c n", p=128)[:, :, h * 256:(h + 1) * 256])
                k.wt("sync", "cV", tbl_rd)
                tt = [k.dma("sync", out=tblh[:, :, j, :],
                            in_=D[tsrc[j]][h].rearrange("(t p) g -> p t g", p=128))
                      for j in range(4)]
                k.wtd("tensor", [t1, t2, t3])
                k.wtd("vector", tt)
                for tc in range(nt):
                    tb = tc % 2
                    r0, r1_ = tc * 128, (tc + 1) * 128
                    k.wt("tensor", "cV", psq_free[tb])
                    vm = None
                    for c in range(8):
                        vm = k.op("tensor", "matmul", out=psq[tb][:],
                                  lhsT=xnT[:, c, r0:r1_], rhs=wqk[:, b, c, :],
                                  start=(c == 0), stop=(c == 7))
                    k.wt("tensor", "cS", psv_free[tb])
                    vm2 = None
                    for c in range(8):
                        vm2 = k.op("tensor", "matmul", out=psv[tb][:],
                                   lhsT=xnT[:, c, r0:r1_], rhs=wv[:, b, c, :],
                                   start=(c == 0), stop=(c == 7))
                    wuse[b] = vm2
                    k.wt("vector", "cT", vm)
                    k.wt("vector", "cV", qk_free[tb])
                    vcp = k.op("vector", "tensor_copy", out=qkf[:, tb, :], in_=psq[tb][:])
                    psq_free[tb] = vcp
                    vr = rope_pairs(k, qkf[:, tb, :].rearrange("p (h2 g two) -> p h2 g two",
                                                               h2=2, two=2),
                                    tblh[:, tc, 0:2, :], tblh[:, tc, 2:4, :],
                                    scrE, scrO, scrE2, scrO2)
                    tbl_rd = vr
                    k.wtd("vector", st_qk[tb])
                    k.wtd("vector", st_qk2[tb])
                    vqk = k.op("vector", "tensor_copy", out=qkb[:, tb, :], in_=qkf[:, tb, :])
                    qk_free[tb] = vqk
                    k.wt("sync", "cV", vqk)
                    st_qk[tb] = k.dma("sync", out=hscr[r0:r1_, DM + h * 128:DM + (h + 1) * 128],
                                      in_=qkb[:, tb, 0:128])
                    st_qk2[tb] = k.dma("sync", out=hscr2[r0:r1_, h * 128:(h + 1) * 128],
                                       in_=qkb[:, tb, 128:256])
                    k.wt("scalar", "cT", vm2)
                    k.wtd("scalar", st_v[tb])
                    vv = k.op("scalar", "copy", out=vbf[:, tb, :], in_=psv[tb][:],
                              selfwait=False)
                    psv_free[tb] = vv
                    k.wt("sync", "cS", vv)
                    st_v[tb] = k.dma("sync", out=vb[r0:r1_, h * 256:(h + 1) * 256], in_=vbf[:, tb, :])
            k.wtd("sync", [x for d in (st_qk, st_qk2, st_v) for x in d.values() if x])

        with contextlib.ExitStack() as s3:
            k.fence()
            wg_sb = s3.enter_context(nc.sbuf_tensor("r_wg", [128, 2, 8, 256], BF16))
            wo_sb = s3.enter_context(nc.sbuf_tensor("r_wo", [128, 2, 2, DM], BF16))
            kTp = s3.enter_context(nc.sbuf_tensor("r_kT", [128, 2, seq], BF16))
            qTp = s3.enter_context(nc.sbuf_tensor("r_qT", [128, 2, 2, 512], BF16))
            v_sb = s3.enter_context(nc.sbuf_tensor("r_v", [128, 2, nt, 256], BF16))
            ssb = s3.enter_context(nc.sbuf_tensor("r_ssb", [128, 2, 512], BF16))
            osb = s3.enter_context(nc.sbuf_tensor("r_osb", [128, 4, DM], BF16))
            dsb = s3.enter_context(nc.sbuf_tensor("r_dsb", [128, 4, 4], F32))
            gsb = s3.enter_context(nc.sbuf_tensor("r_gsb", [128, 2, 8, 256], BF16))
            densb = s3.enter_context(nc.sbuf_tensor("r_den", [128, 8], F32))
            dtmp = s3.enter_context(nc.sbuf_tensor("r_dtmp", [128, 8, 4], F32))
            sqs = s3.enter_context(nc.sbuf_tensor("r_sqs", [128, 1024], F32))
            ysub = s3.enter_context(nc.sbuf_tensor("r_ysub", [128, 256], BF16))
            yT = s3.enter_context(nc.sbuf_tensor("r_yT", [128, 256], BF16))
            acc = s3.enter_context(nc.sbuf_tensor("r_acc", [128, 8, DM], F32))
            xr = s3.enter_context(nc.sbuf_tensor("r_xr", [128, 2, DM], F32))
            st2 = s3.enter_context(nc.psum_tensor("r_st2", [128, 1024], F32))
            denp = s3.enter_context(nc.psum_tensor("r_denp", [128, 512], F32))
            outp = s3.enter_context(nc.psum_tensor("r_outp", [128, 1024], F32))
            ptp = s3.enter_context(nc.psum_tensor("r_ptp", [128, 256], BF16))
            wop = s3.enter_context(nc.psum_tensor("r_wop", [128, 1024], F32))

            items = [(pp, h) for pp in range(nq // 2) for h in range(H)]
            NI = len(items)
            ld = {0: None, 1: None}
            free_w = {0: 0, 1: 0}
            free_kq = {0: 0, 1: 0}
            free_v = {0: 0, 1: 0}
            slot_free = [0, 0, 0, 0]
            wgr = D["L1_attn_wg"].rearrange("(c p) n -> p c n", p=128)

            def emit_loads(i):
                pp, h = items[i]
                b = i % 2
                ntk_hi = 8 * pp + 8
                k.wt("gpsimd", "cT", free_w[b])
                t_wg = k.dma("gpsimd", out=wg_sb[:, b], in_=wgr[:, :, h * 256:(h + 1) * 256])
                t_wo = k.dma("gpsimd", out=wo_sb[:, b],
                             in_=D["L1_attn_wo"][h * 256:(h + 1) * 256].rearrange("(c p) n -> p c n", p=128))
                k.wt("sync", "cT", max(free_kq[b], free_v[b]))
                t_k = k.dma("sync", out=kTp[:, b, 0:ntk_hi * 128],
                            in_=hscr2[0:ntk_hi * 128, h * 128:(h + 1) * 128], transpose=True)
                qts = [t_k]
                for qi in range(2):
                    qc = 2 * pp + qi
                    qts.append(k.dma("sync", out=qTp[:, b, qi],
                                     in_=hscr[qc * 512:(qc + 1) * 512, DM + h * 128:DM + (h + 1) * 128],
                                     transpose=True))
                t_v = k.dma("sync", out=v_sb[:, b, 0:ntk_hi, :],
                            in_=vb[0:ntk_hi * 128, h * 256:(h + 1) * 256].rearrange(
                                "(t p) d -> p t d", p=128))
                ld[b] = {"w": [t_wg, t_wo], "kq": qts, "v": t_v}

            gsb_free = {0: 0, 1: 0}
            ev_done = {0: ("cS", 0), 1: ("cS", 0)}
            pv_done = {0: 0, 1: 0}
            outp_free = 0
            den_free = {0: 0, 1: 0}
            pcount = 0
            st_x = {0: None, 1: None}

            def emit_A(i):
                nonlocal outp_free, pcount
                pp, h = items[i]
                b = i % 2
                k.wtd("tensor", ld[b]["kq"])
                k.wtd("tensor", ld[b]["v"])
                k.wtd("tensor", ld[b]["w"])
                gp_units = [(qi, s) for qi in range(2) for s in range(4)]
                gpi = [0]

                def emit_gp():
                    nonlocal pcount
                    if gpi[0] >= len(gp_units):
                        return
                    qi, s = gp_units[gpi[0]]
                    u = qi * 4 + s
                    gpi[0] += 1
                    qc = 2 * pp + qi
                    p = pcount % 2
                    pcount += 1
                    sem, val = ev_done[p]
                    if val:
                        k.wt("tensor", sem, val)
                    vgp = None
                    for c in range(8):
                        vgp = k.op("tensor", "matmul", out=st2[:, p * 512:p * 512 + 256],
                                   lhsT=xnT[:, c, qc * 512 + s * 128:qc * 512 + (s + 1) * 128],
                                   rhs=wg_sb[:, b, c, :], start=(c == 0), stop=(c == 7),
                                   skip_group_check=True)
                    k.wt("scalar", "cT", vgp)
                    k.wt("scalar", "cV", gsb_free[b])
                    vgs = k.op("scalar", "activation", out=gsb[:, b, u, :],
                               in_=st2[:, p * 512:p * 512 + 256], func=AF.Silu,
                               selfwait=False)
                    ev_done[p] = ("cS", vgs)

                for qi in range(2):
                    qc = 2 * pp + qi
                    slot = b * 2 + qi
                    keep = [kt for kt in range(4 * qc + 4) if bsc[h, qc, kt] >= 1e-25]
                    first_den = [True]
                    fbank = {0: True, 1: True}
                    vpv = [0]
                    pend = None

                    def emit_pvden(pd, qi=qi):
                        p2, kt2, gsem2, vev2 = pd
                        k.wt("tensor", gsem2, vev2)
                        if fbank[0]:
                            k.wt("tensor", "cS", outp_free)
                            k.wt("tensor", "cS", den_free[qi])
                        for s in range(4):
                            k.op("tensor", "matmul", out=denp[:, qi * 4 + s:qi * 4 + s + 1],
                                 lhsT=ssb[:, p2, s * 128:(s + 1) * 128], rhs=ones_col[:],
                                 start=first_den[0], stop=False, skip_group_check=True)
                            first_den[0] = False
                            vpv[0] = k.op("tensor", "matmul", out=outp[:, s * 256:(s + 1) * 256],
                                          lhsT=ssb[:, p2, s * 128:(s + 1) * 128],
                                          rhs=v_sb[:, b, kt2, :],
                                          start=fbank[s // 2], stop=False, skip_group_check=True)
                            fbank[s // 2] = False
                        pv_done[p2] = vpv[0]

                    for bi, kt in enumerate(keep):
                        p = pcount % 2
                        pcount += 1
                        sl = st2[:, p * 512:(p + 1) * 512]
                        sem, val = ev_done[p]
                        if val:
                            k.wt("tensor", sem, val)
                        vmm = k.op("tensor", "matmul", out=sl,
                                   lhsT=kTp[:, b, kt * 128:(kt + 1) * 128],
                                   rhs=qTp[:, b, qi], start=True, stop=True)
                        scale = float(bsc[h, qc, kt])
                        e = kt - 4 * qc
                        if e >= 0:
                            k.wt("vector", "cT", vmm)
                            k.wt("vector", "cT", pv_done[p])
                            vev = k.op("vector", "scalar_tensor_tensor", out=ssb[:, p, :], in0=sl,
                                       scalar=scale, in1=trmask[:, e, :], op0=ALU.mult, op1=ALU.mult,
                                       selfwait=False)
                            ev_done[p] = ("cV", vev)
                            gsem = "cV"
                        else:
                            k.wt("scalar", "cT", vmm)
                            k.wt("scalar", "cT", pv_done[p])
                            vev = k.op("scalar", "activation", out=ssb[:, p, :], in_=sl,
                                       func=AF.Copy, scale=scale, selfwait=False)
                            ev_done[p] = ("cS", vev)
                            gsem = "cS"
                        if pend is not None:
                            emit_pvden(pend)
                        pend = (p, kt, gsem, vev)
                        if bi % 3 == 2:
                            emit_pvden(pend)
                            pend = None
                            emit_gp()
                    if pend is not None:
                        emit_pvden(pend)
                    while gpi[0] < (qi + 1) * 4:
                        emit_gp()
                    k.wt("scalar", "cT", vpv[0])
                    k.wt("scalar", "cV", slot_free[slot])
                    k.op("scalar", "activation", out=dsb[:, slot, :], in_=denp[:, qi * 4:qi * 4 + 4],
                         func=AF.Abs)
                    den_free[qi] = k.sems["cS"].n
                    outp_free = k.op("scalar", "copy", out=osb[:, slot, :], in_=outp[:])
                free_kq[b] = k.sems["cT"].n
                free_v[b] = k.sems["cT"].n
                return outp_free

            acc_rd = 0

            def emit_B(j, ready_cs):
                pp, h = items[j]
                bj = j % 2
                vwo_last = 0
                for qi in range(2):
                    qc = 2 * pp + qi
                    slot = bj * 2 + qi
                    k.wt("vector", "cS", ready_cs)
                    vcl = k.op("vector", "tensor_scalar", out=densb[:, qi * 4:qi * 4 + 4],
                               in0=dsb[:, slot, :], scalar1=1.0, scalar2=None, op0=ALU.max)
                    vsq = k.op("scalar", "activation", out=sqs[:], in_=osb[:, slot, :],
                               func=AF.Square)
                    k.wt("vector", "cS", vsq)
                    k.op("vector", "tensor_reduce", out=dtmp[:, qi, 0:4],
                         in_=sqs[:].rearrange("p (u d) -> p u d", u=4), axis=AX.X, op=ALU.add)
                    k.op("vector", "tensor_scalar", out=dtmp[:, qi, 0:4], in0=dtmp[:, qi, 0:4],
                         scalar1=1.0 / 256, scalar2=None, op0=ALU.mult)
                    k.op("vector", "tensor_tensor", out=dtmp[:, qi + 2, 0:4],
                         in0=densb[:, qi * 4:qi * 4 + 4], in1=densb[:, qi * 4:qi * 4 + 4],
                         op=ALU.mult)
                    vd2 = k.op("vector", "scalar_tensor_tensor", out=dtmp[:, qi + 4, 0:4],
                               in0=dtmp[:, qi + 2, 0:4], scalar=EPS, in1=dtmp[:, qi, 0:4],
                               op0=ALU.mult, op1=ALU.add)
                    k.wt("scalar", "cV", vd2)
                    vln = k.op("scalar", "activation", out=dtmp[:, qi + 6, 0:4],
                               in_=dtmp[:, qi + 4, 0:4], func=AF.Ln)
                    k.wt("scalar", "cS", vln)
                    vrs = k.op("scalar", "activation", out=dtmp[:, qi + 6, 0:4],
                               in_=dtmp[:, qi + 6, 0:4], func=AF.Exp, scale=-0.5)
                    k.wt("vector", "cS", vrs)
                    vy_slot = 0
                    for s in range(4):
                        u = qi * 4 + s
                        t = qi * 4 + s
                        vy = k.op("vector", "scalar_tensor_tensor", out=ysub[:],
                                  in0=osb[:, slot, s * 256:(s + 1) * 256],
                                  scalar=dtmp[:, qi + 6, s:s + 1],
                                  in1=gsb[:, bj, u, :], op0=ALU.mult, op1=ALU.mult)
                        vy_slot = vy
                        vevt = None
                        for cc in range(2):
                            k.wt("tensor", "cV", vy if cc == 0 else vevt)
                            vtp = k.op("tensor", "transpose", out=ptp[:, cc * 128:(cc + 1) * 128],
                                       in_=ysub[:, cc * 128:(cc + 1) * 128], identity=ident[:])
                            k.wt("vector", "cT", vtp)
                            vevt = k.op("vector", "tensor_copy", out=yT[:, cc * 128:(cc + 1) * 128],
                                        in_=ptp[:, cc * 128:(cc + 1) * 128])
                        k.wt("tensor", "cV", vevt)
                        if h == 0:
                            k.wt("tensor", "cV", acc_rd)
                        vmo = None
                        for cc in range(2):
                            for n0 in (0, 512):
                                vmo = k.op("tensor", "matmul", out=wop[:, n0:n0 + 512],
                                           lhsT=yT[:, cc * 128:(cc + 1) * 128],
                                           rhs=wo_sb[:, bj, cc, n0:n0 + 512],
                                           start=(cc == 0), stop=(cc == 1))
                        vwo_last = vmo
                        k.wt("vector", "cT", vmo)
                        if h == 0:
                            k.op("vector", "tensor_copy", out=acc[:, t, :], in_=wop[:])
                        else:
                            k.op("vector", "tensor_tensor", out=acc[:, t, :], in0=acc[:, t, :],
                                 in1=wop[:], op=ALU.add)
                    slot_free[slot] = max(vy_slot, vcl)
                    gsb_free[bj] = max(gsb_free[bj], vy_slot)
                free_w[bj] = vwo_last

            def emit_final(pp):
                nonlocal acc_rd
                for t in range(8):
                    fb = t % 2
                    qc = 2 * pp + t // 4
                    s = t % 4
                    k.wtd("sync", st_x[fb])
                    vxl = k.dma("sync", out=xr[:, fb, :],
                                in_=xin[qc * 512 + s * 128:qc * 512 + (s + 1) * 128, :])
                    k.wtd("vector", vxl)
                    k.wtd("vector", st_x[fb])
                    vadd = k.op("vector", "tensor_tensor", out=xr[:, fb, :], in0=acc[:, t, :],
                                in1=xr[:, fb, :], op=ALU.add)
                    acc_rd = vadd
                    k.wt("sync", "cV", vadd)
                    st_x[fb] = k.dma("sync", out=xout[qc * 512 + s * 128:qc * 512 + (s + 1) * 128, :],
                                     in_=xr[:, fb, :])

            emit_loads(0)
            prev_ready = None
            for i in range(NI):
                ready = emit_A(i)
                if i > 0:
                    emit_B(i - 1, prev_ready)
                    if items[i - 1][1] == H - 1:
                        emit_final(items[i - 1][0])
                prev_ready = ready
                if i + 1 < NI:
                    emit_loads(i + 1)
            emit_B(NI - 1, prev_ready)
            emit_final(items[NI - 1][0])
        k.fence()


WSPECS_L1A = [
    ("L1_attn_wq", (DM, DM), F32),
    ("L1_attn_wk", (DM, DM), F32),
    ("L1_attn_wv", (DM, 2048), F32),
    ("L1_attn_wg", (DM, 2048), F32),
    ("L1_attn_wo", (2048, DM), F32),
]


def tspecs_l1a(seq):
    return [
        ("t_ident", (128, 128), BF16),
        ("t_trmask", (4, 128, 512), BF16),
        ("t_l1_cosq", (8, seq, 64), F32),
        ("t_l1_sinq", (8, seq, 64), F32),
        ("t_l1_cosk", (8, seq, 64), F32),
        ("t_l1_sink", (8, seq, 64), F32),
    ]


def build_l1a(seq):
    nc = bass.Bass("TRN2", target_bir_lowering=False)
    es = contextlib.ExitStack()
    with es:
        k = K(nc, es, seq)
        D = k.dram
        D["xin"] = nc.dram_tensor("xin", [seq, DM], F32, kind="ExternalInput").ap()
        for nm, shp, dt in WSPECS_L1A + tspecs_l1a(seq):
            D[nm] = nc.dram_tensor(nm, list(shp), dt, kind="ExternalInput").ap()
        D["hscr"] = nc.dram_tensor("hscr", [seq, 2048], BF16, kind="Internal").ap()
        D["hscr2"] = nc.dram_tensor("hscr2", [seq, 1024], BF16, kind="Internal").ap()
        D["vb"] = nc.dram_tensor("vb", [seq, 2048], BF16, kind="Internal").ap()
        D["xout"] = nc.dram_tensor("xout", [seq, DM], F32, kind="ExternalOutput").ap()
        emit_l1a(k, D["xin"], D["xout"])
        k.build()
    return nc



def emit_l1f(k, xin, xout):
    nc = k.nc
    seq, nt, nq = k.seq, k.nt, k.nq
    D = k.dram
    NL = 16
    hscr, qTb, vb = D["hscr"], D["qTb"], D["vb"]

    with contextlib.ExitStack() as ph:
        k.fence()
        ident = ph.enter_context(nc.sbuf_tensor("e_ident", [128, 128], BF16))
        k.dma("sync", out=ident[:], in_=D["t_ident"][:, :])
        stats = ph.enter_context(nc.sbuf_tensor("e_stats", [128, 12], F32))
        mvst = ph.enter_context(nc.sbuf_tensor("e_mv", [128, 4], F32))
        kT = ph.enter_context(nc.sbuf_tensor("e_kT", [128, 8, seq], BF16))

        with contextlib.ExitStack() as sff:
            xnT = sff.enter_context(nc.sbuf_tensor("e_xnT", [128, 8, seq], BF16))

            with contextlib.ExitStack() as s1:
                k.fence()
                x_sb = s1.enter_context(nc.sbuf_tensor("e_x", [128, 2, DM], F32))
                xn = s1.enter_context(nc.sbuf_tensor("e_xn", [128, DM], BF16))
                last_st = None
                prev = {0: 0, 1: 0}
                for tc in range(nt):
                    b = tc % 2
                    k.wt("sync", "cV", prev[b])
                    vl = k.dma("sync", out=x_sb[:, b, :], in_=xin[tc * 128:(tc + 1) * 128, :])
                    k.wtd("vector", vl)
                    k.wtd("vector", last_st)
                    vn = layer_norm_tile(k, x_sb[:, b, :], xn[:], stats, mvst)
                    prev[b] = vn
                    k.wt("sync", "cV", vn)
                    last_st = k.dma("sync", out=hscr[tc * 128:(tc + 1) * 128, 0:DM], in_=xn[:])
                k.wtd("sync", last_st)
                vxt = []
                for c in range(8):
                    vxt.append(k.dma("sync", out=xnT[:, c, :], in_=hscr[:, c * 128:(c + 1) * 128],
                                     transpose=True))
                k.wtd("tensor", vxt)

            for pref in ("q", "v", "k"):
                with contextlib.ExitStack() as s2:
                    k.fence()
                    nw = s2.enter_context(nc.sbuf_tensor("e_nw_" + pref, [15, DM], BF16))
                    nwT = s2.enter_context(nc.sbuf_tensor("e_nwT_" + pref, [128, 8, 16], BF16))
                    w1 = s2.enter_context(nc.sbuf_tensor("e_w1_" + pref, [128, 8, NL, 64], BF16))
                    w2 = s2.enter_context(nc.sbuf_tensor("e_w2_" + pref, [128, 8, DM], BF16))
                    b2 = s2.enter_context(nc.sbuf_tensor("e_b2_" + pref, [16, DM], BF16))
                    selT = s2.enter_context(nc.sbuf_tensor("e_selT_" + pref, [16, seq], BF16))
                    nbb, _ = bcast_row(k, "e_nb_" + pref, D[f"L1_ffn_{pref}_node_b"][None, :], 15, s2)
                    b1b, _ = bcast_row(k, "e_b1_" + pref,
                                       D[f"L1_ffn_{pref}_b1"].rearrange("l h -> (l h)")[None, :], DM, s2)
                    iot, viot = bcast_row(k, "e_iota_" + pref, D["t_iota16"][None, :], 16, s2)
                    v1 = k.dma("gpsimd", out=nw[:], in_=D[f"L1_ffn_{pref}_node_w"][:, :])
                    v2 = []
                    w1r = D[f"L1_ffn_{pref}_w1"].rearrange("l (c p) h -> p c l h", p=128)
                    for c in range(8):
                        v2.append(k.dma("gpsimd", out=w1[:, c], in_=w1r[:, c]))
                    v3 = k.dma("gpsimd", out=w2[:],
                               in_=D[f"L1_ffn_{pref}_w2"].rearrange("l h n -> (l h) n")
                               .rearrange("(c p) n -> p c n", p=128))
                    v4 = k.dma("gpsimd", out=b2[:], in_=D[f"L1_ffn_{pref}_b2"][:, :])
                    psn = {j: s2.enter_context(nc.psum_tensor(f"e_psn{j}_" + pref, [128, 16], F32))
                           for j in range(2)}
                    psh = s2.enter_context(nc.psum_tensor("e_psh_" + pref, [128, 1024], F32))
                    pso = {j: s2.enter_context(nc.psum_tensor(f"e_pso{j}_" + pref, [128, 512], F32))
                           for j in range(2)}
                    ptp = s2.enter_context(nc.psum_tensor("e_ptp_" + pref, [128, 1024], BF16))
                    ptq = s2.enter_context(nc.psum_tensor("e_ptq_" + pref, [16, 128], BF16))
                    sc_sb = s2.enter_context(nc.sbuf_tensor("e_sc_" + pref, [128, 16], F32))
                    tr_sb = s2.enter_context(nc.sbuf_tensor("e_tr_" + pref, [128, 12], F32))
                    sel = s2.enter_context(nc.sbuf_tensor("e_sel_" + pref, [128, 16], F32))
                    selb = s2.enter_context(nc.sbuf_tensor("e_selb_" + pref, [128, 16], BF16))
                    h1 = s2.enter_context(nc.sbuf_tensor("e_h1_" + pref, [128, DM], F32))
                    hb = s2.enter_context(nc.sbuf_tensor("e_hb_" + pref, [128, DM], BF16))
                    hT = s2.enter_context(nc.sbuf_tensor("e_hT_" + pref, [128, 8, 512], BF16))
                    ob = s2.enter_context(nc.sbuf_tensor("e_ob_" + pref, [128, 2, 512], BF16))
                    k.wtd("tensor", [v1, v3, v4] + v2)
                    k.wt("tensor", "cV", viot)
                    vnt = 0
                    for c in range(8):
                        k.wt("tensor", "cV", vnt)
                        vtp = k.op("tensor", "transpose", out=ptp[:, 0:15],
                                   in_=nw[:, c * 128:(c + 1) * 128], identity=ident[0:15, 0:15])
                        k.wt("vector", "cT", vtp)
                        vnt = k.op("vector", "tensor_copy", out=nwT[:, c, 0:15], in_=ptp[:, 0:15])
                    k.wt("tensor", "cV", vnt)

                    st_sel = 0
                    vhb_used = 0
                    vh1_used = 0
                    vpsn_free = {0: 0, 1: 0}
                    vob_st = {0: None, 1: None}
                    pso_free = {0: 0, 1: 0}
                    vsel_used = 0
                    vhT_used = 0
                    for ch in range(seq // 512):
                        for ti in range(4):
                            tc = ch * 4 + ti
                            r0, r1_ = tc * 128, (tc + 1) * 128
                            pslot = tc % 2
                            k.wt("tensor", "cV", vpsn_free[pslot])
                            vm = None
                            for c in range(8):
                                vm = k.op("tensor", "matmul", out=psn[pslot][:, 0:15],
                                          lhsT=xnT[:, c, r0:r1_], rhs=nwT[:, c, 0:15],
                                          start=(c == 0), stop=(c == 7))
                            k.wt("tensor", "cV", vh1_used)
                            vm2 = None
                            for half in range(2):
                                for c in range(8):
                                    vm2 = k.op("tensor", "matmul", out=psh[:, half * 512:(half + 1) * 512],
                                               lhsT=xnT[:, c, r0:r1_],
                                               rhs=w1[:, c, half * 8:(half + 1) * 8, :].rearrange(
                                                   "p l h -> p (l h)"),
                                               start=(c == 0), stop=(c == 7))
                            k.wt("vector", "cT", vm)
                            vpsn_free[pslot] = k.op("vector", "tensor_tensor", out=sc_sb[:, 0:15],
                                                    in0=psn[pslot][:, 0:15],
                                                    in1=nbb[:, 0:15], op=ALU.add)
                            k.wt("vector", "cT", vm2)
                            vh1 = k.op("vector", "tensor_tensor", out=h1[:], in0=psh[:], in1=b1b[:],
                                       op=ALU.add)
                            vh1_used = vh1
                            k.op("vector", "tensor_scalar", out=tr_sb[:, 0:1], in0=sc_sb[:, 0:1],
                                 scalar1=0.0, scalar2=None, op0=ALU.is_ge)
                            k.op("vector", "tensor_tensor", out=tr_sb[:, 7:8], in0=sc_sb[:, 2:3],
                                 in1=sc_sb[:, 1:2], op=ALU.subtract)
                            k.op("vector", "tensor_tensor", out=tr_sb[:, 8:9], in0=tr_sb[:, 0:1],
                                 in1=tr_sb[:, 7:8], op=ALU.mult)
                            k.op("vector", "tensor_tensor", out=tr_sb[:, 9:10], in0=sc_sb[:, 1:2],
                                 in1=tr_sb[:, 8:9], op=ALU.add)
                            k.op("vector", "tensor_scalar", out=tr_sb[:, 1:2], in0=tr_sb[:, 9:10],
                                 scalar1=0.0, scalar2=None, op0=ALU.is_ge)
                            k.op("vector", "scalar_tensor_tensor", out=tr_sb[:, 4:5], in0=tr_sb[:, 0:1],
                                 scalar=2.0, in1=tr_sb[:, 1:2], op0=ALU.mult, op1=ALU.add)
                            bc4 = tr_sb[:, 4:5].broadcast_to([128, 4])
                            k.op("vector", "tensor_tensor", out=tr_sb[:, 7:11], in0=bc4,
                                 in1=iot[:, 0:4], op=ALU.is_equal)
                            k.op("vector", "tensor_tensor", out=tr_sb[:, 7:11],
                                 in0=tr_sb[:, 7:11], in1=sc_sb[:, 3:7], op=ALU.mult)
                            k.op("vector", "tensor_reduce", out=tr_sb[:, 9:10],
                                 in_=tr_sb[:, 7:11], axis=AX.X, op=ALU.add)
                            k.op("vector", "tensor_scalar", out=tr_sb[:, 2:3], in0=tr_sb[:, 9:10],
                                 scalar1=0.0, scalar2=None, op0=ALU.is_ge)
                            k.op("vector", "scalar_tensor_tensor", out=tr_sb[:, 5:6], in0=tr_sb[:, 4:5],
                                 scalar=2.0, in1=tr_sb[:, 2:3], op0=ALU.mult, op1=ALU.add)
                            k.wt("vector", "cT", vsel_used)
                            bc8 = tr_sb[:, 5:6].broadcast_to([128, 8])
                            k.op("vector", "tensor_tensor", out=sel[:, 0:8], in0=bc8,
                                 in1=iot[:, 0:8], op=ALU.is_equal)
                            k.op("vector", "tensor_tensor", out=sel[:, 8:16],
                                 in0=sel[:, 0:8], in1=sc_sb[:, 7:15], op=ALU.mult)
                            k.op("vector", "tensor_reduce", out=tr_sb[:, 9:10],
                                 in_=sel[:, 8:16], axis=AX.X, op=ALU.add)
                            k.op("vector", "tensor_scalar", out=tr_sb[:, 3:4], in0=tr_sb[:, 9:10],
                                 scalar1=0.0, scalar2=None, op0=ALU.is_ge)
                            k.op("vector", "scalar_tensor_tensor", out=tr_sb[:, 6:7], in0=tr_sb[:, 5:6],
                                 scalar=2.0, in1=tr_sb[:, 3:4], op0=ALU.mult, op1=ALU.add)
                            bc16 = tr_sb[:, 6:7].broadcast_to([128, 16])
                            vsel = k.op("vector", "tensor_tensor", out=sel[:], in0=bc16,
                                        in1=iot[:], op=ALU.is_equal)
                            vselb = k.op("vector", "tensor_copy", out=selb[:], in_=sel[:])
                            k.wt("tensor", "cV", vselb)
                            vtq = k.op("tensor", "transpose", out=ptq[:],
                                       in_=selb[:], identity=ident[:, :])
                            vsel_used = vtq
                            k.wt("vector", "cT", vtq)
                            k.op("vector", "tensor_copy", out=selT[:, r0:r1_], in_=ptq[:])
                            k.wt("vector", "cT", vhb_used)
                            vhb = None
                            for l in range(NL):
                                vhb = k.op("vector", "tensor_scalar", out=hb[:, l * 64:(l + 1) * 64],
                                           in0=h1[:, l * 64:(l + 1) * 64], scalar1=sel[:, l:l + 1],
                                           scalar2=0.0, op0=ALU.mult, op1=ALU.max,
                                           selfwait=(l == NL - 1))
                            vev = None
                            for c in range(8):
                                k.wt("tensor", "cV", vhb if c == 0 else vev)
                                k.wt("tensor", "cT", vhT_used) if c == 0 else None
                                vtp = k.op("tensor", "transpose",
                                           out=ptp[:, c * 128:(c + 1) * 128],
                                           in_=hb[:, c * 128:(c + 1) * 128], identity=ident[:])
                                vhb_used = vtp
                                k.wt("vector", "cT", vtp)
                                vev = k.op("vector", "tensor_copy",
                                           out=hT[:, c, ti * 128:(ti + 1) * 128],
                                           in_=ptp[:, c * 128:(c + 1) * 128])
                            k.wt("tensor", "cV", vev)
                        ch0 = ch * 512
                        if pref == "v":
                            for ti in range(4):
                                r0 = ch0 + ti * 128
                                for n0 in (0, 512):
                                    oslot = (ti * 2 + n0 // 512) % 2
                                    k.wt("tensor", "cV", pso_free[oslot])
                                    vmo = None
                                    for c in range(8):
                                        vmo = k.op("tensor", "matmul", out=pso[oslot][:],
                                                   lhsT=hT[:, c, ti * 128:(ti + 1) * 128],
                                                   rhs=w2[:, c, n0:n0 + 512],
                                                   start=(c == 0), stop=False)
                                    vmo = k.op("tensor", "matmul", out=pso[oslot][:],
                                               lhsT=selT[:, r0:r0 + 128],
                                               rhs=b2[:, n0:n0 + 512], start=False, stop=True)
                                    vhT_used = vmo
                                    k.wt("vector", "cT", vmo)
                                    k.wtd("vector", vob_st[oslot])
                                    vo = k.op("vector", "tensor_copy", out=ob[:, oslot, :],
                                              in_=pso[oslot][:], selfwait=False)
                                    pso_free[oslot] = vo
                                    k.wt("sync", "cV", vo)
                                    vob_st[oslot] = k.dma("sync", out=vb[r0:r0 + 128, n0:n0 + 512],
                                                          in_=ob[:, oslot, :])
                        else:
                            for oc in range(8):
                                oslot = oc % 2
                                k.wt("tensor", "cV", pso_free[oslot])
                                vmo = None
                                for c in range(8):
                                    vmo = k.op("tensor", "matmul", out=pso[oslot][:],
                                               lhsT=w2[:, c, oc * 128:(oc + 1) * 128],
                                               rhs=hT[:, c, :], start=(c == 0), stop=False)
                                vmo = k.op("tensor", "matmul", out=pso[oslot][:],
                                           lhsT=b2[:, oc * 128:(oc + 1) * 128],
                                           rhs=selT[:, ch0:ch0 + 512], start=False, stop=True)
                                vhT_used = vmo
                                k.wt("vector", "cT", vmo)
                                k.wtd("vector", vob_st[oslot])
                                if pref == "q":
                                    vo = k.op("vector", "tensor_scalar", out=ob[:, oslot, :],
                                              in0=pso[oslot][:], scalar1=1.0 / 32.0, scalar2=None,
                                              op0=ALU.mult, selfwait=False)
                                    pso_free[oslot] = vo
                                    k.wt("sync", "cV", vo)
                                    vob_st[oslot] = k.dma("sync", out=qTb[:, oc, ch0:ch0 + 512],
                                                          in_=ob[:, oslot, :])
                                else:
                                    vo = k.op("vector", "tensor_copy", out=kT[:, oc, ch0:ch0 + 512],
                                              in_=pso[oslot][:], selfwait=False)
                                    pso_free[oslot] = vo
                                    vob_st[oslot] = None
                    k.fence()

        with contextlib.ExitStack() as s3:
            k.fence()
            v_sb = s3.enter_context(nc.sbuf_tensor("e_v", [128, nt, 1032], BF16))
            vzero = k.op("vector", "memset", ap=v_sb[:, :, 1024:1032], constant=0.0)
            k.op("vector", "memset", ap=v_sb[:, :, 1024:1025], constant=1.0)
            vvl = k.dma("sync", out=v_sb[:, :, 0:1024], in_=vb.rearrange("(t p) n -> p t n", p=128))
            qTc = s3.enter_context(nc.sbuf_tensor("e_qTc", [128, 2, 8, 256], BF16))
            pts = s3.enter_context(nc.sbuf_tensor("e_pts", [128, 3, 256], BF16))
            mbias = s3.enter_context(nc.sbuf_tensor("e_mbias", [128, 1], F32))
            vmb = k.op("vector", "memset", ap=mbias[:], constant=-12.0)
            rv = s3.enter_context(nc.sbuf_tensor("e_rv", [128, 2], F32))
            xr = s3.enter_context(nc.sbuf_tensor("e_xr", [128, 2, DM], F32))
            xnew = s3.enter_context(nc.sbuf_tensor("e_xnew", [128, 2, DM], F32))
            sc = {j: s3.enter_context(nc.psum_tensor(f"e_sc{j}", [128, 256], F32))
                  for j in range(2)}
            y = {j: s3.enter_context(nc.psum_tensor(f"e_y{j}", [128, 1032], F32))
                 for j in range(2)}
            k.wtd("tensor", vvl)
            k.wt("scalar", "cV", vmb)
            k.wt("tensor", "cV", vzero)
            scfree = [None, None]
            ptsfree = [None, None, None]
            ptsi = [0]
            yfree = [None, None]
            qld = {0: None, 1: None}
            qfree = {0: 0, 1: 0}
            st_x = {0: None, 1: None}
            xprev = {0: 0, 1: 0}
            NCH = seq // 256
            for cc in range(NCH):
                b = cc % 2
                q0 = cc * 256
                k.wt("sync", "cT", qfree[b])
                qld[b] = k.dma("sync", out=qTc[:, b], in_=qTb[:, :, q0:q0 + 256])
                k.wtd("tensor", qld[b])

                def emit_pv(p):
                    ps2, kt2, gate2 = p
                    k.wt("tensor", *gate2)
                    vpv = None
                    for qs2 in range(2):
                        if kt2 == 0 and yfree[qs2]:
                            k.wt("tensor", *yfree[qs2])
                        for n0 in (0, 512, 1024):
                            n1 = min(n0 + 512, 1032)
                            vpv = k.op("tensor", "matmul", out=y[qs2][:, n0:n1],
                                       lhsT=pts[:, ps2, qs2 * 128:(qs2 + 1) * 128],
                                       rhs=v_sb[:, kt2, n0:n1],
                                       start=(kt2 == 0), stop=(kt2 == nt - 1),
                                       skip_group_check=True)
                    ptsfree[ps2] = ("cT", vpv)
                    return vpv

                pend = None
                for kt in range(nt):
                    ss = kt % 2
                    if scfree[ss]:
                        k.wt("tensor", *scfree[ss])
                    vmm = None
                    for c in range(8):
                        vmm = k.op("tensor", "matmul", out=sc[ss][:],
                                   lhsT=kT[:, c, kt * 128:(kt + 1) * 128],
                                   rhs=qTc[:, b, c, :], start=(c == 0), stop=(c == 7))
                    qfree[b] = vmm
                    ps = ptsi[0] % 3
                    ptsi[0] += 1
                    if ptsfree[ps]:
                        k.wt("scalar", *ptsfree[ps])
                    k.wt("scalar", "cT", vmm)
                    vexp = k.op("scalar", "activation", out=pts[:, ps, :], in_=sc[ss][:],
                                func=AF.Exp, bias=mbias[:], scale=1.0, selfwait=False)
                    scfree[ss] = ("cS", vexp)
                    if pend is not None:
                        emit_pv(pend)
                    pend = (ps, kt, ("cS", vexp))
                last_pv = emit_pv(pend)
                for qs2 in range(2):
                    tok0 = q0 + qs2 * 128
                    xb = qs2
                    k.wt("sync", "cV", xprev[xb])
                    vxl = k.dma("sync", out=xr[:, xb, :], in_=xin[tok0:tok0 + 128, :])
                    k.wt("vector", "cT", last_pv)
                    k.op("vector", "reciprocal", out=rv[:, qs2:qs2 + 1],
                         in_=y[qs2][:, 1024:1025])
                    k.wtd("vector", vxl)
                    k.wtd("vector", st_x[xb])
                    vfin = k.op("vector", "scalar_tensor_tensor", out=xnew[:, xb, :],
                                in0=y[qs2][:, 0:1024], scalar=rv[:, qs2:qs2 + 1],
                                in1=xr[:, xb, :], op0=ALU.mult, op1=ALU.add)
                    yfree[qs2] = ("cV", vfin)
                    xprev[xb] = vfin
                    k.wt("sync", "cV", vfin)
                    st_x[xb] = k.dma("sync", out=xout[tok0:tok0 + 128, :], in_=xnew[:, xb, :])
        k.fence()


WSPECS_L1F = []
for _p in ("q", "k", "v"):
    WSPECS_L1F += [
        (f"L1_ffn_{_p}_node_w", (15, DM), F32),
        (f"L1_ffn_{_p}_node_b", (15,), F32),
        (f"L1_ffn_{_p}_w1", (16, DM, 64), F32),
        (f"L1_ffn_{_p}_b1", (16, 64), F32),
        (f"L1_ffn_{_p}_w2", (16, 64, DM), F32),
        (f"L1_ffn_{_p}_b2", (16, DM), F32),
    ]


def tspecs_l1f(seq):
    return [
        ("t_ident", (128, 128), BF16),
        ("t_iota16", (16,), F32),
    ]


def build_l1f(seq):
    nc = bass.Bass("TRN2", target_bir_lowering=False)
    es = contextlib.ExitStack()
    with es:
        k = K(nc, es, seq)
        D = k.dram
        D["xin"] = nc.dram_tensor("xin", [seq, DM], F32, kind="ExternalInput").ap()
        for nm, shp, dt in WSPECS_L1F + tspecs_l1f(seq):
            D[nm] = nc.dram_tensor(nm, list(shp), dt, kind="ExternalInput").ap()
        D["hscr"] = nc.dram_tensor("hscr", [seq, 2048], BF16, kind="Internal").ap()
        D["qTb"] = nc.dram_tensor("qTb", [128, 8, seq], BF16, kind="Internal").ap()
        D["vb"] = nc.dram_tensor("vb", [seq, DM], BF16, kind="Internal").ap()
        D["xout"] = nc.dram_tensor("xout", [seq, DM], F32, kind="ExternalOutput").ap()
        emit_l1f(k, D["xin"], D["xout"])
        k.build()
    return nc


def all_tables(seq):
    t = host_tables(seq)
    tab = {
        "t_ident": t["ident"],
        "t_trmask": t["trmask"],
        "t_pvlr": t["pvlr"],
        "t_pvq4": t["pvq4"],
        "t_iota16": np.arange(16, dtype=np.float32),
    }
    qq2 = np.arange(128)[:, None]
    kk2 = np.arange(512)[None, :]
    tab["t_mlamask"] = np.stack([np.where(kk2 <= qq2 + 128 * e, 0.0, -30000.0)
                                 for e in range(4)]).astype(ml_dtypes.bfloat16)
    freqs = (1.0 / 100000.0 ** (np.arange(0, 32, 2, dtype=np.float64) / 32))
    ang = np.outer(np.arange(seq, dtype=np.float64), freqs)
    tab["t_mla_cos"] = np.cos(ang).astype(np.float32)
    tab["t_mla_sin"] = np.sin(ang).astype(np.float32)
    tl1, _ = l1a_host_tables(seq)
    tab.update(tl1)
    return tab


ALL_WSPECS = WSPECS_L0A + WSPECS_L0F + WSPECS_L1A + WSPECS_L1F


def all_tspecs(seq):
    seen = {}
    for nm, shp, dt in (list(TSPECS) + tspecs_l0f(seq) + tspecs_l1a(seq)
                        + tspecs_l1f(seq)):
        seen[nm] = (shp, dt)
    return [(nm, shp, dt) for nm, (shp, dt) in seen.items()]


def build_fused(seq):
    nc = bass.Bass("TRN2", target_bir_lowering=False)
    es = contextlib.ExitStack()
    with es:
        k = K(nc, es, seq)
        D = k.dram
        D["xin"] = nc.dram_tensor("xin", [seq, DM], F32, kind="ExternalInput").ap()
        for nm, shp, dt in ALL_WSPECS + all_tspecs(seq):
            D[nm] = nc.dram_tensor(nm, list(shp), dt, kind="ExternalInput").ap()
        D["hscr"] = nc.dram_tensor("hscr", [seq, 2048], BF16, kind="Internal").ap()
        D["zb"] = nc.dram_tensor("zb", [seq, DM], F32, kind="Internal").ap()
        D["r1b"] = nc.dram_tensor("r1b", [seq, DM], F32, kind="Internal").ap()
        D["macc"] = nc.dram_tensor("macc", [seq, DM], F32, kind="Internal").ap()
        D["hscr2"] = nc.dram_tensor("hscr2", [seq, 1024], BF16, kind="Internal").ap()
        D["vb"] = nc.dram_tensor("vb", [seq, 2048], BF16, kind="Internal").ap()
        vb_l1f = nc.dram_tensor("vb2", [seq, DM], BF16, kind="Internal").ap()
        D["qTb"] = nc.dram_tensor("qTb", [128, 8, seq], BF16, kind="Internal").ap()
        x1 = nc.dram_tensor("x1", [seq, DM], F32, kind="Internal").ap()
        x2 = nc.dram_tensor("x2", [seq, DM], F32, kind="Internal").ap()
        x3 = nc.dram_tensor("x3", [seq, DM], F32, kind="Internal").ap()
        D["xout"] = nc.dram_tensor("xout", [seq, DM], F32, kind="ExternalOutput").ap()
        emit_l0a(k, D["xin"], x1)
        emit_l0f(k, x1, x2)
        emit_l1a(k, x2, x3)
        D["vb"] = vb_l1f
        emit_l1f(k, x3, D["xout"])
        k.build()
    return nc


def kernel(**inputs):
    seq = SEQ_DEFAULT
    tab = all_tables(seq)
    x = np.ascontiguousarray(np.asarray(inputs["L0_attn_x"], dtype=np.float32))
    ncores = x.shape[0]
    nc = build_fused(seq)
    shared = {nm: np.ascontiguousarray(np.asarray(inputs[nm], dtype=np.float32))
              for nm, _, _ in ALL_WSPECS}
    for nm, _, _ in all_tspecs(seq):
        shared[nm] = tab[nm]
    in_maps = []
    for i in range(ncores):
        m = {"xin": np.ascontiguousarray(x[i])}
        m.update(shared)
        in_maps.append(m)
    res = run_bass_kernel_spmd(nc, in_maps, core_ids=list(range(ncores)))
    return np.stack([np.asarray(r["xout"], dtype=np.float32) for r in res.results],
                    axis=0)


SEQ_DEFAULT = 4096

